# version 04
# baseline (speedup 1.0000x reference)
import math
from contextlib import ExitStack

import numpy as np
import ml_dtypes

import concourse.bass as bass
import concourse.mybir as mybir
from concourse.bass_utils import run_bass_kernel_spmd

F32 = mybir.dt.float32
BF16 = mybir.dt.bfloat16
AF = mybir.ActivationFunctionType
ALU = mybir.AluOpType
AX = mybir.AxisListType
BF = ml_dtypes.bfloat16

B, S, E = 2, 8192, 2048
H, KV, D = 16, 4, 128
WIN, CAP = 1024, 50.0
EPS, LNEPS = 1e-6, 1e-5
IFF = 1408
NEXP = 9
NCORE = 8
OWN, HALO = 2048, 1024
SLC = OWN + 2 * HALO
NT = SLC // 128
OT0 = HALO // 128
NOWN = OWN // 128
NEC = E // 128
NIC = IFF // 128
NKT = 20
TANH_SC = 1.0 / (CAP * math.sqrt(D))
NEG = -30000.0
GGRP = ((0, 3), (3, 6), (6, 9), (9, 11))
GCOL = tuple((b - a) * 128 for a, b in GGRP)
GOFF = tuple(sum(NEC * GCOL[k] for k in range(g)) for g in range(4))
QBUF = NEC * 384

DEBUG = False


class Ctr:
    def __init__(self, nc, name):
        self.h = nc.alloc_semaphore(name)
        self.n = 0

    def inc(self, instr, k=1):
        instr.then_inc(self.h, k)
        self.n += k
        return self.n


def band_cls(idx):
    dr = idx - 8
    if -8 <= dr <= -5:
        return dr + 8
    if 8 <= dr <= 11:
        return dr - 4
    return None


def build_nc():
    nc = bass.Bass("TRN2", target_bir_lowering=False, debug=False)

    x_in = nc.dram_tensor("x_slice", [SLC, E], F32, kind="ExternalInput").ap()
    cs_in = nc.dram_tensor("cs", [SLC, 2 * D], F32, kind="ExternalInput").ap()
    kval_in = nc.dram_tensor("kvalid01", [128, NT], F32, kind="ExternalInput").ap()
    bm_in = nc.dram_tensor("bmask", [128, 2 * 128], BF16, kind="ExternalInput").ap()
    qs_in = nc.dram_tensor("qscale_t", [128, D], F32, kind="ExternalInput").ap()
    ks_in = nc.dram_tensor("kscale_t", [128, D], F32, kind="ExternalInput").ap()
    idf_in = nc.dram_tensor("identf", [128, 128], F32, kind="ExternalInput").ap()
    idb_in = nc.dram_tensor("identb", [128, 128], BF16, kind="ExternalInput").ap()
    onesc_in = nc.dram_tensor("onescol", [128, 1], BF16, kind="ExternalInput").ap()
    ones1_in = nc.dram_tensor("onesrow", [1, 128], F32, kind="ExternalInput").ap()
    epsln_in = nc.dram_tensor("epsln", [128, 1], F32, kind="ExternalInput").ap()
    eps30_in = nc.dram_tensor("eps30", [128, 1], F32, kind="ExternalInput").ap()
    wq_in = nc.dram_tensor("wq_h", [128, 4 * NEC * 512], F32, kind="ExternalInput").ap()
    wk_in = nc.dram_tensor("wk_h", [128, NEC * 512], F32, kind="ExternalInput").ap()
    wv_in = nc.dram_tensor("wv_h", [128, NEC * 512], F32, kind="ExternalInput").ap()
    wo_in = nc.dram_tensor("wo_h", [128, 4 * H * 512], F32, kind="ExternalInput").ap()
    rw_in = nc.dram_tensor("rw_h", [128, NEC * 8], F32, kind="ExternalInput").ap()
    wg_in = nc.dram_tensor("wg_h", [NEXP, 128, NEC * IFF], F32, kind="ExternalInput").ap()
    wu_in = nc.dram_tensor("wu_h", [NEXP, 128, NEC * IFF], F32, kind="ExternalInput").ap()
    wd_in = nc.dram_tensor("wd_h", [NEXP, 128, 4 * NIC * 512], F32, kind="ExternalInput").ap()
    out_d = nc.dram_tensor("out", [OWN, E], F32, kind="ExternalOutput").ap()
    xnimg = nc.dram_tensor("xnimg", [128, NEC * OWN], BF16, kind="Internal").ap()

    out_tpe = out_d.rearrange("(tt p) e -> p tt e", p=128)
    xnimg_r = xnimg.rearrange("p (ec tk) -> p ec tk", tk=OWN)

    dbg = {}
    if DEBUG:
        for nm, shape, dt in [
            ("dbg_xnimg", [128, NEC * OWN], BF16),
            ("dbg_KT", [128, KV * SLC], BF16),
            ("dbg_V", [128, NT * 512], BF16),
            ("dbg_logits", [128, NOWN * 8], F32),
            ("dbg_comb", [128, NOWN * 8], F32),
            ("dbg_QT", [128, H * 512], BF16),
            ("dbg_OT", [128, H * 512], BF16),
            ("dbg_h", [128, NIC * OWN], BF16),
        ]:
            dbg[nm] = nc.dram_tensor(nm, shape, dt, kind="ExternalOutput").ap()

    es = ExitStack()

    def sb(name, shape, dt):
        return es.enter_context(nc.sbuf_tensor(name, shape, dt))

    def g3(ap, w):
        return ap.rearrange("p (g w) -> p g w", w=w)

    onesc = sb("onesc", [128, 1], BF16)
    ones1 = sb("ones1", [1, 128], F32)
    qsc = sb("qsc", [128, D], F32)
    ksc = sb("ksc", [128, D], F32)
    kval = sb("kval", [128, NT], F32)
    comb = sb("comb", [128, NOWN * 8], F32)
    epsln = sb("epsln_sb", [128, 1], F32)
    eps30 = sb("eps30_sb", [128, 1], F32)

    pab = ExitStack()

    def absb(n, s_, d_):
        return pab.enter_context(nc.sbuf_tensor(n, s_, d_))

    idb = absb("idb", [128, 128], BF16)
    idf = absb("idf_sb", [128, 128], F32)
    stash_opr = absb("stash_opr", [128, NOWN], F32)
    stash_nb = absb("stash_nb", [128, NOWN], F32)
    KT = absb("KT", [128, KV * SLC], BF16)
    Vsb = absb("Vsb", [128, NT * 512], BF16)

    st = {n: absb("st_" + n, [128, 1], F32)
          for n in ("ssq", "sx", "msq", "var", "std", "rstd", "nb", "rstdc")}
    nrm4 = absb("nrm4", [128, 4], F32)
    nrm4c = absb("nrm4c", [128, 4], F32)
    std4 = absb("std4", [128, 4], F32)
    den4 = absb("den4", [128, 4], F32)
    den4c = absb("den4c", [128, 4], F32)
    rn4 = absb("rn4", [128, 4], F32)
    rn4c = absb("rn4c", [128, 4], F32)
    rec4 = absb("rec4", [128, 4], F32)

    rro = absb("rro", [128, 512], F32)
    rt1 = absb("rt1", [128, 512], F32)
    rt2 = absb("rt2", [128, 512], F32)
    rfo = absb("rfo", [128, 512], F32)
    rfin = absb("rfin", [128, 512], BF16)
    sq512 = absb("sq512", [128, 512], F32)

    sLD = Ctr(nc, "sLD")
    sGW = Ctr(nc, "sGW")
    sPE = Ctr(nc, "sPE")
    sVE = Ctr(nc, "sVE")
    sSC = Ctr(nc, "sSC")
    sGP = Ctr(nc, "sGP")
    sSO = Ctr(nc, "sSO")
    sAC = Ctr(nc, "sAC")
    sCN = Ctr(nc, "sCN")
    sXQ = [Ctr(nc, "sXQ0"), Ctr(nc, "sXQ1")]
    sCS = [Ctr(nc, "sCS0"), Ctr(nc, "sCS1")]
    sQL = Ctr(nc, "sQL")
    sBM = Ctr(nc, "sBM")
    sXO = [Ctr(nc, "sXO0"), Ctr(nc, "sXO1")]
    sWQ = [Ctr(nc, "sWQ0"), Ctr(nc, "sWQ1")]
    sWG = [Ctr(nc, "sWG0"), Ctr(nc, "sWG1")]
    sWU = [Ctr(nc, "sWU0"), Ctr(nc, "sWU1")]
    sWD = [Ctr(nc, "sWD0"), Ctr(nc, "sWD1")]
    sEV = [Ctr(nc, "sEV0"), Ctr(nc, "sEV1")]

    blk = es.enter_context(nc.Block())

    def emit_norm_chain(v_nrm_sc):
        def _(g):
            g.wait_ge(sSC.h, v_nrm_sc)
            sGP.inc(g.tensor_scalar(nrm4c[:], nrm4[:], 1.0, None, ALU.mult))
        emit(blk.gpsimd, _, sGP)
        v1 = sGP.n

        def _(sc):
            sc.wait_ge(sGP.h, v1)
            sSC.inc(sc.activation(std4[:], nrm4c[:], AF.Ln,
                                  bias=eps30[:, 0:1]))
        emit(blk.scalar, _, sSC)
        v2 = sSC.n

        def _(g):
            g.wait_ge(sSC.h, v2)
            sGP.inc(g.tensor_scalar(den4[:], std4[:], 1.0, None, ALU.mult))
        emit(blk.gpsimd, _, sGP)
        v3 = sGP.n

        def _(sc):
            sc.wait_ge(sGP.h, v3)
            sSC.inc(sc.activation(rn4[:], den4[:], AF.Exp, scale=-0.5))
        emit(blk.scalar, _, sSC)
        v4 = sSC.n

        def _(v):
            v.wait_ge(sSC.h, v4)
            sVE.inc(v.tensor_scalar(den4c[:], rn4[:], EPS, 1.0,
                                    ALU.mult, ALU.add))
            v.wait_ge(sVE.h, sVE.n)
            sVE.inc(v.reciprocal(rec4[:], den4c[:]))
        emit(blk.vector, _, sVE)
        v5 = sVE.n

        def _(g):
            g.wait_ge(sVE.h, v5)
            sGP.inc(g.tensor_tensor(rn4c[:], rec4[:], rn4[:], ALU.mult))
        emit(blk.gpsimd, _, sGP)
        return sGP.n

    def emit(dec, fn, ctr=None):
        if ctr is None:
            dec(fn)
            return

        def wrapped(eng):
            eng.wait_ge(ctr.h, ctr.n)
            fn(eng)
        dec(wrapped)

    pa = ExitStack()

    def asb(n, s_, d_):
        return pa.enter_context(nc.sbuf_tensor(n, s_, d_))

    def aps(n, s_, d_):
        return pa.enter_context(nc.psum_tensor(n, s_, d_))

    rwsb = asb("rwsb", [128, NEC * 8], F32)
    logits = asb("logits", [128, NOWN * 8], F32)
    xq = asb("xq", [128, 2 * E], F32)
    cst = asb("cst", [128, 2 * 256], F32)
    xnf = asb("xnf", [128, 2 * E], F32)
    sqscr = asb("sqscr", [128, E], F32)
    xnbt = asb("xnbt", [128, NEC * 128], BF16)
    xf32 = asb("xf32", [128, NEC * 128], F32)
    wk_sb = asb("wk_sb", [128, NEC * 512], BF16)
    wv_sb = asb("wv_sb", [128, NEC * 512], BF16)

    r_ = {n: asb("r_" + n, [128, NOWN], F32)
          for n in ("m1", "m1c", "m2", "m2c", "ss", "ssc", "rs", "rsc")}
    rb_ = {n: asb("rb_" + n, [128, NOWN * 8], F32)
           for n in ("eqt", "sh", "ew", "l2", "keep", "cmb0")}

    trA = aps("trA", [128, 512], F32)
    trB = aps("trB", [128, 512], F32)
    Kps = aps("Kps", [128, 512], F32)
    Vps = aps("Vps", [128, 512], F32)
    Lps = aps("Lps", [128, 8], F32)
    ktrA = aps("ktrA", [128, 512], BF16)
    ktrB = aps("ktrB", [128, 512], BF16)

    def _(s):
        for dst, src in ((idf, idf_in), (qsc, qs_in), (ksc, ks_in),
                         (kval, kval_in), (rwsb, rw_in),
                         (ones1, ones1_in), (epsln, epsln_in),
                         (eps30, eps30_in)):
            sCN.inc(s.dma_start(out=dst[:], in_=src), 16)
    emit(blk.sync, _)
    V_CONST = sCN.n

    def _(g):
        sGW.inc(g.dma_start(out=idb[:], in_=idb_in), 16)
        sGW.inc(g.dma_start(out=onesc[:], in_=onesc_in), 16)
        sGW.inc(g.dma_start(out=wk_sb[:], in_=wk_in), 16)
        sGW.inc(g.dma_start(out=wv_sb[:], in_=wv_in), 16)
    emit(blk.gpsimd, _, sGP)
    V_WKV = sGW.n

    def _(s):
        for t in range(2):
            sXQ[t % 2].inc(s.dma_start(out=xq[:, (t % 2) * E:(t % 2 + 1) * E],
                                       in_=x_in[t * 128:(t + 1) * 128, :]), 16)
            sCS[t % 2].inc(s.dma_start(out=cst[:, (t % 2) * 256:(t % 2 + 1) * 256],
                                       in_=cs_in[t * 128:(t + 1) * 128, :]), 16)
    emit(blk.sync, _)

    tr_last = [0] * NT
    kv_done = [0] * NT
    vevac_done = [0] * NT
    rope_rd = [0] * NT
    spill_done = [0] * NT
    router_done = [0] * NT
    logev_done = [0] * NT
    ktr_done = [0] * NT
    pre_evac = [0, 0]
    pre_evac32 = [0, 0]

    for t in range(NT):
        own = OT0 <= t < OT0 + NOWN
        ot = t - OT0
        sl = t % 2
        xsl = xq[:, sl * E:(sl + 1) * E]
        xn_t = xnf[:, sl * E:(sl + 1) * E]
        cos_t = cst[:, sl * 256:sl * 256 + 128]
        sin_t = cst[:, sl * 256 + 128:sl * 256 + 256]

        def _(sc, xsl=xsl, t=t):
            if t == 0:
                sc.wait_ge(sCN.h, V_CONST)
            sc.wait_ge(sXQ[t % 2].h, 16 * (t // 2 + 1))
            sSC.inc(sc.activation(sqscr[:], xsl, AF.Square,
                                  accum_out=st["ssq"][:, 0:1]))
        emit(blk.scalar, _, sSC)

        def _(v, xsl=xsl, t=t):
            v.wait_ge(sXQ[t % 2].h, 16 * (t // 2 + 1))
            sVE.inc(v.reduce_sum(st["sx"][:, 0:1], xsl, axis=AX.X))
        emit(blk.vector, _, sVE)
        v_sx = sVE.n

        def _(sc):
            sc.wait_ge(sVE.h, v_sx)
            sSC.inc(sc.activation(st["msq"][:, 0:1], st["sx"][:, 0:1],
                                  AF.Square, scale=1.0 / E))
        emit(blk.scalar, _, sSC)
        v_msq = sSC.n

        def _(v):
            v.wait_ge(sSC.h, v_msq)
            sVE.inc(v.scalar_tensor_tensor(st["var"][:, 0:1], st["ssq"][:, 0:1],
                                           1.0 / E, st["msq"][:, 0:1],
                                           ALU.mult, ALU.subtract))
        emit(blk.vector, _, sVE)
        v_var = sVE.n

        def _(sc):
            sc.wait_ge(sVE.h, v_var)
            sSC.inc(sc.activation(st["std"][:, 0:1], st["var"][:, 0:1],
                                  AF.Ln, bias=epsln[:, 0:1]))
        emit(blk.scalar, _, sSC)
        v_std = sSC.n

        def _(g):
            g.wait_ge(sSC.h, v_std)
            sGP.inc(g.tensor_scalar(st["nb"][:, 0:1], st["std"][:, 0:1],
                                    1.0, None, ALU.mult))
        emit(blk.gpsimd, _, sGP)
        v_lnc = sGP.n

        def _(sc):
            sc.wait_ge(sGP.h, v_lnc)
            sSC.inc(sc.activation(st["rstd"][:, 0:1], st["nb"][:, 0:1],
                                  AF.Exp, scale=-0.5))
        emit(blk.scalar, _, sSC)
        v_rstd = sSC.n

        def _(g, own=own, ot=ot):
            g.wait_ge(sSC.h, v_rstd)
            sGP.inc(g.tensor_scalar(st["nb"][:, 0:1], st["sx"][:, 0:1],
                                    st["rstd"][:, 0:1], -1.0 / E,
                                    ALU.mult, ALU.mult))
            sGP.inc(g.tensor_scalar(st["rstdc"][:, 0:1], st["rstd"][:, 0:1],
                                    1.0, None, ALU.mult))
            if own:
                sGP.inc(g.tensor_scalar(stash_opr[:, ot:ot + 1],
                                        st["rstd"][:, 0:1], 1.0, None, ALU.add))
                g.wait_ge(sGP.h, sGP.n)
                sGP.inc(g.tensor_scalar(stash_nb[:, ot:ot + 1],
                                        st["nb"][:, 0:1], 1.0, None, ALU.mult))
        emit(blk.gpsimd, _, sGP)
        v_nb = sGP.n

        def _(v, xsl=xsl, xn_t=xn_t, t=t):
            v.wait_ge(sGP.h, v_nb)
            if t >= 2:
                v.wait_ge(sPE.h, tr_last[t - 2])
            sVE.inc(v.tensor_scalar(xn_t, xsl, st["rstdc"][:, 0:1],
                                    st["nb"][:, 0:1], ALU.mult, ALU.add))
        emit(blk.vector, _, sVE)
        v_xn = sVE.n

        grp_evac = []
        grp_evac32 = []
        for grp in range(4):
            trp = trA if grp % 2 == 0 else trB

            def _(tn, grp=grp, trp=trp, xn_t=xn_t, t=t):
                if t == 0 and grp == 0:
                    tn.wait_ge(sCN.h, V_CONST)
                tn.wait_ge(sVE.h, v_xn)
                if grp >= 2:
                    tn.wait_ge(sSC.h, grp_evac[grp - 2])
                    if grp_evac32[grp - 2]:
                        tn.wait_ge(sVE.h, grp_evac32[grp - 2])
                else:
                    if pre_evac[grp]:
                        tn.wait_ge(sSC.h, pre_evac[grp])
                    if pre_evac32[grp]:
                        tn.wait_ge(sVE.h, pre_evac32[grp])
                for j in range(4):
                    ec = grp * 4 + j
                    sPE.inc(tn.transpose(trp[:, j * 128:(j + 1) * 128],
                                         xn_t[:, ec * 128:(ec + 1) * 128],
                                         idf[:]))
            emit(blk.tensor, _, sPE)
            v_tr = sPE.n
            if grp == 3:
                tr_last[t] = v_tr

            def _(sc, grp=grp, trp=trp, t=t):
                sc.wait_ge(sPE.h, v_tr)
                if t > 0 and grp == 0:
                    sc.wait_ge(sPE.h, kv_done[t - 1])
                    if spill_done[t - 1]:
                        sc.wait_ge(sAC.h, spill_done[t - 1])
                for j in range(4):
                    ec = grp * 4 + j
                    sSC.inc(sc.copy(xnbt[:, ec * 128:(ec + 1) * 128],
                                    trp[:, j * 128:(j + 1) * 128]))
            emit(blk.scalar, _, sSC)
            grp_evac.append(sSC.n)

            if own:
                def _(v, grp=grp, trp=trp, t=t):
                    v.wait_ge(sPE.h, v_tr)
                    if grp == 0 and router_done[t - 1]:
                        v.wait_ge(sPE.h, router_done[t - 1])
                    for j in range(4):
                        ec = grp * 4 + j
                        sVE.inc(v.tensor_copy(
                            xf32[:, ec * 128:(ec + 1) * 128],
                            trp[:, j * 128:(j + 1) * 128]))
                emit(blk.vector, _, sVE)
                grp_evac32.append(sVE.n)
            else:
                grp_evac32.append(0)
        pre_evac = grp_evac[2:]
        pre_evac32 = grp_evac32[2:]

        def _(tn, t=t):
            tn.wait_ge(sSC.h, grp_evac[-1])
            if t == 0:
                tn.wait_ge(sGW.h, V_WKV)
            else:
                tn.wait_ge(sVE.h, rope_rd[t - 1])
                tn.wait_ge(sSC.h, vevac_done[t - 1])
            lastV = None
            for ec in range(NEC):
                lhs = xnbt[:, ec * 128:(ec + 1) * 128]
                tn.matmul(Kps[:], lhs, wk_sb[:, ec * 512:(ec + 1) * 512],
                          start=(ec == 0), stop=(ec == NEC - 1))
                lastV = tn.matmul(Vps[:], lhs,
                                  wv_sb[:, ec * 512:(ec + 1) * 512],
                                  start=(ec == 0), stop=(ec == NEC - 1))
            sPE.inc(lastV)
        emit(blk.tensor, _, sPE)
        kv_done[t] = sPE.n

        if own:
            def _(s, ot=ot):
                s.wait_ge(sSC.h, grp_evac[-1])
                sAC.inc(s.dma_start(
                    out=xnimg_r[:, :, ot * 128:(ot + 1) * 128],
                    in_=xnbt[:].rearrange("p (ec c) -> p ec c", c=128)), 16)
            emit(blk.sync, _)
            spill_done[t] = sAC.n

            def _(tn, t=t):
                tn.wait_ge(sVE.h, grp_evac32[-1])
                if logev_done[t - 1]:
                    tn.wait_ge(sVE.h, logev_done[t - 1])
                last = None
                for ec in range(NEC):
                    last = tn.matmul(Lps[:], xf32[:, ec * 128:(ec + 1) * 128],
                                     rwsb[:, ec * 8:(ec + 1) * 8],
                                     start=(ec == 0), stop=(ec == NEC - 1))
                sPE.inc(last)
            emit(blk.tensor, _, sPE)
            router_done[t] = sPE.n
            v_rt = sPE.n

            def _(v, ot=ot):
                v.wait_ge(sPE.h, v_rt)
                sVE.inc(v.tensor_copy(logits[:, ot * 8:(ot + 1) * 8], Lps[:]))
            emit(blk.vector, _, sVE)
            logev_done[t] = sVE.n
        else:
            spill_done[t] = spill_done[t - 1] if t else 0
            router_done[t] = router_done[t - 1] if t else 0
            logev_done[t] = logev_done[t - 1] if t else 0

        kp3 = g3(Kps[:], 128)
        rro3 = g3(rro[:], 128)

        def _(v, t=t, cos_t=cos_t, sin_t=sin_t, kp3=kp3):
            v.wait_ge(sPE.h, kv_done[t])
            v.wait_ge(sCS[t % 2].h, 16 * (t // 2 + 1))
            sVE.inc(v.tensor_scalar(rro3[:, :, 0:64], kp3[:, :, 64:128],
                                    -1.0, None, ALU.mult))
            sVE.inc(v.tensor_scalar(rro3[:, :, 64:128], kp3[:, :, 0:64],
                                    1.0, None, ALU.mult))
            cos_b = cos_t.rearrange("p (o d) -> p o d", o=1).broadcast_to([128, 4, 128])
            sin_b = sin_t.rearrange("p (o d) -> p o d", o=1).broadcast_to([128, 4, 128])
            sVE.inc(v.tensor_tensor(g3(rt1[:], 128), kp3, cos_b, ALU.mult))
            v.wait_ge(sVE.h, sVE.n)
            sVE.inc(v.tensor_tensor(g3(rt2[:], 128), rro3, sin_b, ALU.mult))
            v.wait_ge(sVE.h, sVE.n)
            sVE.inc(v.tensor_add(rfo[:], rt1[:], rt2[:]))
        emit(blk.vector, _, sVE)
        rope_rd[t] = sVE.n
        v_rfo = sVE.n

        if t + 2 < NT:
            t2 = t + 2

            def _(s, t2=t2):
                s.wait_ge(sVE.h, v_rfo)
                s.wait_ge(sXQ[t2 % 2].h, sXQ[t2 % 2].n)
                s.wait_ge(sCS[t2 % 2].h, sCS[t2 % 2].n)
                sXQ[t2 % 2].inc(s.dma_start(
                    out=xq[:, (t2 % 2) * E:(t2 % 2 + 1) * E],
                    in_=x_in[t2 * 128:(t2 + 1) * 128, :]), 16)
                sCS[t2 % 2].inc(s.dma_start(
                    out=cst[:, (t2 % 2) * 256:(t2 % 2 + 1) * 256],
                    in_=cs_in[t2 * 128:(t2 + 1) * 128, :]), 16)
            emit(blk.sync, _)

        def _(sc, t=t):
            sc.wait_ge(sVE.h, v_rfo)
            for hh in range(4):
                sSC.inc(sc.activation(sq512[:, hh * 128:(hh + 1) * 128],
                                      rfo[:, hh * 128:(hh + 1) * 128],
                                      AF.Square,
                                      accum_out=nrm4[:, hh:hh + 1]))
            sSC.inc(sc.copy(Vsb[:, t * 512:(t + 1) * 512], Vps[:]))
        emit(blk.scalar, _, sSC)
        vevac_done[t] = sSC.n
        v_nrm = sSC.n

        v_r4c = emit_norm_chain(v_nrm)

        def _(v, t=t):
            v.wait_ge(sGP.h, v_r4c)
            if t == 0:
                v.wait_ge(sCN.h, V_CONST)
            else:
                v.wait_ge(sSC.h, ktr_done[t - 1])
            for hh in range(4):
                sVE.inc(v.scalar_tensor_tensor(
                    rfin[:, hh * 128:(hh + 1) * 128],
                    rfo[:, hh * 128:(hh + 1) * 128],
                    rn4c[:, hh:hh + 1], ksc[:], ALU.mult, ALU.mult))
        emit(blk.vector, _, sVE)
        v_rfin = sVE.n

        ktrp = ktrA if t % 2 == 0 else ktrB

        def _(tn, ktrp=ktrp, t=t):
            tn.wait_ge(sVE.h, v_rfin)
            if t >= 2:
                tn.wait_ge(sSC.h, ktr_done[t - 2])
            for hh in range(4):
                sPE.inc(tn.transpose(ktrp[:, hh * 128:(hh + 1) * 128],
                                     rfin[:, hh * 128:(hh + 1) * 128],
                                     idb[:]))
        emit(blk.tensor, _, sPE)
        v_ktr = sPE.n

        def _(sc, ktrp=ktrp, t=t):
            sc.wait_ge(sPE.h, v_ktr)
            for hh in range(4):
                sSC.inc(sc.copy(
                    KT[:, hh * SLC + t * 128:hh * SLC + (t + 1) * 128],
                    ktrp[:, hh * 128:(hh + 1) * 128]))
        emit(blk.scalar, _, sSC)
        ktr_done[t] = sSC.n

    v_logits = logev_done[OT0 + NOWN - 1]
    lg3 = g3(logits[:], 8)

    def _(v):
        v.wait_ge(sVE.h, v_logits)
        sVE.inc(v.reduce_max(g3(r_["m1"][:], 1), lg3, axis=AX.X))
    emit(blk.vector, _, sVE)
    rv1 = sVE.n

    def _(g):
        g.wait_ge(sVE.h, rv1)
        sGP.inc(g.tensor_scalar(r_["m1c"][:], r_["m1"][:], 1.0, None, ALU.mult))
    emit(blk.gpsimd, _, sGP)
    rg1 = sGP.n

    def _(v):
        v.wait_ge(sGP.h, rg1)
        m1b = g3(r_["m1c"][:], 1).broadcast_to([128, NOWN, 8])
        sVE.inc(v.tensor_tensor(g3(rb_["eqt"][:], 8), lg3, m1b, ALU.is_equal))
        sVE.inc(v.tensor_tensor(g3(rb_["sh"][:], 8), lg3, m1b, ALU.subtract))
    emit(blk.vector, _, sVE)
    rv2 = sVE.n

    def _(sc):
        sc.wait_ge(sVE.h, rv2)
        sSC.inc(sc.activation(rb_["ew"][:], rb_["sh"][:], AF.Exp))
    emit(blk.scalar, _, sSC)
    rs1 = sSC.n

    def _(v):
        sVE.inc(v.scalar_tensor_tensor(rb_["l2"][:], rb_["eqt"][:], -1e30,
                                       logits[:], ALU.mult, ALU.add))
        v.wait_ge(sVE.h, sVE.n)
        sVE.inc(v.reduce_max(g3(r_["m2"][:], 1), g3(rb_["l2"][:], 8),
                             axis=AX.X))
        v.wait_ge(sSC.h, rs1)
        sVE.inc(v.reduce_sum(g3(r_["ss"][:], 1), g3(rb_["ew"][:], 8),
                             axis=AX.X))
    emit(blk.vector, _, sVE)
    rv3 = sVE.n

    def _(g):
        g.wait_ge(sVE.h, rv3)
        sGP.inc(g.tensor_scalar(r_["m2c"][:], r_["m2"][:], 1.0, None, ALU.mult))
        sGP.inc(g.tensor_scalar(r_["ssc"][:], r_["ss"][:], 1.0, None, ALU.mult))
    emit(blk.gpsimd, _, sGP)
    rg2 = sGP.n

    def _(v):
        v.wait_ge(sGP.h, rg2)
        m2b = g3(r_["m2c"][:], 1).broadcast_to([128, NOWN, 8])
        sVE.inc(v.tensor_tensor(g3(rb_["keep"][:], 8), lg3, m2b, ALU.is_ge))
        sVE.inc(v.reciprocal(r_["rs"][:], r_["ssc"][:]))
        v.wait_ge(sVE.h, sVE.n)
        sVE.inc(v.tensor_mul(rb_["cmb0"][:], rb_["keep"][:], rb_["ew"][:]))
    emit(blk.vector, _, sVE)
    rv4 = sVE.n

    def _(g):
        g.wait_ge(sVE.h, rv4)
        sGP.inc(g.tensor_scalar(r_["rsc"][:], r_["rs"][:], 1.0, None, ALU.mult))
    emit(blk.gpsimd, _, sGP)
    rg3 = sGP.n

    def _(v):
        v.wait_ge(sGP.h, rg3)
        rsb = g3(r_["rsc"][:], 1).broadcast_to([128, NOWN, 8])
        sVE.inc(v.tensor_tensor(g3(comb[:], 8), g3(rb_["cmb0"][:], 8), rsb,
                                ALU.mult))
    emit(blk.vector, _, sVE)

    if DEBUG:
        def _(s):
            for c in (sPE, sVE, sSC, sGP):
                s.wait_ge(c.h, c.n)
            s.wait_ge(sAC.h, sAC.n)
            sSO.inc(s.dma_start(out=dbg["dbg_KT"], in_=KT[:]), 16)
            sSO.inc(s.dma_start(out=dbg["dbg_V"], in_=Vsb[:]), 16)
            sSO.inc(s.dma_start(out=dbg["dbg_logits"], in_=logits[:]), 16)
            sSO.inc(s.dma_start(out=dbg["dbg_comb"], in_=comb[:]), 16)
            sSO.inc(s.dma_start(out=dbg["dbg_xnimg"], in_=xnimg), 16)
        emit(blk.sync, _)

    A_PE, A_VE, A_SC, A_GP = sPE.n, sVE.n, sSC.n, sGP.n
    A_LD, A_GW, A_AC = sLD.n, sGW.n, sAC.n
    pa.close()

    pb = ExitStack()

    def bsb(n, s_, d_):
        return pb.enter_context(nc.sbuf_tensor(n, s_, d_))

    def bps(n, s_, d_):
        return pb.enter_context(nc.psum_tensor(n, s_, d_))

    bmask = bsb("bmask_sb", [128, 2 * 128], BF16)
    ttb = bsb("ttb", [128, 2 * 1024], BF16)
    xnbtq = bsb("xnbtq", [128, 4 * NEC * 128], BF16)
    wqo = bsb("wqo", [128, 2 * NEC * 512], BF16)
    QT = bsb("QT", [128, H * 512], BF16)
    OTb = bsb("OTb", [128, H * 512], BF16)
    ab = bsb("ab", [128, 2 * 1024], BF16)
    rqf = bsb("rqf", [128, 512], F32)
    rcp1 = bsb("rcp1", [1, 512], F32)
    bcs = bsb("bcs", [128, 512], F32)
    xo = bsb("xo", [128, 2 * 2048], F32)
    outs = bsb("outs", [128, 4 * 512], F32)
    t512 = bsb("t512", [128, 512], F32)
    cstq = bsb("cstq", [128, 4 * 256], F32)

    qp = [bps(f"qp{i}", [128, 512], F32) for i in range(4)]
    s2 = [bps(f"s2_{i}", [128, 1024], F32) for i in range(2)]

    barred = set()

    def barA(eng, key):
        if key not in barred:
            eng.wait_ge(sPE.h, A_PE)
            eng.wait_ge(sVE.h, A_VE)
            eng.wait_ge(sSC.h, A_SC)
            eng.wait_ge(sGP.h, A_GP)
            eng.wait_ge(sGW.h, A_GW)
            eng.wait_ge(sAC.h, A_AC)
            for cc in (sXQ[0], sXQ[1], sCS[0], sCS[1]):
                eng.wait_ge(cc.h, cc.n)
            barred.add(key)

    wqo_ld = {}
    wslot_free = [0, 0]
    qproj_done = {}
    rope_q_rd = {}
    qtr_evac = {}
    tanh_done = {}
    avden_done = {}
    ot_evac = {}
    woproj_done = {}
    asm_done = {}
    asm_dma = {}

    for qb in range(4):
        def _(s, qb=qb):
            barA(s, "sync")
            if qb == 0:
                sBM.inc(s.dma_start(out=bmask[:], in_=bm_in), 16)
            if qb > 0:
                s.wait_ge(sPE.h, qproj_done[(qb - 1, 3)])
                s.wait_ge(sVE.h, rope_q_rd[(qb - 1, 3)])
                s.wait_ge(sQL.h, sQL.n)
            for qt in range(4):
                ot = qb * 4 + qt
                sQL.inc(s.dma_start(
                    out=xnbtq[:, qt * E:(qt + 1) * E].rearrange(
                        "p (ec c) -> p ec c", c=128),
                    in_=xnimg_r[:, :, ot * 128:(ot + 1) * 128]), 16)
                sQL.inc(s.dma_start(
                    out=cstq[:, qt * 256:(qt + 1) * 256],
                    in_=cs_in[HALO + ot * 128:HALO + (ot + 1) * 128, :]), 16)
        emit(blk.sync, _)
        v_qld = sQL.n

        for hc in range(4):
            slot = hc % 2

            def _(g, qb=qb, hc=hc, slot=slot):
                barA(g, "gp")
                if wslot_free[slot]:
                    g.wait_ge(sPE.h, wslot_free[slot])
                g.wait_ge(sWQ[slot].h, sWQ[slot].n)
                sWQ[slot].inc(g.dma_start(
                    out=wqo[:, slot * (NEC * 512):(slot + 1) * (NEC * 512)],
                    in_=wq_in[:, hc * (NEC * 512):(hc + 1) * (NEC * 512)]), 16)
                wqo_ld[(qb, "wq", hc)] = sWQ[slot].n
            emit(blk.gpsimd, _, sGP)

            def _(tn, qb=qb, hc=hc, slot=slot):
                barA(tn, "pe")
                tn.wait_ge(sWQ[slot].h, wqo_ld[(qb, "wq", hc)])
                tn.wait_ge(sQL.h, v_qld)
                if hc >= 1:
                    tn.wait_ge(sVE.h, rope_q_rd[(qb, hc - 1)])
                elif qb > 0:
                    tn.wait_ge(sVE.h, asm_done[(qb - 1, 3)])
                last = None
                for qt in range(4):
                    for ec in range(NEC):
                        last = tn.matmul(
                            qp[qt][:],
                            xnbtq[:, qt * E + ec * 128:qt * E + (ec + 1) * 128],
                            wqo[:, slot * (NEC * 512) + ec * 512:
                                slot * (NEC * 512) + (ec + 1) * 512],
                            start=(ec == 0), stop=(ec == NEC - 1))
                sPE.inc(last)
            emit(blk.tensor, _, sPE)
            qproj_done[(qb, hc)] = sPE.n
            wslot_free[slot] = sPE.n

            for qt in range(4):
                qps3 = g3(qp[qt][:], 128)
                cos_q = cstq[:, qt * 256:qt * 256 + 128]
                sin_q = cstq[:, qt * 256 + 128:qt * 256 + 256]

                def _(v, qb=qb, hc=hc, qps3=qps3, cos_q=cos_q, sin_q=sin_q):
                    v.wait_ge(sPE.h, qproj_done[(qb, hc)])
                    sVE.inc(v.tensor_scalar(rro3[:, :, 0:64],
                                            qps3[:, :, 64:128],
                                            -1.0, None, ALU.mult))
                    sVE.inc(v.tensor_scalar(rro3[:, :, 64:128],
                                            qps3[:, :, 0:64],
                                            1.0, None, ALU.mult))
                    cos_b = cos_q.rearrange("p (o d) -> p o d", o=1).broadcast_to([128, 4, 128])
                    sin_b = sin_q.rearrange("p (o d) -> p o d", o=1).broadcast_to([128, 4, 128])
                    sVE.inc(v.tensor_tensor(g3(rt1[:], 128), qps3, cos_b,
                                            ALU.mult))
                    v.wait_ge(sVE.h, sVE.n)
                    sVE.inc(v.tensor_tensor(g3(rt2[:], 128), rro3, sin_b,
                                            ALU.mult))
                    v.wait_ge(sVE.h, sVE.n)
                    sVE.inc(v.tensor_add(rfo[:], rt1[:], rt2[:]))
                emit(blk.vector, _, sVE)
                v_qrfo = sVE.n
                if qt == 3:
                    rope_q_rd[(qb, hc)] = sVE.n

                def _(sc):
                    sc.wait_ge(sVE.h, v_qrfo)
                    for hh in range(4):
                        sSC.inc(sc.activation(
                            sq512[:, hh * 128:(hh + 1) * 128],
                            rfo[:, hh * 128:(hh + 1) * 128],
                            AF.Square, accum_out=nrm4[:, hh:hh + 1]))
                emit(blk.scalar, _, sSC)
                v_qn = sSC.n

                v_qr4c = emit_norm_chain(v_qn)

                def _(v):
                    v.wait_ge(sGP.h, v_qr4c)
                    for hh in range(4):
                        sVE.inc(v.scalar_tensor_tensor(
                            rqf[:, hh * 128:(hh + 1) * 128],
                            rfo[:, hh * 128:(hh + 1) * 128],
                            rn4c[:, hh:hh + 1], qsc[:], ALU.mult, ALU.mult))
                emit(blk.vector, _, sVE)
                v_qfin = sVE.n

                tsl = (hc * 4 + qt) % 2

                def _(tn, qb=qb, hc=hc, qt=qt, tsl=tsl):
                    tn.wait_ge(sVE.h, v_qfin)
                    n = hc * 4 + qt
                    if n >= 2:
                        tn.wait_ge(sSC.h, qtr_evac[(qb, n - 2)])
                    elif qb > 0:
                        if n == 0:
                            tn.wait_ge(sSC.h, tanh_done[(qb - 1, H - 1, 8)])
                        else:
                            tn.wait_ge(sVE.h, ot_evac[(qb - 1, H - 1)])
                    for hh in range(4):
                        sPE.inc(tn.transpose(
                            s2[tsl][:, hh * 128:(hh + 1) * 128],
                            rqf[:, hh * 128:(hh + 1) * 128], idf[:]))
                emit(blk.tensor, _, sPE)
                v_qtr = sPE.n

                def _(sc, qb=qb, hc=hc, qt=qt, tsl=tsl):
                    sc.wait_ge(sPE.h, v_qtr)
                    for hh in range(4):
                        h = hc * 4 + hh
                        sSC.inc(sc.copy(
                            QT[:, h * 512 + qt * 128:h * 512 + (qt + 1) * 128],
                            s2[tsl][:, hh * 128:(hh + 1) * 128]))
                emit(blk.scalar, _, sSC)
                qtr_evac[(qb, hc * 4 + qt)] = sSC.n

        def _load_wo(ec, qb=qb):
            slot = ec % 2

            def _(g, qb=qb, ec=ec, slot=slot):
                if wslot_free[slot]:
                    g.wait_ge(sPE.h, wslot_free[slot])
                g.wait_ge(sWQ[slot].h, sWQ[slot].n)
                sWQ[slot].inc(g.dma_start(
                    out=wqo[:, slot * (NEC * 512):(slot + 1) * (NEC * 512)],
                    in_=wo_in[:, ec * (H * 512):(ec + 1) * (H * 512)]), 16)
                wqo_ld[(qb, "wo", ec)] = sWQ[slot].n
            emit(blk.gpsimd, _, sGP)

        for ec in range(2):
            _load_wo(ec)

        NPAIR = NKT // 2
        for h in range(H):
            kv = h // 4
            otp = qp[0] if h % 2 == 0 else qp[2]
            dnp = qp[1] if h % 2 == 0 else qp[3]
            v_sc = {}

            def emit_scores(pi, h=h, kv=kv, qb=qb):
                psl = pi % 2
                kt0 = qb * 4 + 2 * pi

                def _(tn, pi=pi, psl=psl, kt0=kt0, h=h, kv=kv, qb=qb):
                    if h == 0 and pi == 0:
                        tn.wait_ge(sSC.h, qtr_evac[(qb, 15)])
                    if pi >= 2:
                        tn.wait_ge(sSC.h, tanh_done[(qb, h, pi - 2)])
                    elif pi == 0:
                        if h > 0:
                            tn.wait_ge(sSC.h, tanh_done[(qb, h - 1, 8)])
                        elif qb > 0:
                            tn.wait_ge(sSC.h, tanh_done[(qb - 1, H - 1, 8)])
                    else:
                        if h > 0:
                            tn.wait_ge(sVE.h, ot_evac[(qb, h - 1)])
                        elif qb > 0:
                            tn.wait_ge(sVE.h, ot_evac[(qb - 1, H - 1)])
                    tn.matmul(
                        s2[psl][:, 0:512],
                        KT[:, kv * SLC + kt0 * 128:kv * SLC + (kt0 + 1) * 128],
                        QT[:, h * 512:(h + 1) * 512], start=True, stop=True)
                    sPE.inc(tn.matmul(
                        s2[psl][:, 512:1024],
                        KT[:, kv * SLC + (kt0 + 1) * 128:
                           kv * SLC + (kt0 + 2) * 128],
                        QT[:, h * 512:(h + 1) * 512], start=True, stop=True))
                emit(blk.tensor, _, sPE)
                v_sc[pi] = sPE.n

            emit_scores(0)
            for pi in range(NPAIR):
                psl = pi % 2
                kt0 = qb * 4 + 2 * pi

                def _(sc, psl=psl, pi=pi, h=h, qb=qb):
                    sc.wait_ge(sPE.h, v_sc[pi])
                    sSC.inc(sc.activation(ttb[:, psl * 1024:(psl + 1) * 1024],
                                          s2[psl][:], AF.Tanh,
                                          scale=TANH_SC))
                    tanh_done[(qb, h, pi)] = sSC.n
                    sc.wait_ge(sSC.h, sSC.n)
                    if pi >= 2:
                        sc.wait_ge(sPE.h, avden_done[(qb, h, pi - 2)])
                    elif h > 0:
                        sc.wait_ge(sPE.h,
                                   avden_done[(qb, h - 1, NPAIR - 2 + pi)])
                    elif qb > 0:
                        sc.wait_ge(sPE.h,
                                   avden_done[(qb - 1, H - 1, NPAIR - 2 + pi)])
                    sSC.inc(sc.activation(ab[:, psl * 1024:(psl + 1) * 1024],
                                          ttb[:, psl * 1024:(psl + 1) * 1024],
                                          AF.Exp, scale=CAP))
                emit(blk.scalar, _, sSC)
                v_exp = sSC.n

                v_msk = 0
                mask_ops = []
                for half in range(2):
                    kt = kt0 + half
                    dr = 2 * pi + half - 8
                    pad = kt < 8 or kt >= 24
                    base = psl * 1024 + half * 512
                    run0 = None
                    for j in range(4):
                        diff = dr - j
                        if abs(diff) <= 7:
                            st_ = "full"
                        elif diff == -8:
                            st_ = "ge"
                        elif diff == 8:
                            st_ = "le"
                        else:
                            st_ = "zero"
                        if st_ == "zero":
                            if run0 is None:
                                run0 = j
                        else:
                            if run0 is not None:
                                mask_ops.append((base + run0 * 128,
                                                 (j - run0) * 128, 0))
                                run0 = None
                            if st_ == "ge":
                                mask_ops.append((base + j * 128, 128, 1))
                            elif st_ == "le":
                                mask_ops.append((base + j * 128, 128, 2))
                    if run0 is not None:
                        mask_ops.append((base + run0 * 128,
                                         (4 - run0) * 128, 0))
                    if pad:
                        mask_ops.append((base, 512, 3, kt))
                if mask_ops:
                    def _(v, mask_ops=mask_ops):
                        v.wait_ge(sBM.h, 16)
                        v.wait_ge(sSC.h, v_exp)
                        for op in mask_ops:
                            if op[2] == 0:
                                sVE.inc(v.memset(ab[:, op[0]:op[0] + op[1]],
                                                 0.0))
                            elif op[2] in (1, 2):
                                tri = bmask[:, (op[2] - 1) * 128:op[2] * 128]
                                sVE.inc(v.tensor_tensor(
                                    ab[:, op[0]:op[0] + op[1]],
                                    ab[:, op[0]:op[0] + op[1]],
                                    tri, ALU.mult))
                            else:
                                sVE.inc(v.tensor_scalar(
                                    ab[:, op[0]:op[0] + op[1]],
                                    ab[:, op[0]:op[0] + op[1]],
                                    kval[:, op[3]:op[3] + 1],
                                    None, ALU.mult))
                            v.wait_ge(sVE.h, sVE.n)
                    emit(blk.vector, _, sVE)
                    v_msk = sVE.n

                if pi + 1 < NPAIR:
                    emit_scores(pi + 1)

                def _(tn, psl=psl, pi=pi, h=h, kt0=kt0, kv=kv, otp=otp,
                      dnp=dnp, v_msk=v_msk, qb=qb):
                    if v_msk:
                        tn.wait_ge(sVE.h, v_msk)
                    else:
                        tn.wait_ge(sSC.h, v_exp)
                    if pi == 0:
                        if h >= 2:
                            tn.wait_ge(sVE.h, ot_evac[(qb, h - 2)])
                        elif qb > 0:
                            tn.wait_ge(sVE.h, ot_evac[(qb - 1, H - 2 + h)])
                    last = None
                    for half in range(2):
                        kt = kt0 + half
                        a_ap = ab[:, psl * 1024 + half * 512:
                                  psl * 1024 + (half + 1) * 512]
                        tn.matmul(
                            otp[:],
                            Vsb[:, kt * 512 + kv * 128:
                                kt * 512 + (kv + 1) * 128],
                            a_ap, start=(pi == 0 and half == 0),
                            stop=(pi == NPAIR - 1 and half == 1))
                        last = tn.matmul(
                            dnp[0:1, :], onesc[:], a_ap,
                            start=(pi == 0 and half == 0),
                            stop=(pi == NPAIR - 1 and half == 1))
                    sPE.inc(last)
                emit(blk.tensor, _, sPE)
                avden_done[(qb, h, pi)] = sPE.n

            def _(v, h=h, dnp=dnp, qb=qb):
                v.wait_ge(sPE.h, avden_done[(qb, h, NPAIR - 1)])
                sVE.inc(v.reciprocal(rcp1[:], dnp[0:1, :]))
            emit(blk.vector, _, sVE)
            v_rcp = sVE.n

            def _(tn, qb=qb, h=h):
                tn.wait_ge(sVE.h, v_rcp)
                tn.wait_ge(sSC.h, tanh_done[(qb, h, NPAIR - 1)])
                sPE.inc(tn.matmul(s2[1][:, 0:512], ones1[:], rcp1[:],
                                  start=True, stop=True))
            emit(blk.tensor, _, sPE)
            v_bc = sPE.n

            def _(v, h=h, otp=otp, qb=qb):
                v.wait_ge(sPE.h, v_bc)
                if h == 0 and qb > 0:
                    v.wait_ge(sPE.h, woproj_done[(qb - 1, 3)])
                sVE.inc(v.tensor_copy(bcs[:], s2[1][:, 0:512]))
                v.wait_ge(sVE.h, sVE.n)
                sVE.inc(v.tensor_tensor(OTb[:, h * 512:(h + 1) * 512],
                                        otp[:], bcs[:], ALU.mult))
            emit(blk.vector, _, sVE)
            ot_evac[(qb, h)] = sVE.n

        x_tpe = x_in.rearrange("(tt p) e -> p tt e", p=128)
        xo_ld = {}
        for ec in range(4):
            slot = ec % 2

            def _(s, qb=qb, ec=ec, slot=slot):
                if ec >= 2:
                    s.wait_ge(sVE.h, asm_done[(qb, ec - 2)])
                elif qb > 0:
                    s.wait_ge(sVE.h, asm_done[(qb - 1, 2 + ec)])
                s.wait_ge(sXO[slot].h, sXO[slot].n)
                sXO[slot].inc(s.dma_start(
                    out=xo[:, slot * 2048:(slot + 1) * 2048].rearrange(
                        "p (q c) -> p q c", c=512),
                    in_=x_tpe[:, OT0 + qb * 4:OT0 + qb * 4 + 4,
                              ec * 512:(ec + 1) * 512]), 16)
                xo_ld[ec] = sXO[slot].n
            emit(blk.sync, _)

            def _(tn, qb=qb, ec=ec, slot=slot):
                tn.wait_ge(sWQ[slot].h, wqo_ld[(qb, "wo", ec)])
                tn.wait_ge(sVE.h, ot_evac[(qb, H - 1)])
                if ec >= 1:
                    tn.wait_ge(sVE.h, asm_done[(qb, ec - 1)])
                last = None
                for qt in range(4):
                    for hcc in range(H):
                        last = tn.matmul(
                            qp[qt][:],
                            OTb[:, hcc * 512 + qt * 128:hcc * 512 + (qt + 1) * 128],
                            wqo[:, slot * (NEC * 512) + hcc * 512:
                                slot * (NEC * 512) + (hcc + 1) * 512],
                            start=(hcc == 0), stop=(hcc == H - 1))
                sPE.inc(last)
            emit(blk.tensor, _, sPE)
            woproj_done[(qb, ec)] = sPE.n
            wslot_free[slot] = sPE.n
            if ec + 2 <= 3:
                _load_wo(ec + 2)

            def _(v, qb=qb, ec=ec, slot=slot):
                v.wait_ge(sPE.h, woproj_done[(qb, ec)])
                v.wait_ge(sXO[ec % 2].h, xo_ld[ec])
                if asm_dma.get((qb, ec - 1)):
                    v.wait_ge(sSO.h, asm_dma[(qb, ec - 1)])
                elif qb > 0 and ec == 0:
                    v.wait_ge(sSO.h, asm_dma[(qb - 1, 3)])
                for qt in range(4):
                    ot = qb * 4 + qt
                    sVE.inc(v.tensor_scalar(t512[:],
                                            xo[:, slot * 2048 + qt * 512:
                                                slot * 2048 + (qt + 1) * 512],
                                            stash_opr[:, ot:ot + 1],
                                            stash_nb[:, ot:ot + 1],
                                            ALU.mult, ALU.add))
                    v.wait_ge(sVE.h, sVE.n)
                    sVE.inc(v.tensor_tensor(outs[:, qt * 512:(qt + 1) * 512],
                                            t512[:], qp[qt][:], ALU.add))
                    v.wait_ge(sVE.h, sVE.n)
            emit(blk.vector, _, sVE)
            asm_done[(qb, ec)] = sVE.n

            def _(s, qb=qb, ec=ec):
                s.wait_ge(sVE.h, asm_done[(qb, ec)])
                for qt in range(4):
                    sSO.inc(s.dma_start(
                        out=out_tpe[:, qb * 4 + qt, ec * 512:(ec + 1) * 512],
                        in_=outs[:, qt * 512:(qt + 1) * 512]), 16)
            emit(blk.sync, _)
            asm_dma[(qb, ec)] = sSO.n

    if DEBUG:
        def _(s):
            s.wait_ge(sSC.h, qtr_evac[(3, 15)])
            sSO.inc(s.dma_start(out=dbg["dbg_QT"], in_=QT[:]), 16)
            s.wait_ge(sVE.h, ot_evac[(3, H - 1)])
            sSO.inc(s.dma_start(out=dbg["dbg_OT"], in_=OTb[:]), 16)
        emit(blk.sync, _)

    B_PE, B_VE, B_SC, B_GP = sPE.n, sVE.n, sSC.n, sGP.n
    B_SO, B_LD, B_GW = sSO.n, sLD.n, sGW.n
    pb.close()
    pab.close()

    pc = ExitStack()

    def csb(n, s_, d_):
        return pc.enter_context(nc.sbuf_tensor(n, s_, d_))

    def cps(n, s_, d_):
        return pc.enter_context(nc.psum_tensor(n, s_, d_))

    xnTf = csb("xnTf", [128, NEC * OWN], BF16)
    hsb = csb("hsb", [128, NIC * OWN], BF16)
    wgq = csb("wgq", [128, 2 * QBUF], BF16)
    wuq = csb("wuq", [128, 2 * QBUF], BF16)
    wdc = csb("wdc", [128, 2 * NIC * 512], BF16)
    sgs = csb("sgs", [128, 2 * 512], F32)
    evb = csb("evb", [128, 2 * 4 * 512], BF16)

    gp_ = [cps(f"gp{i}", [128, 512], F32) for i in range(4)]
    up_ = [cps(f"up{i}", [128, 512], F32) for i in range(4)]

    cbarred = set()

    def barB(eng, key):
        if key not in cbarred:
            eng.wait_ge(sPE.h, B_PE)
            eng.wait_ge(sVE.h, B_VE)
            eng.wait_ge(sSC.h, B_SC)
            eng.wait_ge(sGP.h, B_GP)
            for cc in (sQL, sXO[0], sXO[1], sWQ[0], sWQ[1]):
                eng.wait_ge(cc.h, cc.n)
            cbarred.add(key)

    def _(s):
        barB(s, "sync")
        sLD.inc(s.dma_start(out=xnTf[:], in_=xnimg), 16)
    emit(blk.sync, _)
    v_xnTf = sLD.n

    def _(g):
        barB(g, "gp")
        g.wait_ge(sSO.h, B_SO)
    emit(blk.gpsimd, _, sGP)

    wld = {}
    sgs_free = [0, 0]
    wgu_free = [0, 0]
    wd_free = [0, 0]
    gu_done = {}
    h_evac = {}
    dn_evac = {}
    acc_dma = {}
    down_mm_last = [0] * NEXP

    def _load_gu(qidx):
        if qidx >= NEXP * 4:
            return
        e, gq = qidx // 4, qidx % 4
        slot = qidx % 2

        def _(g, e=e, gq=gq, slot=slot):
            if wgu_free[slot]:
                g.wait_ge(sPE.h, wgu_free[slot])
            g.wait_ge(sWG[slot].h, sWG[slot].n)
            g.wait_ge(sWU[slot].h, sWU[slot].n)
            cols = NEC * GCOL[gq]
            sWG[slot].inc(g.dma_start(
                out=wgq[:, slot * QBUF:slot * QBUF + cols],
                in_=wg_in[e, :, GOFF[gq]:GOFF[gq] + cols]), 16)
            wld[(e, "g", gq)] = sWG[slot].n
            sWU[slot].inc(g.dma_start(
                out=wuq[:, slot * QBUF:slot * QBUF + cols],
                in_=wu_in[e, :, GOFF[gq]:GOFF[gq] + cols]), 16)
            wld[(e, "u", gq)] = sWU[slot].n
        emit(blk.gpsimd, _, sGP)

    def _load_wd(didx):
        if didx >= NEXP * 4:
            return
        e, ecq = didx // 4, didx % 4
        slot = didx % 2

        def _(g, e=e, ecq=ecq, slot=slot):
            if wd_free[slot]:
                g.wait_ge(sPE.h, wd_free[slot])
            g.wait_ge(sWD[slot].h, sWD[slot].n)
            sWD[slot].inc(g.dma_start(
                out=wdc[:, slot * (NIC * 512):(slot + 1) * (NIC * 512)],
                in_=wd_in[e, :, ecq * (NIC * 512):(ecq + 1) * (NIC * 512)]), 16)
            wld[(e, "d", ecq)] = sWD[slot].n
        emit(blk.gpsimd, _, sGP)

    _load_gu(0)
    _load_gu(1)
    _load_wd(0)
    _load_wd(1)

    for e in range(NEXP):
        for gq in range(4):
            slot = (e * 4 + gq) % 2
            for icl in range(GGRP[gq][1] - GGRP[gq][0]):
                ic = GGRP[gq][0] + icl
                for half in range(2):
                    bank = ([gp_[0], gp_[1], up_[0], up_[1]] if half == 0
                            else [gp_[2], gp_[3], up_[2], up_[3]])
                    tc0, tc1 = 2 * half, 2 * half + 1

                    def _(tn, e=e, gq=gq, icl=icl, ic=ic, half=half,
                          slot=slot, bank=bank, tc0=tc0, tc1=tc1):
                        barB(tn, "pe")
                        tn.wait_ge(sWG[slot].h, wld[(e, "g", gq)])
                        tn.wait_ge(sWU[slot].h, wld[(e, "u", gq)])
                        if e == 0 and gq == 0 and icl == 0 and half == 0:
                            tn.wait_ge(sLD.h, v_xnTf)
                        if ic > 0:
                            tn.wait_ge(sVE.h, h_evac[(e, ic - 1, half)])
                        elif e > 0:
                            tn.wait_ge(sVE.h, dn_evac[(e - 1, 3, NOWN - 1)])
                        base = slot * QBUF
                        last = None
                        for ec in range(NEC):
                            off = base + ec * GCOL[gq] + icl * 128
                            lg = wgq[:, off:off + 128]
                            lu = wuq[:, off:off + 128]
                            x0 = xnTf[:, ec * OWN + tc0 * 512:
                                      ec * OWN + (tc0 + 1) * 512]
                            x1 = xnTf[:, ec * OWN + tc1 * 512:
                                      ec * OWN + (tc1 + 1) * 512]
                            tn.matmul(bank[0][:], lg, x0, start=(ec == 0),
                                      stop=(ec == NEC - 1))
                            tn.matmul(bank[1][:], lg, x1, start=(ec == 0),
                                      stop=(ec == NEC - 1))
                            tn.matmul(bank[2][:], lu, x0, start=(ec == 0),
                                      stop=(ec == NEC - 1))
                            last = tn.matmul(bank[3][:], lu, x1,
                                             start=(ec == 0),
                                             stop=(ec == NEC - 1))
                        sPE.inc(last)
                    emit(blk.tensor, _, sPE)
                    gu_done[(e, ic, half)] = sPE.n

                    for k in range(2):
                        tc = 2 * half + k
                        gbank = bank[k]
                        ubank = bank[2 + k]
                        ssl = k

                        def _(sc, e=e, ic=ic, half=half, gbank=gbank, ssl=ssl):
                            sc.wait_ge(sPE.h, gu_done[(e, ic, half)])
                            if sgs_free[ssl]:
                                sc.wait_ge(sVE.h, sgs_free[ssl])
                            sSC.inc(sc.activation(
                                sgs[:, ssl * 512:(ssl + 1) * 512],
                                gbank[:], AF.Sigmoid))
                        emit(blk.scalar, _, sSC)
                        v_sg = sSC.n

                        def _(v, e=e, ic=ic, tc=tc, ubank=ubank, ssl=ssl):
                            v.wait_ge(sSC.h, v_sg)
                            if e > 0 and ic == 0 and tc == 0:
                                v.wait_ge(sPE.h, down_mm_last[e - 1])
                            sVE.inc(v.tensor_tensor(
                                hsb[:, ic * OWN + tc * 512:
                                    ic * OWN + (tc + 1) * 512],
                                sgs[:, ssl * 512:(ssl + 1) * 512],
                                ubank[:], ALU.mult))
                        emit(blk.vector, _, sVE)
                        sgs_free[ssl] = sVE.n
                    h_evac[(e, ic, half)] = sVE.n

            wgu_free[slot] = sPE.n
            _load_gu(e * 4 + gq + 2)

        for ecq in range(4):
            slot = (e * 4 + ecq) % 2
            for tt in range(NOWN):
                dps = gp_[tt % 2]

                def _(tn, e=e, ecq=ecq, tt=tt, slot=slot, dps=dps):
                    tn.wait_ge(sWD[slot].h, wld[(e, "d", ecq)])
                    if ecq == 0 and tt < 2:
                        tn.wait_ge(sVE.h, h_evac[(e, NIC - 1, 1)])
                        tn.wait_ge(sVE.h, h_evac[(e, NIC - 1, 0)])
                    elif tt >= 2:
                        tn.wait_ge(sVE.h, dn_evac[(e, ecq, tt - 2)])
                    else:
                        tn.wait_ge(sVE.h, dn_evac[(e, ecq - 1, NOWN - 2 + tt)])
                    last = None
                    for ic in range(NIC):
                        last = tn.matmul(
                            dps[:],
                            hsb[:, ic * OWN + tt * 128:ic * OWN + (tt + 1) * 128],
                            wdc[:, slot * (NIC * 512) + ic * 512:
                                slot * (NIC * 512) + (ic + 1) * 512],
                            start=(ic == 0), stop=(ic == NIC - 1))
                    sPE.inc(last)
                emit(blk.tensor, _, sPE)
                v_dn = sPE.n
                if tt == NOWN - 1:
                    wd_free[slot] = sPE.n
                    if ecq == 3:
                        down_mm_last[e] = sPE.n
                    _load_wd(e * 4 + ecq + 2)

                def _(v, e=e, ecq=ecq, tt=tt, dps=dps):
                    v.wait_ge(sPE.h, v_dn)
                    grp = (tt // 4) % 2
                    if tt % 4 == 0:
                        gi = e * 16 + ecq * 4 + tt // 4
                        if gi >= 2 and acc_dma.get(gi - 2):
                            v.wait_ge(sEV[grp].h, acc_dma[gi - 2])
                    dst = evb[:, grp * 2048 + (tt % 4) * 512:
                              grp * 2048 + (tt % 4 + 1) * 512]
                    if e == 0:
                        sVE.inc(v.tensor_copy(dst, dps[:]))
                    else:
                        sVE.inc(v.tensor_scalar(
                            dst, dps[:],
                            comb[:, tt * 8 + (e - 1):tt * 8 + e],
                            None, ALU.mult))
                emit(blk.vector, _, sVE)
                dn_evac[(e, ecq, tt)] = sVE.n

                if tt % 4 == 3:
                    gslot = (tt // 4) % 2

                    def _(g, e=e, ecq=ecq, tt=tt, grp=gslot):
                        g.wait_ge(sVE.h, dn_evac[(e, ecq, tt)])
                        g.wait_ge(sEV[grp].h, sEV[grp].n)
                        sEV[grp].inc(g.dma_start(
                            out=out_tpe[:, tt - 3:tt + 1,
                                        ecq * 512:(ecq + 1) * 512],
                            in_=evb[:, grp * 2048:(grp + 1) * 2048].rearrange(
                                "p (q c) -> p q c", c=512),
                            accum_op=ALU.add), 16)
                    emit(blk.gpsimd, _, sGP)
                    acc_dma[e * 16 + ecq * 4 + tt // 4] = sEV[gslot].n

        if DEBUG and e == 1:
            def _(s, e=e):
                s.wait_ge(sVE.h, h_evac[(e, NIC - 1, 1)])
                sSO.inc(s.dma_start(out=dbg["dbg_h"], in_=hsb[:]), 16)
            emit(blk.sync, _)

    def _(g):
        g.wait_ge(sAC.h, sAC.n)
        g.wait_ge(sSO.h, sSO.n)
        g.wait_ge(sEV[0].h, sEV[0].n)
        g.wait_ge(sEV[1].h, sEV[1].n)
    emit(blk.gpsimd, _, sGP)

    def _(s):
        s.wait_ge(sAC.h, sAC.n)
        s.wait_ge(sSO.h, sSO.n)
        s.wait_ge(sEV[0].h, sEV[0].n)
        s.wait_ge(sEV[1].h, sEV[1].n)
    emit(blk.sync, _)

    pc.close()
    es.close()
    return nc


def _host_prep(inputs):
    x = np.asarray(inputs["L0_attn_hidden_states"])
    wq = np.asarray(inputs["L0_attn_wq"])
    wk = np.asarray(inputs["L0_attn_wk"])
    wv = np.asarray(inputs["L0_attn_wv"])
    wo = np.asarray(inputs["L0_attn_wo"])
    qscale = np.asarray(inputs["L0_attn_query_scale"])
    kscale = np.asarray(inputs["L0_attn_key_scale"])
    rw = np.asarray(inputs["L0_ffn_router_w"])
    sg = np.asarray(inputs["L0_ffn_shared_gate"])
    su = np.asarray(inputs["L0_ffn_shared_up"])
    sd = np.asarray(inputs["L0_ffn_shared_down"])
    wg = np.asarray(inputs["L0_ffn_w_gate"])
    wu = np.asarray(inputs["L0_ffn_w_up"])
    wd = np.asarray(inputs["L0_ffn_w_down"])

    def pmajor(a):
        r, c = a.shape
        return np.ascontiguousarray(
            a.reshape(r // 128, 128, c).transpose(1, 0, 2).reshape(128, -1))

    wq4 = np.ascontiguousarray(
        wq.reshape(NEC, 128, 4, 512).transpose(1, 2, 0, 3).reshape(128, -1))
    wo4 = np.ascontiguousarray(
        wo.reshape(H, 128, 4, 512).transpose(1, 2, 0, 3).reshape(128, -1))
    wk_h = pmajor(wk)
    wv_h = pmajor(wv)
    rw_h = pmajor(np.ascontiguousarray(rw.T))

    def gate_h(w):
        wT = np.ascontiguousarray(w.T)
        t = wT.reshape(NEC, 128, NIC, 128)
        parts = [np.ascontiguousarray(
            t[:, :, a:b, :].transpose(1, 0, 2, 3).reshape(128, -1))
            for a, b in GGRP]
        return np.concatenate(parts, axis=1)

    def down_h(w):
        wT = np.ascontiguousarray(w.T)
        t = wT.reshape(NIC, 128, 4, 512)
        return np.ascontiguousarray(t.transpose(1, 2, 0, 3).reshape(128, -1))

    wg_h = np.stack([gate_h(sg)] + [gate_h(wg[i]) for i in range(8)])
    wu_h = np.stack([gate_h(su)] + [gate_h(wu[i]) for i in range(8)])
    wd_h = np.stack([down_h(sd)] + [down_h(wd[i]) for i in range(8)])

    half = D // 2
    inv_freq = 1.0 / (10000.0 ** (np.arange(half, dtype=np.float64) / half))

    ar = np.arange(128)
    tri_ge = (ar[:, None] >= ar[None, :]).astype(np.float32)
    tri_le = (ar[:, None] <= ar[None, :]).astype(np.float32)
    bmask = np.concatenate([tri_ge, tri_le], axis=1).astype(BF)

    common = dict(
        wq_h=wq4, wk_h=wk_h, wv_h=wv_h, wo_h=wo4, rw_h=rw_h,
        wg_h=wg_h, wu_h=wu_h, wd_h=wd_h,
        qscale_t=np.ascontiguousarray(np.tile(qscale[None, :], (128, 1))),
        kscale_t=np.ascontiguousarray(np.tile(kscale[None, :], (128, 1))),
        identf=np.eye(128, dtype=np.float32),
        identb=np.eye(128, dtype=np.float32).astype(BF),
        onescol=np.ones((128, 1), dtype=BF),
        onesrow=np.ones((1, 128), dtype=np.float32),
        epsln=np.full((128, 1), LNEPS, dtype=np.float32),
        eps30=np.full((128, 1), 1e-30, dtype=np.float32),
        bmask=bmask,
    )

    in_maps = []
    for c in range(NCORE):
        bi, qq = c // 4, (c % 4) * OWN
        lo, hi = qq - HALO, qq + OWN + HALO
        xs = np.zeros((SLC, E), dtype=np.float32)
        a, b_ = max(lo, 0), min(hi, S)
        xs[a - lo:b_ - lo] = x[bi, a:b_]
        pos = np.clip(np.arange(lo, hi), 0, S - 1).astype(np.float64)
        ang = np.outer(pos, inv_freq)
        cs = np.concatenate([np.cos(ang), np.cos(ang),
                             np.sin(ang), np.sin(ang)],
                            axis=1).astype(np.float32)
        kvalid = np.ones((128, NT), dtype=np.float32)
        rowpos = lo + (np.arange(NT)[None, :] * 128 + np.arange(128)[:, None])
        kvalid[(rowpos < 0) | (rowpos >= S)] = 0.0
        m = dict(common)
        m.update(x_slice=xs, cs=cs, kvalid01=kvalid)
        in_maps.append(m)
    return in_maps


def kernel(**inputs):
    in_maps = _host_prep(inputs)
    nc = build_nc()
    res = run_bass_kernel_spmd(nc, in_maps, core_ids=list(range(NCORE)))
    out = np.empty((B, S, E), dtype=np.float32)
    for c in range(NCORE):
        bi, qq = c // 4, (c % 4) * OWN
        out[bi, qq:qq + OWN] = res.results[c]["out"]
    return out


# version 05
# speedup vs baseline: 1.0472x; 1.0472x over previous
import math
from contextlib import ExitStack

import numpy as np
import ml_dtypes

import concourse.bass as bass
import concourse.mybir as mybir
from concourse.bass_utils import run_bass_kernel_spmd

F32 = mybir.dt.float32
BF16 = mybir.dt.bfloat16
AF = mybir.ActivationFunctionType
ALU = mybir.AluOpType
AX = mybir.AxisListType
BF = ml_dtypes.bfloat16

B, S, E = 2, 8192, 2048
H, KV, D = 16, 4, 128
WIN, CAP = 1024, 50.0
EPS, LNEPS = 1e-6, 1e-5
IFF = 1408
NEXP = 9
NCORE = 8
OWN, HALO = 2048, 1024
SLC = OWN + 2 * HALO
NT = SLC // 128
OT0 = HALO // 128
NOWN = OWN // 128
NEC = E // 128
NIC = IFF // 128
NKT = 20
TANH_SC = 1.0 / (CAP * math.sqrt(D))
NEG = -30000.0
GGRP = ((0, 3), (3, 6), (6, 9), (9, 11))
GCOL = tuple((b - a) * 128 for a, b in GGRP)
GOFF = tuple(sum(NEC * GCOL[k] for k in range(g)) for g in range(4))
QBUF = NEC * 384

DEBUG = False


class Ctr:
    def __init__(self, nc, name):
        self.h = nc.alloc_semaphore(name)
        self.n = 0

    def inc(self, instr, k=1):
        instr.then_inc(self.h, k)
        self.n += k
        return self.n


def qrange(idx):
    dr = idx - 8
    if dr <= -5:
        return 0, (dr + 9) * 128
    if dr >= 8:
        return (dr - 8) * 128, 512
    return 0, 512


def band_cls(idx):
    dr = idx - 8
    if -8 <= dr <= -5:
        return dr + 8
    if 8 <= dr <= 11:
        return dr - 4
    return None


def build_nc():
    nc = bass.Bass("TRN2", target_bir_lowering=False, debug=False)

    x_in = nc.dram_tensor("x_slice", [SLC, E], F32, kind="ExternalInput").ap()
    cs_in = nc.dram_tensor("cs", [SLC, 2 * D], F32, kind="ExternalInput").ap()
    kval_in = nc.dram_tensor("kvalid01", [128, NT], F32, kind="ExternalInput").ap()
    bm_in = nc.dram_tensor("bmask", [128, 2 * 128], BF16, kind="ExternalInput").ap()
    qs_in = nc.dram_tensor("qscale_t", [128, D], F32, kind="ExternalInput").ap()
    ks_in = nc.dram_tensor("kscale_t", [128, D], F32, kind="ExternalInput").ap()
    idf_in = nc.dram_tensor("identf", [128, 128], F32, kind="ExternalInput").ap()
    idb_in = nc.dram_tensor("identb", [128, 128], BF16, kind="ExternalInput").ap()
    onesc_in = nc.dram_tensor("onescol", [128, 1], BF16, kind="ExternalInput").ap()
    ones1_in = nc.dram_tensor("onesrow", [1, 128], F32, kind="ExternalInput").ap()
    epsln_in = nc.dram_tensor("epsln", [128, 1], F32, kind="ExternalInput").ap()
    eps30_in = nc.dram_tensor("eps30", [128, 1], F32, kind="ExternalInput").ap()
    wq_in = nc.dram_tensor("wq_h", [128, 4 * NEC * 512], F32, kind="ExternalInput").ap()
    wk_in = nc.dram_tensor("wk_h", [128, NEC * 512], F32, kind="ExternalInput").ap()
    wv_in = nc.dram_tensor("wv_h", [128, NEC * 512], F32, kind="ExternalInput").ap()
    wo_in = nc.dram_tensor("wo_h", [128, 4 * H * 512], F32, kind="ExternalInput").ap()
    rw_in = nc.dram_tensor("rw_h", [128, NEC * 8], F32, kind="ExternalInput").ap()
    wg_in = nc.dram_tensor("wg_h", [NEXP, 128, NEC * IFF], F32, kind="ExternalInput").ap()
    wu_in = nc.dram_tensor("wu_h", [NEXP, 128, NEC * IFF], F32, kind="ExternalInput").ap()
    wd_in = nc.dram_tensor("wd_h", [NEXP, 128, 4 * NIC * 512], F32, kind="ExternalInput").ap()
    out_d = nc.dram_tensor("out", [OWN, E], F32, kind="ExternalOutput").ap()
    xnimg = nc.dram_tensor("xnimg", [128, NEC * OWN], BF16, kind="Internal").ap()

    out_tpe = out_d.rearrange("(tt p) e -> p tt e", p=128)
    xnimg_r = xnimg.rearrange("p (ec tk) -> p ec tk", tk=OWN)

    dbg = {}
    if DEBUG:
        for nm, shape, dt in [
            ("dbg_xnimg", [128, NEC * OWN], BF16),
            ("dbg_KT", [128, KV * SLC], BF16),
            ("dbg_V", [128, NT * 512], BF16),
            ("dbg_logits", [128, NOWN * 8], F32),
            ("dbg_comb", [128, NOWN * 8], F32),
            ("dbg_QT", [128, H * 512], BF16),
            ("dbg_OT", [128, H * 512], BF16),
            ("dbg_h", [128, NIC * OWN], BF16),
        ]:
            dbg[nm] = nc.dram_tensor(nm, shape, dt, kind="ExternalOutput").ap()

    es = ExitStack()

    def sb(name, shape, dt):
        return es.enter_context(nc.sbuf_tensor(name, shape, dt))

    def g3(ap, w):
        return ap.rearrange("p (g w) -> p g w", w=w)

    onesc = sb("onesc", [128, 1], BF16)
    ones1 = sb("ones1", [1, 128], F32)
    qsc = sb("qsc", [128, D], F32)
    ksc = sb("ksc", [128, D], F32)
    kval = sb("kval", [128, NT], F32)
    comb = sb("comb", [128, NOWN * 8], F32)
    epsln = sb("epsln_sb", [128, 1], F32)
    eps30 = sb("eps30_sb", [128, 1], F32)

    pab = ExitStack()

    def absb(n, s_, d_):
        return pab.enter_context(nc.sbuf_tensor(n, s_, d_))

    idb = absb("idb", [128, 128], BF16)
    idf = absb("idf_sb", [128, 128], F32)
    stash_opr = absb("stash_opr", [128, NOWN], F32)
    stash_nb = absb("stash_nb", [128, NOWN], F32)
    KT = absb("KT", [128, KV * SLC], BF16)
    Vsb = absb("Vsb", [128, NT * 512], BF16)

    st = {n: absb("st_" + n, [128, 1], F32)
          for n in ("ssq", "sx", "msq", "var", "std", "rstd", "nb", "rstdc")}
    nrm4 = absb("nrm4", [128, 4], F32)
    nrm4c = absb("nrm4c", [128, 4], F32)
    std4 = absb("std4", [128, 4], F32)
    den4 = absb("den4", [128, 4], F32)
    den4c = absb("den4c", [128, 4], F32)
    rn4 = absb("rn4", [128, 4], F32)
    rn4c = absb("rn4c", [128, 4], F32)
    rec4 = absb("rec4", [128, 4], F32)

    rro = absb("rro", [128, 512], F32)
    rt1 = absb("rt1", [128, 512], F32)
    rt2 = absb("rt2", [128, 512], F32)
    rfo = absb("rfo", [128, 512], F32)
    rfin = absb("rfin", [128, 512], BF16)
    sq512 = absb("sq512", [128, 512], F32)

    sLD = Ctr(nc, "sLD")
    sGW = Ctr(nc, "sGW")
    sPE = Ctr(nc, "sPE")
    sVE = Ctr(nc, "sVE")
    sSC = Ctr(nc, "sSC")
    sGP = Ctr(nc, "sGP")
    sSO = Ctr(nc, "sSO")
    sAC = Ctr(nc, "sAC")
    sCN = Ctr(nc, "sCN")
    sXQ = [Ctr(nc, "sXQ0"), Ctr(nc, "sXQ1")]
    sCS = [Ctr(nc, "sCS0"), Ctr(nc, "sCS1")]
    sQL = Ctr(nc, "sQL")
    sBM = Ctr(nc, "sBM")
    sXO = [Ctr(nc, "sXO0"), Ctr(nc, "sXO1")]
    sWQ = [Ctr(nc, "sWQ0"), Ctr(nc, "sWQ1")]
    sWG = [Ctr(nc, "sWG0"), Ctr(nc, "sWG1")]
    sWU = [Ctr(nc, "sWU0"), Ctr(nc, "sWU1")]
    sWD = [Ctr(nc, "sWD0"), Ctr(nc, "sWD1")]
    sEV = [Ctr(nc, "sEV0"), Ctr(nc, "sEV1")]

    blk = es.enter_context(nc.Block())

    def emit_norm_chain(v_nrm_sc, eng="sc"):
        def _(g):
            g.wait_ge((sSC if eng == "sc" else sVE).h, v_nrm_sc)
            sGP.inc(g.tensor_scalar(nrm4c[:], nrm4[:], 1.0, None, ALU.mult))
        emit(blk.gpsimd, _, sGP)
        v1 = sGP.n

        def _(sc):
            sc.wait_ge(sGP.h, v1)
            sSC.inc(sc.activation(std4[:], nrm4c[:], AF.Ln,
                                  bias=eps30[:, 0:1]))
        emit(blk.scalar, _, sSC)
        v2 = sSC.n

        def _(g):
            g.wait_ge(sSC.h, v2)
            sGP.inc(g.tensor_scalar(den4[:], std4[:], 1.0, None, ALU.mult))
        emit(blk.gpsimd, _, sGP)
        v3 = sGP.n

        def _(sc):
            sc.wait_ge(sGP.h, v3)
            sSC.inc(sc.activation(rn4[:], den4[:], AF.Exp, scale=-0.5))
        emit(blk.scalar, _, sSC)
        v4 = sSC.n

        def _(v):
            v.wait_ge(sSC.h, v4)
            sVE.inc(v.tensor_scalar(den4c[:], rn4[:], EPS, 1.0,
                                    ALU.mult, ALU.add))
            v.wait_ge(sVE.h, sVE.n)
            sVE.inc(v.reciprocal(rec4[:], den4c[:]))
        emit(blk.vector, _, sVE)
        v5 = sVE.n

        def _(g):
            g.wait_ge(sVE.h, v5)
            sGP.inc(g.tensor_tensor(rn4c[:], rec4[:], rn4[:], ALU.mult))
        emit(blk.gpsimd, _, sGP)
        return sGP.n

    def emit(dec, fn, ctr=None):
        if ctr is None:
            dec(fn)
            return

        def wrapped(eng):
            eng.wait_ge(ctr.h, ctr.n)
            fn(eng)
        dec(wrapped)

    pa = ExitStack()

    def asb(n, s_, d_):
        return pa.enter_context(nc.sbuf_tensor(n, s_, d_))

    def aps(n, s_, d_):
        return pa.enter_context(nc.psum_tensor(n, s_, d_))

    rwsb = asb("rwsb", [128, NEC * 8], F32)
    logits = asb("logits", [128, NOWN * 8], F32)
    xq = asb("xq", [128, 2 * E], F32)
    cst = asb("cst", [128, 2 * 256], F32)
    xnf = asb("xnf", [128, 2 * E], F32)
    sqscr = asb("sqscr", [128, E], F32)
    xnbt = asb("xnbt", [128, NEC * 128], BF16)
    xf32 = asb("xf32", [128, NEC * 128], F32)
    wk_sb = asb("wk_sb", [128, NEC * 512], BF16)
    wv_sb = asb("wv_sb", [128, NEC * 512], BF16)

    r_ = {n: asb("r_" + n, [128, NOWN], F32)
          for n in ("m1", "m1c", "m2", "m2c", "ss", "ssc", "rs", "rsc")}
    rb_ = {n: asb("rb_" + n, [128, NOWN * 8], F32)
           for n in ("eqt", "sh", "ew", "l2", "keep", "cmb0")}

    trA = aps("trA", [128, 512], F32)
    trB = aps("trB", [128, 512], F32)
    Kps = aps("Kps", [128, 512], F32)
    Vps = aps("Vps", [128, 512], F32)
    Lps = aps("Lps", [128, 8], F32)
    ktrA = aps("ktrA", [128, 512], BF16)
    ktrB = aps("ktrB", [128, 512], BF16)

    def _(s):
        for dst, src in ((idf, idf_in), (qsc, qs_in), (ksc, ks_in),
                         (kval, kval_in), (rwsb, rw_in),
                         (ones1, ones1_in), (epsln, epsln_in),
                         (eps30, eps30_in)):
            sCN.inc(s.dma_start(out=dst[:], in_=src), 16)
    emit(blk.sync, _)
    V_CONST = sCN.n

    def _(g):
        sGW.inc(g.dma_start(out=idb[:], in_=idb_in), 16)
        sGW.inc(g.dma_start(out=onesc[:], in_=onesc_in), 16)
        sGW.inc(g.dma_start(out=wk_sb[:], in_=wk_in), 16)
        sGW.inc(g.dma_start(out=wv_sb[:], in_=wv_in), 16)
    emit(blk.gpsimd, _, sGP)
    V_WKV = sGW.n

    def _(s):
        for t in range(2):
            sXQ[t % 2].inc(s.dma_start(out=xq[:, (t % 2) * E:(t % 2 + 1) * E],
                                       in_=x_in[t * 128:(t + 1) * 128, :]), 16)
            sCS[t % 2].inc(s.dma_start(out=cst[:, (t % 2) * 256:(t % 2 + 1) * 256],
                                       in_=cs_in[t * 128:(t + 1) * 128, :]), 16)
    emit(blk.sync, _)

    tr_last = [0] * NT
    kv_done = [0] * NT
    vevac_done = [0] * NT
    rope_rd = [0] * NT
    spill_done = [0] * NT
    router_done = [0] * NT
    logev_done = [0] * NT
    ktr_done = [0] * NT
    pre_evac = [0, 0]
    pre_evac32 = [0, 0]

    for t in range(NT):
        own = OT0 <= t < OT0 + NOWN
        ot = t - OT0
        sl = t % 2
        xsl = xq[:, sl * E:(sl + 1) * E]
        xn_t = xnf[:, sl * E:(sl + 1) * E]
        cos_t = cst[:, sl * 256:sl * 256 + 128]
        sin_t = cst[:, sl * 256 + 128:sl * 256 + 256]

        def _(sc, xsl=xsl, t=t):
            if t == 0:
                sc.wait_ge(sCN.h, V_CONST)
            sc.wait_ge(sXQ[t % 2].h, 16 * (t // 2 + 1))
            sSC.inc(sc.activation(sqscr[:], xsl, AF.Square,
                                  accum_out=st["ssq"][:, 0:1]))
        emit(blk.scalar, _, sSC)
        v_ssq = sSC.n

        def _(v, xsl=xsl, t=t):
            v.wait_ge(sXQ[t % 2].h, 16 * (t // 2 + 1))
            sVE.inc(v.reduce_sum(st["sx"][:, 0:1], xsl, axis=AX.X))
        emit(blk.vector, _, sVE)
        v_sx = sVE.n

        def _(g):
            g.wait_ge(sVE.h, v_sx)
            sGP.inc(g.tensor_scalar(st["msq"][:, 0:1], st["sx"][:, 0:1],
                                    st["sx"][:, 0:1], 1.0 / (E * E),
                                    ALU.mult, ALU.mult))
        emit(blk.gpsimd, _, sGP)
        v_msq = sGP.n

        def _(v):
            v.wait_ge(sGP.h, v_msq)
            v.wait_ge(sSC.h, v_ssq)
            sVE.inc(v.scalar_tensor_tensor(st["var"][:, 0:1], st["ssq"][:, 0:1],
                                           1.0 / E, st["msq"][:, 0:1],
                                           ALU.mult, ALU.subtract))
        emit(blk.vector, _, sVE)
        v_var = sVE.n

        def _(sc):
            sc.wait_ge(sVE.h, v_var)
            sSC.inc(sc.activation(st["std"][:, 0:1], st["var"][:, 0:1],
                                  AF.Ln, bias=epsln[:, 0:1]))
        emit(blk.scalar, _, sSC)
        v_std = sSC.n

        def _(g):
            g.wait_ge(sSC.h, v_std)
            sGP.inc(g.tensor_scalar(st["nb"][:, 0:1], st["std"][:, 0:1],
                                    1.0, None, ALU.mult))
        emit(blk.gpsimd, _, sGP)
        v_lnc = sGP.n

        def _(sc):
            sc.wait_ge(sGP.h, v_lnc)
            sSC.inc(sc.activation(st["rstd"][:, 0:1], st["nb"][:, 0:1],
                                  AF.Exp, scale=-0.5))
        emit(blk.scalar, _, sSC)
        v_rstd = sSC.n

        def _(g, own=own, ot=ot):
            g.wait_ge(sSC.h, v_rstd)
            sGP.inc(g.tensor_scalar(st["nb"][:, 0:1], st["sx"][:, 0:1],
                                    st["rstd"][:, 0:1], -1.0 / E,
                                    ALU.mult, ALU.mult))
            sGP.inc(g.tensor_scalar(st["rstdc"][:, 0:1], st["rstd"][:, 0:1],
                                    1.0, None, ALU.mult))
            if own:
                sGP.inc(g.tensor_scalar(stash_opr[:, ot:ot + 1],
                                        st["rstd"][:, 0:1], 1.0, None, ALU.add))
                g.wait_ge(sGP.h, sGP.n)
                sGP.inc(g.tensor_scalar(stash_nb[:, ot:ot + 1],
                                        st["nb"][:, 0:1], 1.0, None, ALU.mult))
        emit(blk.gpsimd, _, sGP)
        v_nb = sGP.n

        def _(v, xsl=xsl, xn_t=xn_t, t=t):
            v.wait_ge(sGP.h, v_nb)
            if t >= 2:
                v.wait_ge(sPE.h, tr_last[t - 2])
            sVE.inc(v.tensor_scalar(xn_t, xsl, st["rstdc"][:, 0:1],
                                    st["nb"][:, 0:1], ALU.mult, ALU.add))
        emit(blk.vector, _, sVE)
        v_xn = sVE.n

        grp_evac = []
        grp_evac32 = []
        for grp in range(4):
            trp = trA if grp % 2 == 0 else trB

            def _(tn, grp=grp, trp=trp, xn_t=xn_t, t=t):
                if t == 0 and grp == 0:
                    tn.wait_ge(sCN.h, V_CONST)
                tn.wait_ge(sVE.h, v_xn)
                if grp >= 2:
                    tn.wait_ge(sVE.h, grp_evac[grp - 2])
                    if grp_evac32[grp - 2]:
                        tn.wait_ge(sVE.h, grp_evac32[grp - 2])
                else:
                    if pre_evac[grp]:
                        tn.wait_ge(sVE.h, pre_evac[grp])
                    if pre_evac32[grp]:
                        tn.wait_ge(sVE.h, pre_evac32[grp])
                for j in range(4):
                    ec = grp * 4 + j
                    sPE.inc(tn.transpose(trp[:, j * 128:(j + 1) * 128],
                                         xn_t[:, ec * 128:(ec + 1) * 128],
                                         idf[:]))
            emit(blk.tensor, _, sPE)
            v_tr = sPE.n
            if grp == 3:
                tr_last[t] = v_tr

            def _(v, grp=grp, trp=trp, t=t):
                v.wait_ge(sPE.h, v_tr)
                if t > 0 and grp == 0:
                    v.wait_ge(sPE.h, kv_done[t - 1])
                    if spill_done[t - 1]:
                        v.wait_ge(sAC.h, spill_done[t - 1])
                sVE.inc(v.tensor_copy(xnbt[:, grp * 512:(grp + 1) * 512],
                                      trp[:]))
            emit(blk.vector, _, sVE)
            grp_evac.append(sVE.n)

            if own:
                def _(v, grp=grp, trp=trp, t=t):
                    v.wait_ge(sPE.h, v_tr)
                    if grp == 0 and router_done[t - 1]:
                        v.wait_ge(sPE.h, router_done[t - 1])
                    sVE.inc(v.tensor_copy(
                        xf32[:, grp * 512:(grp + 1) * 512], trp[:]))
                emit(blk.vector, _, sVE)
                grp_evac32.append(sVE.n)
            else:
                grp_evac32.append(0)
        pre_evac = grp_evac[2:]
        pre_evac32 = grp_evac32[2:]

        def _(tn, t=t):
            tn.wait_ge(sVE.h, grp_evac[-1])
            if t == 0:
                tn.wait_ge(sGW.h, V_WKV)
            else:
                tn.wait_ge(sVE.h, rope_rd[t - 1])
                tn.wait_ge(sVE.h, vevac_done[t - 1])
            lastV = None
            for ec in range(NEC):
                lhs = xnbt[:, ec * 128:(ec + 1) * 128]
                tn.matmul(Kps[:], lhs, wk_sb[:, ec * 512:(ec + 1) * 512],
                          start=(ec == 0), stop=(ec == NEC - 1))
                lastV = tn.matmul(Vps[:], lhs,
                                  wv_sb[:, ec * 512:(ec + 1) * 512],
                                  start=(ec == 0), stop=(ec == NEC - 1))
            sPE.inc(lastV)
        emit(blk.tensor, _, sPE)
        kv_done[t] = sPE.n

        if own:
            def _(s, ot=ot):
                s.wait_ge(sVE.h, grp_evac[-1])
                sAC.inc(s.dma_start(
                    out=xnimg_r[:, :, ot * 128:(ot + 1) * 128],
                    in_=xnbt[:].rearrange("p (ec c) -> p ec c", c=128)), 16)
            emit(blk.sync, _)
            spill_done[t] = sAC.n

            def _(tn, t=t):
                tn.wait_ge(sVE.h, grp_evac32[-1])
                if logev_done[t - 1]:
                    tn.wait_ge(sVE.h, logev_done[t - 1])
                last = None
                for ec in range(NEC):
                    last = tn.matmul(Lps[:], xf32[:, ec * 128:(ec + 1) * 128],
                                     rwsb[:, ec * 8:(ec + 1) * 8],
                                     start=(ec == 0), stop=(ec == NEC - 1))
                sPE.inc(last)
            emit(blk.tensor, _, sPE)
            router_done[t] = sPE.n
            v_rt = sPE.n

            def _(v, ot=ot):
                v.wait_ge(sPE.h, v_rt)
                sVE.inc(v.tensor_copy(logits[:, ot * 8:(ot + 1) * 8], Lps[:]))
            emit(blk.vector, _, sVE)
            logev_done[t] = sVE.n
        else:
            spill_done[t] = spill_done[t - 1] if t else 0
            router_done[t] = router_done[t - 1] if t else 0
            logev_done[t] = logev_done[t - 1] if t else 0

        kp3 = g3(Kps[:], 128)
        rro3 = g3(rro[:], 128)

        def _(v, t=t, cos_t=cos_t, sin_t=sin_t, kp3=kp3):
            v.wait_ge(sPE.h, kv_done[t])
            v.wait_ge(sCS[t % 2].h, 16 * (t // 2 + 1))
            sVE.inc(v.tensor_scalar(rro3[:, :, 0:64], kp3[:, :, 64:128],
                                    -1.0, None, ALU.mult))
            sVE.inc(v.tensor_scalar(rro3[:, :, 64:128], kp3[:, :, 0:64],
                                    1.0, None, ALU.mult))
            cos_b = cos_t.rearrange("p (o d) -> p o d", o=1).broadcast_to([128, 4, 128])
            sin_b = sin_t.rearrange("p (o d) -> p o d", o=1).broadcast_to([128, 4, 128])
            sVE.inc(v.tensor_tensor(g3(rt1[:], 128), kp3, cos_b, ALU.mult))
            v.wait_ge(sVE.h, sVE.n)
            sVE.inc(v.tensor_tensor(g3(rt2[:], 128), rro3, sin_b, ALU.mult))
            v.wait_ge(sVE.h, sVE.n)
            sVE.inc(v.tensor_add(rfo[:], rt1[:], rt2[:]))
        emit(blk.vector, _, sVE)
        rope_rd[t] = sVE.n
        v_rfo = sVE.n

        if t + 2 < NT:
            t2 = t + 2

            def _(s, t2=t2):
                s.wait_ge(sVE.h, v_rfo)
                s.wait_ge(sXQ[t2 % 2].h, sXQ[t2 % 2].n)
                s.wait_ge(sCS[t2 % 2].h, sCS[t2 % 2].n)
                sXQ[t2 % 2].inc(s.dma_start(
                    out=xq[:, (t2 % 2) * E:(t2 % 2 + 1) * E],
                    in_=x_in[t2 * 128:(t2 + 1) * 128, :]), 16)
                sCS[t2 % 2].inc(s.dma_start(
                    out=cst[:, (t2 % 2) * 256:(t2 % 2 + 1) * 256],
                    in_=cs_in[t2 * 128:(t2 + 1) * 128, :]), 16)
            emit(blk.sync, _)

        def _(sc, t=t):
            sc.wait_ge(sVE.h, v_rfo)
            sSC.inc(sc.activation(sq512[:], rfo[:], AF.Square))
        emit(blk.scalar, _, sSC)
        v_sq = sSC.n

        def _(v, t=t):
            v.wait_ge(sSC.h, v_sq)
            sVE.inc(v.reduce_sum(g3(nrm4[:], 1), g3(sq512[:], 128),
                                 axis=AX.X))
            sVE.inc(v.tensor_copy(Vsb[:, t * 512:(t + 1) * 512], Vps[:]))
        emit(blk.vector, _, sVE)
        vevac_done[t] = sVE.n
        v_nrm = sVE.n

        v_r4c = emit_norm_chain(v_nrm, eng="ve")

        def _(v, t=t):
            v.wait_ge(sGP.h, v_r4c)
            if t == 0:
                v.wait_ge(sCN.h, V_CONST)
            else:
                v.wait_ge(sVE.h, ktr_done[t - 1])
            for hh in range(4):
                sVE.inc(v.scalar_tensor_tensor(
                    rfin[:, hh * 128:(hh + 1) * 128],
                    rfo[:, hh * 128:(hh + 1) * 128],
                    rn4c[:, hh:hh + 1], ksc[:], ALU.mult, ALU.mult))
        emit(blk.vector, _, sVE)
        v_rfin = sVE.n

        ktrp = ktrA if t % 2 == 0 else ktrB

        def _(tn, ktrp=ktrp, t=t):
            tn.wait_ge(sVE.h, v_rfin)
            if t >= 2:
                tn.wait_ge(sVE.h, ktr_done[t - 2])
            for hh in range(4):
                sPE.inc(tn.transpose(ktrp[:, hh * 128:(hh + 1) * 128],
                                     rfin[:, hh * 128:(hh + 1) * 128],
                                     idb[:]))
        emit(blk.tensor, _, sPE)
        v_ktr = sPE.n

        def _(v, ktrp=ktrp, t=t):
            v.wait_ge(sPE.h, v_ktr)
            for hh in range(4):
                sVE.inc(v.tensor_copy(
                    KT[:, hh * SLC + t * 128:hh * SLC + (t + 1) * 128],
                    ktrp[:, hh * 128:(hh + 1) * 128]))
        emit(blk.vector, _, sVE)
        ktr_done[t] = sVE.n

    v_logits = logev_done[OT0 + NOWN - 1]
    lg3 = g3(logits[:], 8)

    def _(v):
        v.wait_ge(sVE.h, v_logits)
        sVE.inc(v.reduce_max(g3(r_["m1"][:], 1), lg3, axis=AX.X))
    emit(blk.vector, _, sVE)
    rv1 = sVE.n

    def _(g):
        g.wait_ge(sVE.h, rv1)
        sGP.inc(g.tensor_scalar(r_["m1c"][:], r_["m1"][:], 1.0, None, ALU.mult))
    emit(blk.gpsimd, _, sGP)
    rg1 = sGP.n

    def _(v):
        v.wait_ge(sGP.h, rg1)
        m1b = g3(r_["m1c"][:], 1).broadcast_to([128, NOWN, 8])
        sVE.inc(v.tensor_tensor(g3(rb_["eqt"][:], 8), lg3, m1b, ALU.is_equal))
        sVE.inc(v.tensor_tensor(g3(rb_["sh"][:], 8), lg3, m1b, ALU.subtract))
    emit(blk.vector, _, sVE)
    rv2 = sVE.n

    def _(sc):
        sc.wait_ge(sVE.h, rv2)
        sSC.inc(sc.activation(rb_["ew"][:], rb_["sh"][:], AF.Exp))
    emit(blk.scalar, _, sSC)
    rs1 = sSC.n

    def _(v):
        sVE.inc(v.scalar_tensor_tensor(rb_["l2"][:], rb_["eqt"][:], -1e30,
                                       logits[:], ALU.mult, ALU.add))
        v.wait_ge(sVE.h, sVE.n)
        sVE.inc(v.reduce_max(g3(r_["m2"][:], 1), g3(rb_["l2"][:], 8),
                             axis=AX.X))
        v.wait_ge(sSC.h, rs1)
        sVE.inc(v.reduce_sum(g3(r_["ss"][:], 1), g3(rb_["ew"][:], 8),
                             axis=AX.X))
    emit(blk.vector, _, sVE)
    rv3 = sVE.n

    def _(g):
        g.wait_ge(sVE.h, rv3)
        sGP.inc(g.tensor_scalar(r_["m2c"][:], r_["m2"][:], 1.0, None, ALU.mult))
        sGP.inc(g.tensor_scalar(r_["ssc"][:], r_["ss"][:], 1.0, None, ALU.mult))
    emit(blk.gpsimd, _, sGP)
    rg2 = sGP.n

    def _(v):
        v.wait_ge(sGP.h, rg2)
        m2b = g3(r_["m2c"][:], 1).broadcast_to([128, NOWN, 8])
        sVE.inc(v.tensor_tensor(g3(rb_["keep"][:], 8), lg3, m2b, ALU.is_ge))
        sVE.inc(v.reciprocal(r_["rs"][:], r_["ssc"][:]))
        v.wait_ge(sVE.h, sVE.n)
        sVE.inc(v.tensor_mul(rb_["cmb0"][:], rb_["keep"][:], rb_["ew"][:]))
    emit(blk.vector, _, sVE)
    rv4 = sVE.n

    def _(g):
        g.wait_ge(sVE.h, rv4)
        sGP.inc(g.tensor_scalar(r_["rsc"][:], r_["rs"][:], 1.0, None, ALU.mult))
    emit(blk.gpsimd, _, sGP)
    rg3 = sGP.n

    def _(v):
        v.wait_ge(sGP.h, rg3)
        rsb = g3(r_["rsc"][:], 1).broadcast_to([128, NOWN, 8])
        sVE.inc(v.tensor_tensor(g3(comb[:], 8), g3(rb_["cmb0"][:], 8), rsb,
                                ALU.mult))
    emit(blk.vector, _, sVE)

    if DEBUG:
        def _(s):
            for c in (sPE, sVE, sSC, sGP):
                s.wait_ge(c.h, c.n)
            s.wait_ge(sAC.h, sAC.n)
            sSO.inc(s.dma_start(out=dbg["dbg_KT"], in_=KT[:]), 16)
            sSO.inc(s.dma_start(out=dbg["dbg_V"], in_=Vsb[:]), 16)
            sSO.inc(s.dma_start(out=dbg["dbg_logits"], in_=logits[:]), 16)
            sSO.inc(s.dma_start(out=dbg["dbg_comb"], in_=comb[:]), 16)
            sSO.inc(s.dma_start(out=dbg["dbg_xnimg"], in_=xnimg), 16)
        emit(blk.sync, _)

    A_PE, A_VE, A_SC, A_GP = sPE.n, sVE.n, sSC.n, sGP.n
    A_LD, A_GW, A_AC = sLD.n, sGW.n, sAC.n
    pa.close()

    pb = ExitStack()

    def bsb(n, s_, d_):
        return pb.enter_context(nc.sbuf_tensor(n, s_, d_))

    def bps(n, s_, d_):
        return pb.enter_context(nc.psum_tensor(n, s_, d_))

    bmask = bsb("bmask_sb", [128, 2 * 128], BF16)
    ttb = bsb("ttb", [128, 2 * 1024], BF16)
    xnbtq = bsb("xnbtq", [128, 4 * NEC * 128], BF16)
    wqo = bsb("wqo", [128, 2 * NEC * 512], BF16)
    QT = bsb("QT", [128, H * 512], BF16)
    OTb = bsb("OTb", [128, H * 512], BF16)
    ab = bsb("ab", [128, 2 * 1024], BF16)
    rqf = bsb("rqf", [128, 512], F32)
    asum = bsb("asum", [128, 512], F32)
    tpr = bsb("tpr", [128, 512], BF16)
    asb_bf = bsb("asb_bf", [128, 512], BF16)
    rcp1 = bsb("rcp1", [1, 512], F32)
    bcs = bsb("bcs", [128, 512], F32)
    xo = bsb("xo", [128, 2 * 2048], F32)
    outs = bsb("outs", [128, 4 * 512], F32)
    t512 = bsb("t512", [128, 512], F32)
    cstq = bsb("cstq", [128, 4 * 256], F32)

    qp = [bps(f"qp{i}", [128, 512], F32) for i in range(4)]
    s2 = [bps(f"s2_{i}", [128, 1024], F32) for i in range(2)]

    barred = set()
    s2init = [0]

    def barA(eng, key):
        if key not in barred:
            eng.wait_ge(sPE.h, A_PE)
            eng.wait_ge(sVE.h, A_VE)
            eng.wait_ge(sSC.h, A_SC)
            eng.wait_ge(sGP.h, A_GP)
            eng.wait_ge(sGW.h, A_GW)
            eng.wait_ge(sAC.h, A_AC)
            for cc in (sXQ[0], sXQ[1], sCS[0], sCS[1]):
                eng.wait_ge(cc.h, cc.n)
            barred.add(key)

    wqo_ld = {}
    wslot_free = [0, 0]
    qproj_done = {}
    rope_q_rd = {}
    qtr_evac = {}
    tanh_done = {}
    avden_done = {}
    ot_evac = {}
    rcp_done = {}
    dmm_done = {}
    woproj_done = {}
    asm_done = {}
    asm_dma = {}

    for qb in range(4):
        if qb == 0:
            def _(v):
                barA(v, "vec_init")
                sVE.inc(v.memset(s2[0][:], 0.0))
                sVE.inc(v.memset(s2[1][:], 0.0))
            emit(blk.vector, _, sVE)
            s2init[0] = sVE.n

        def _(s, qb=qb):
            barA(s, "sync")
            if qb == 0:
                sBM.inc(s.dma_start(out=bmask[:], in_=bm_in), 16)
            if qb > 0:
                s.wait_ge(sPE.h, qproj_done[(qb - 1, 3)])
                s.wait_ge(sVE.h, rope_q_rd[(qb - 1, 3)])
                s.wait_ge(sQL.h, sQL.n)
            for qt in range(4):
                ot = qb * 4 + qt
                sQL.inc(s.dma_start(
                    out=xnbtq[:, qt * E:(qt + 1) * E].rearrange(
                        "p (ec c) -> p ec c", c=128),
                    in_=xnimg_r[:, :, ot * 128:(ot + 1) * 128]), 16)
                sQL.inc(s.dma_start(
                    out=cstq[:, qt * 256:(qt + 1) * 256],
                    in_=cs_in[HALO + ot * 128:HALO + (ot + 1) * 128, :]), 16)
        emit(blk.sync, _)
        v_qld = sQL.n

        for hc in range(4):
            slot = hc % 2

            def _(g, qb=qb, hc=hc, slot=slot):
                barA(g, "gp")
                if wslot_free[slot]:
                    g.wait_ge(sPE.h, wslot_free[slot])
                g.wait_ge(sWQ[slot].h, sWQ[slot].n)
                sWQ[slot].inc(g.dma_start(
                    out=wqo[:, slot * (NEC * 512):(slot + 1) * (NEC * 512)],
                    in_=wq_in[:, hc * (NEC * 512):(hc + 1) * (NEC * 512)]), 16)
                wqo_ld[(qb, "wq", hc)] = sWQ[slot].n
            emit(blk.gpsimd, _, sGP)

            def _(tn, qb=qb, hc=hc, slot=slot):
                barA(tn, "pe")
                tn.wait_ge(sWQ[slot].h, wqo_ld[(qb, "wq", hc)])
                tn.wait_ge(sQL.h, v_qld)
                if hc >= 1:
                    tn.wait_ge(sVE.h, rope_q_rd[(qb, hc - 1)])
                elif qb > 0:
                    tn.wait_ge(sVE.h, asm_done[(qb - 1, 3)])
                last = None
                for qt in range(4):
                    for ec in range(NEC):
                        last = tn.matmul(
                            qp[qt][:],
                            xnbtq[:, qt * E + ec * 128:qt * E + (ec + 1) * 128],
                            wqo[:, slot * (NEC * 512) + ec * 512:
                                slot * (NEC * 512) + (ec + 1) * 512],
                            start=(ec == 0), stop=(ec == NEC - 1))
                sPE.inc(last)
            emit(blk.tensor, _, sPE)
            qproj_done[(qb, hc)] = sPE.n
            wslot_free[slot] = sPE.n

            for qt in range(4):
                qps3 = g3(qp[qt][:], 128)
                cos_q = cstq[:, qt * 256:qt * 256 + 128]
                sin_q = cstq[:, qt * 256 + 128:qt * 256 + 256]

                def _(v, qb=qb, hc=hc, qps3=qps3, cos_q=cos_q, sin_q=sin_q):
                    v.wait_ge(sPE.h, qproj_done[(qb, hc)])
                    sVE.inc(v.tensor_scalar(rro3[:, :, 0:64],
                                            qps3[:, :, 64:128],
                                            -1.0, None, ALU.mult))
                    sVE.inc(v.tensor_scalar(rro3[:, :, 64:128],
                                            qps3[:, :, 0:64],
                                            1.0, None, ALU.mult))
                    cos_b = cos_q.rearrange("p (o d) -> p o d", o=1).broadcast_to([128, 4, 128])
                    sin_b = sin_q.rearrange("p (o d) -> p o d", o=1).broadcast_to([128, 4, 128])
                    sVE.inc(v.tensor_tensor(g3(rt1[:], 128), qps3, cos_b,
                                            ALU.mult))
                    v.wait_ge(sVE.h, sVE.n)
                    sVE.inc(v.tensor_tensor(g3(rt2[:], 128), rro3, sin_b,
                                            ALU.mult))
                    v.wait_ge(sVE.h, sVE.n)
                    sVE.inc(v.tensor_add(rfo[:], rt1[:], rt2[:]))
                emit(blk.vector, _, sVE)
                v_qrfo = sVE.n
                if qt == 3:
                    rope_q_rd[(qb, hc)] = sVE.n

                def _(sc):
                    sc.wait_ge(sVE.h, v_qrfo)
                    for hh in range(4):
                        sSC.inc(sc.activation(
                            sq512[:, hh * 128:(hh + 1) * 128],
                            rfo[:, hh * 128:(hh + 1) * 128],
                            AF.Square, accum_out=nrm4[:, hh:hh + 1]))
                emit(blk.scalar, _, sSC)
                v_qn = sSC.n

                v_qr4c = emit_norm_chain(v_qn)

                def _(v):
                    v.wait_ge(sGP.h, v_qr4c)
                    for hh in range(4):
                        sVE.inc(v.scalar_tensor_tensor(
                            rqf[:, hh * 128:(hh + 1) * 128],
                            rfo[:, hh * 128:(hh + 1) * 128],
                            rn4c[:, hh:hh + 1], qsc[:], ALU.mult, ALU.mult))
                emit(blk.vector, _, sVE)
                v_qfin = sVE.n

                tsl = (hc * 4 + qt) % 2

                def _(tn, qb=qb, hc=hc, qt=qt, tsl=tsl):
                    tn.wait_ge(sVE.h, v_qfin)
                    if qb == 0 and hc == 0 and qt == 0:
                        tn.wait_ge(sVE.h, s2init[0])
                    n = hc * 4 + qt
                    if n >= 2:
                        tn.wait_ge(sSC.h, qtr_evac[(qb, n - 2)])
                    elif qb > 0:
                        if n == 0:
                            tn.wait_ge(sSC.h, tanh_done[(qb - 1, H - 1, 8)])
                        else:
                            tn.wait_ge(sVE.h, ot_evac[(qb - 1, H - 1)])
                    for hh in range(4):
                        sPE.inc(tn.transpose(
                            s2[tsl][:, hh * 128:(hh + 1) * 128],
                            rqf[:, hh * 128:(hh + 1) * 128], idf[:]))
                emit(blk.tensor, _, sPE)
                v_qtr = sPE.n

                def _(sc, qb=qb, hc=hc, qt=qt, tsl=tsl):
                    sc.wait_ge(sPE.h, v_qtr)
                    for hh in range(4):
                        h = hc * 4 + hh
                        sSC.inc(sc.copy(
                            QT[:, h * 512 + qt * 128:h * 512 + (qt + 1) * 128],
                            s2[tsl][:, hh * 128:(hh + 1) * 128]))
                emit(blk.scalar, _, sSC)
                qtr_evac[(qb, hc * 4 + qt)] = sSC.n

        def _load_wo(ec, qb=qb):
            slot = ec % 2

            def _(g, qb=qb, ec=ec, slot=slot):
                if wslot_free[slot]:
                    g.wait_ge(sPE.h, wslot_free[slot])
                g.wait_ge(sWQ[slot].h, sWQ[slot].n)
                sWQ[slot].inc(g.dma_start(
                    out=wqo[:, slot * (NEC * 512):(slot + 1) * (NEC * 512)],
                    in_=wo_in[:, ec * (H * 512):(ec + 1) * (H * 512)]), 16)
                wqo_ld[(qb, "wo", ec)] = sWQ[slot].n
            emit(blk.gpsimd, _, sGP)

        for ec in range(2):
            _load_wo(ec)

        NPAIR = NKT // 2
        for h in range(H):
            kv = h // 4
            otp = qp[0] if h % 2 == 0 else qp[2]
            dnp = qp[2] if h % 2 == 0 else qp[0]
            v_sc = {}

            def emit_scores(pi, h=h, kv=kv, qb=qb):
                psl = pi % 2
                kt0 = qb * 4 + 2 * pi

                def _(tn, pi=pi, psl=psl, kt0=kt0, h=h, kv=kv, qb=qb):
                    if h == 0 and pi == 0:
                        tn.wait_ge(sSC.h, qtr_evac[(qb, 15)])
                    if pi >= 2:
                        tn.wait_ge(sSC.h, tanh_done[(qb, h, pi - 2)])
                    elif pi == 0:
                        if h > 0:
                            tn.wait_ge(sSC.h, tanh_done[(qb, h - 1, 8)])
                        elif qb > 0:
                            tn.wait_ge(sSC.h, tanh_done[(qb - 1, H - 1, 8)])
                    else:
                        if h > 0:
                            tn.wait_ge(sVE.h, ot_evac[(qb, h - 1)])
                        elif qb > 0:
                            tn.wait_ge(sVE.h, ot_evac[(qb - 1, H - 1)])
                    last = None
                    for half in range(2):
                        ql, qh = qrange(2 * pi + half)
                        last = tn.matmul(
                            s2[psl][:, half * 512 + ql:half * 512 + qh],
                            KT[:, kv * SLC + (kt0 + half) * 128:
                               kv * SLC + (kt0 + half + 1) * 128],
                            QT[:, h * 512 + ql:h * 512 + qh],
                            start=True, stop=True)
                    sPE.inc(last)
                emit(blk.tensor, _, sPE)
                v_sc[pi] = sPE.n

            emit_scores(0)
            for pi in range(NPAIR):
                psl = pi % 2
                kt0 = qb * 4 + 2 * pi

                def _(sc, psl=psl, pi=pi, h=h, qb=qb):
                    sc.wait_ge(sPE.h, v_sc[pi])
                    sSC.inc(sc.activation(ttb[:, psl * 1024:(psl + 1) * 1024],
                                          s2[psl][:], AF.Tanh,
                                          scale=TANH_SC))
                    tanh_done[(qb, h, pi)] = sSC.n
                    sc.wait_ge(sSC.h, sSC.n)
                    if pi >= 2:
                        sc.wait_ge(sPE.h, avden_done[(qb, h, pi - 2)])
                    elif h > 0:
                        sc.wait_ge(sPE.h,
                                   avden_done[(qb, h - 1, NPAIR - 2 + pi)])
                    elif qb > 0:
                        sc.wait_ge(sPE.h,
                                   avden_done[(qb - 1, H - 1, NPAIR - 2 + pi)])
                    sSC.inc(sc.activation(ab[:, psl * 1024:(psl + 1) * 1024],
                                          ttb[:, psl * 1024:(psl + 1) * 1024],
                                          AF.Exp, scale=CAP))
                emit(blk.scalar, _, sSC)
                v_exp = sSC.n

                v_msk = 0
                mask_ops = []
                for half in range(2):
                    kt = kt0 + half
                    dr = 2 * pi + half - 8
                    pad = kt < 8 or kt >= 24
                    base = psl * 1024 + half * 512
                    run0 = None
                    for j in range(4):
                        diff = dr - j
                        if abs(diff) <= 7:
                            st_ = "full"
                        elif diff == -8:
                            st_ = "ge"
                        elif diff == 8:
                            st_ = "le"
                        else:
                            st_ = "zero"
                        if st_ == "zero":
                            if run0 is None:
                                run0 = j
                        else:
                            if run0 is not None:
                                mask_ops.append((base + run0 * 128,
                                                 (j - run0) * 128, 0))
                                run0 = None
                            if st_ == "ge":
                                mask_ops.append((base + j * 128, 128, 1))
                            elif st_ == "le":
                                mask_ops.append((base + j * 128, 128, 2))
                    if run0 is not None:
                        mask_ops.append((base + run0 * 128,
                                         (4 - run0) * 128, 0))
                    if pad:
                        mask_ops.append((base, 512, 3, kt))
                def _(v, mask_ops=mask_ops, psl=psl, pi=pi):
                    if mask_ops:
                        v.wait_ge(sBM.h, 16)
                    v.wait_ge(sSC.h, v_exp)
                    for op in mask_ops:
                        if op[2] == 0:
                            sVE.inc(v.memset(ab[:, op[0]:op[0] + op[1]],
                                             0.0))
                        elif op[2] in (1, 2):
                            tri = bmask[:, (op[2] - 1) * 128:op[2] * 128]
                            sVE.inc(v.tensor_tensor(
                                ab[:, op[0]:op[0] + op[1]],
                                ab[:, op[0]:op[0] + op[1]],
                                tri, ALU.mult))
                        else:
                            sVE.inc(v.tensor_scalar(
                                ab[:, op[0]:op[0] + op[1]],
                                ab[:, op[0]:op[0] + op[1]],
                                kval[:, op[3]:op[3] + 1],
                                None, ALU.mult))
                        v.wait_ge(sVE.h, sVE.n)
                    sVE.inc(v.tensor_tensor(tpr[:],
                                            ab[:, psl * 1024:psl * 1024 + 512],
                                            ab[:, psl * 1024 + 512:
                                               psl * 1024 + 1024],
                                            ALU.add))
                    v.wait_ge(sVE.h, sVE.n)
                    if pi == 0:
                        sVE.inc(v.tensor_copy(asum[:], tpr[:]))
                    else:
                        sVE.inc(v.tensor_tensor(asum[:], asum[:], tpr[:],
                                                ALU.add))
                emit(blk.vector, _, sVE)
                v_msk = sVE.n

                if pi + 1 < NPAIR:
                    emit_scores(pi + 1)

                def _(tn, psl=psl, pi=pi, h=h, kt0=kt0, kv=kv, otp=otp,
                      v_msk=v_msk, qb=qb):
                    tn.wait_ge(sVE.h, v_msk)
                    if pi == 0:
                        if h >= 2:
                            tn.wait_ge(sVE.h, ot_evac[(qb, h - 2)])
                        elif qb > 0:
                            tn.wait_ge(sVE.h, ot_evac[(qb - 1, H - 2 + h)])
                        if h >= 1:
                            tn.wait_ge(sVE.h, rcp_done[(qb, h - 1)])
                        elif qb > 0:
                            tn.wait_ge(sVE.h, rcp_done[(qb - 1, H - 1)])
                    last = None
                    for half in range(2):
                        kt = kt0 + half
                        ql, qh = qrange(2 * pi + half)
                        if pi == 0 and half == 0:
                            ql, qh = 0, 512
                        a_ap = ab[:, psl * 1024 + half * 512 + ql:
                                  psl * 1024 + half * 512 + qh]
                        last = tn.matmul(
                            otp[:, ql:qh],
                            Vsb[:, kt * 512 + kv * 128:
                                kt * 512 + (kv + 1) * 128],
                            a_ap, start=(pi == 0 and half == 0),
                            stop=(pi == NPAIR - 1 and half == 1))
                    sPE.inc(last)
                emit(blk.tensor, _, sPE)
                avden_done[(qb, h, pi)] = sPE.n

            def _(v, h=h, qb=qb):
                if dmm_done.get((qb, h - 1)):
                    v.wait_ge(sPE.h, dmm_done[(qb, h - 1)])
                elif qb > 0 and h == 0:
                    v.wait_ge(sPE.h, dmm_done[(qb - 1, H - 1)])
                sVE.inc(v.tensor_copy(asb_bf[:], asum[:]))
            emit(blk.vector, _, sVE)
            v_acast = sVE.n

            def _(tn, h=h, dnp=dnp, qb=qb):
                tn.wait_ge(sVE.h, v_acast)
                if h >= 1:
                    tn.wait_ge(sVE.h, ot_evac[(qb, h - 1)])
                elif qb > 0:
                    tn.wait_ge(sVE.h, ot_evac[(qb - 1, H - 1)])
                sPE.inc(tn.matmul(dnp[0:1, :], onesc[:], asb_bf[:],
                                  start=True, stop=True))
                dmm_done[(qb, h)] = sPE.n
            emit(blk.tensor, _, sPE)
            v_dmm = sPE.n

            def _(v, h=h, dnp=dnp, qb=qb):
                v.wait_ge(sPE.h, v_dmm)
                sVE.inc(v.reciprocal(rcp1[:], dnp[0:1, :]))
                rcp_done[(qb, h)] = sVE.n
            emit(blk.vector, _, sVE)
            v_rcp = sVE.n

            def _(tn, qb=qb, h=h):
                tn.wait_ge(sVE.h, v_rcp)
                tn.wait_ge(sSC.h, tanh_done[(qb, h, NPAIR - 1)])
                sPE.inc(tn.matmul(s2[1][:, 0:512], ones1[:], rcp1[:],
                                  start=True, stop=True))
            emit(blk.tensor, _, sPE)
            v_bc = sPE.n

            def _(v, h=h, otp=otp, qb=qb):
                v.wait_ge(sPE.h, v_bc)
                if h == 0 and qb > 0:
                    v.wait_ge(sPE.h, woproj_done[(qb - 1, 3)])
                sVE.inc(v.tensor_copy(bcs[:], s2[1][:, 0:512]))
                v.wait_ge(sVE.h, sVE.n)
                sVE.inc(v.tensor_tensor(OTb[:, h * 512:(h + 1) * 512],
                                        otp[:], bcs[:], ALU.mult))
            emit(blk.vector, _, sVE)
            ot_evac[(qb, h)] = sVE.n

        x_tpe = x_in.rearrange("(tt p) e -> p tt e", p=128)
        xo_ld = {}
        for ec in range(4):
            slot = ec % 2

            def _(s, qb=qb, ec=ec, slot=slot):
                if ec >= 2:
                    s.wait_ge(sVE.h, asm_done[(qb, ec - 2)])
                elif qb > 0:
                    s.wait_ge(sVE.h, asm_done[(qb - 1, 2 + ec)])
                s.wait_ge(sXO[slot].h, sXO[slot].n)
                sXO[slot].inc(s.dma_start(
                    out=xo[:, slot * 2048:(slot + 1) * 2048].rearrange(
                        "p (q c) -> p q c", c=512),
                    in_=x_tpe[:, OT0 + qb * 4:OT0 + qb * 4 + 4,
                              ec * 512:(ec + 1) * 512]), 16)
                xo_ld[ec] = sXO[slot].n
            emit(blk.sync, _)

            def _(tn, qb=qb, ec=ec, slot=slot):
                tn.wait_ge(sWQ[slot].h, wqo_ld[(qb, "wo", ec)])
                tn.wait_ge(sVE.h, ot_evac[(qb, H - 1)])
                if ec >= 1:
                    tn.wait_ge(sVE.h, asm_done[(qb, ec - 1)])
                last = None
                for qt in range(4):
                    for hcc in range(H):
                        last = tn.matmul(
                            qp[qt][:],
                            OTb[:, hcc * 512 + qt * 128:hcc * 512 + (qt + 1) * 128],
                            wqo[:, slot * (NEC * 512) + hcc * 512:
                                slot * (NEC * 512) + (hcc + 1) * 512],
                            start=(hcc == 0), stop=(hcc == H - 1))
                sPE.inc(last)
            emit(blk.tensor, _, sPE)
            woproj_done[(qb, ec)] = sPE.n
            wslot_free[slot] = sPE.n
            if ec + 2 <= 3:
                _load_wo(ec + 2)

            def _(v, qb=qb, ec=ec, slot=slot):
                v.wait_ge(sPE.h, woproj_done[(qb, ec)])
                v.wait_ge(sXO[ec % 2].h, xo_ld[ec])
                if asm_dma.get((qb, ec - 1)):
                    v.wait_ge(sSO.h, asm_dma[(qb, ec - 1)])
                elif qb > 0 and ec == 0:
                    v.wait_ge(sSO.h, asm_dma[(qb - 1, 3)])
                for qt in range(4):
                    ot = qb * 4 + qt
                    sVE.inc(v.tensor_scalar(t512[:],
                                            xo[:, slot * 2048 + qt * 512:
                                                slot * 2048 + (qt + 1) * 512],
                                            stash_opr[:, ot:ot + 1],
                                            stash_nb[:, ot:ot + 1],
                                            ALU.mult, ALU.add))
                    v.wait_ge(sVE.h, sVE.n)
                    sVE.inc(v.tensor_tensor(outs[:, qt * 512:(qt + 1) * 512],
                                            t512[:], qp[qt][:], ALU.add))
                    v.wait_ge(sVE.h, sVE.n)
            emit(blk.vector, _, sVE)
            asm_done[(qb, ec)] = sVE.n

            def _(s, qb=qb, ec=ec):
                s.wait_ge(sVE.h, asm_done[(qb, ec)])
                for qt in range(4):
                    sSO.inc(s.dma_start(
                        out=out_tpe[:, qb * 4 + qt, ec * 512:(ec + 1) * 512],
                        in_=outs[:, qt * 512:(qt + 1) * 512]), 16)
            emit(blk.sync, _)
            asm_dma[(qb, ec)] = sSO.n

    if DEBUG:
        def _(s):
            s.wait_ge(sSC.h, qtr_evac[(3, 15)])
            sSO.inc(s.dma_start(out=dbg["dbg_QT"], in_=QT[:]), 16)
            s.wait_ge(sVE.h, ot_evac[(3, H - 1)])
            sSO.inc(s.dma_start(out=dbg["dbg_OT"], in_=OTb[:]), 16)
        emit(blk.sync, _)

    B_PE, B_VE, B_SC, B_GP = sPE.n, sVE.n, sSC.n, sGP.n
    B_SO, B_LD, B_GW = sSO.n, sLD.n, sGW.n
    pb.close()
    pab.close()

    pc = ExitStack()

    def csb(n, s_, d_):
        return pc.enter_context(nc.sbuf_tensor(n, s_, d_))

    def cps(n, s_, d_):
        return pc.enter_context(nc.psum_tensor(n, s_, d_))

    xnTf = csb("xnTf", [128, NEC * OWN], BF16)
    hsb = csb("hsb", [128, NIC * OWN], BF16)
    wgq = csb("wgq", [128, 2 * QBUF], BF16)
    wuq = csb("wuq", [128, 2 * QBUF], BF16)
    wdc = csb("wdc", [128, 2 * NIC * 512], BF16)
    sgs = csb("sgs", [128, 2 * 512], F32)
    evb = csb("evb", [128, 2 * 4 * 512], BF16)

    gp_ = [cps(f"gp{i}", [128, 512], F32) for i in range(4)]
    up_ = [cps(f"up{i}", [128, 512], F32) for i in range(4)]

    cbarred = set()

    def barB(eng, key):
        if key not in cbarred:
            eng.wait_ge(sPE.h, B_PE)
            eng.wait_ge(sVE.h, B_VE)
            eng.wait_ge(sSC.h, B_SC)
            eng.wait_ge(sGP.h, B_GP)
            for cc in (sQL, sXO[0], sXO[1], sWQ[0], sWQ[1]):
                eng.wait_ge(cc.h, cc.n)
            cbarred.add(key)

    def _(s):
        barB(s, "sync")
        sLD.inc(s.dma_start(out=xnTf[:], in_=xnimg), 16)
    emit(blk.sync, _)
    v_xnTf = sLD.n

    def _(g):
        barB(g, "gp")
        g.wait_ge(sSO.h, B_SO)
    emit(blk.gpsimd, _, sGP)

    wld = {}
    sgs_free = [0, 0]
    wgu_free = [0, 0]
    wd_free = [0, 0]
    gu_done = {}
    h_evac = {}
    dn_evac = {}
    acc_dma = {}
    down_mm_last = [0] * NEXP

    def _load_gu(qidx):
        if qidx >= NEXP * 4:
            return
        e, gq = qidx // 4, qidx % 4
        slot = qidx % 2

        def _(g, e=e, gq=gq, slot=slot):
            if wgu_free[slot]:
                g.wait_ge(sPE.h, wgu_free[slot])
            g.wait_ge(sWG[slot].h, sWG[slot].n)
            g.wait_ge(sWU[slot].h, sWU[slot].n)
            cols = NEC * GCOL[gq]
            sWG[slot].inc(g.dma_start(
                out=wgq[:, slot * QBUF:slot * QBUF + cols],
                in_=wg_in[e, :, GOFF[gq]:GOFF[gq] + cols]), 16)
            wld[(e, "g", gq)] = sWG[slot].n
            sWU[slot].inc(g.dma_start(
                out=wuq[:, slot * QBUF:slot * QBUF + cols],
                in_=wu_in[e, :, GOFF[gq]:GOFF[gq] + cols]), 16)
            wld[(e, "u", gq)] = sWU[slot].n
        emit(blk.gpsimd, _, sGP)

    def _load_wd(didx):
        if didx >= NEXP * 4:
            return
        e, ecq = didx // 4, didx % 4
        slot = didx % 2

        def _(g, e=e, ecq=ecq, slot=slot):
            if wd_free[slot]:
                g.wait_ge(sPE.h, wd_free[slot])
            g.wait_ge(sWD[slot].h, sWD[slot].n)
            sWD[slot].inc(g.dma_start(
                out=wdc[:, slot * (NIC * 512):(slot + 1) * (NIC * 512)],
                in_=wd_in[e, :, ecq * (NIC * 512):(ecq + 1) * (NIC * 512)]), 16)
            wld[(e, "d", ecq)] = sWD[slot].n
        emit(blk.gpsimd, _, sGP)

    _load_gu(0)
    _load_gu(1)
    _load_wd(0)
    _load_wd(1)

    for e in range(NEXP):
        for gq in range(4):
            slot = (e * 4 + gq) % 2
            for icl in range(GGRP[gq][1] - GGRP[gq][0]):
                ic = GGRP[gq][0] + icl
                for half in range(2):
                    bank = ([gp_[0], gp_[1], up_[0], up_[1]] if half == 0
                            else [gp_[2], gp_[3], up_[2], up_[3]])
                    tc0, tc1 = 2 * half, 2 * half + 1

                    def _(tn, e=e, gq=gq, icl=icl, ic=ic, half=half,
                          slot=slot, bank=bank, tc0=tc0, tc1=tc1):
                        barB(tn, "pe")
                        tn.wait_ge(sWG[slot].h, wld[(e, "g", gq)])
                        tn.wait_ge(sWU[slot].h, wld[(e, "u", gq)])
                        if e == 0 and gq == 0 and icl == 0 and half == 0:
                            tn.wait_ge(sLD.h, v_xnTf)
                        if ic > 0:
                            tn.wait_ge(sVE.h, h_evac[(e, ic - 1, half)])
                        elif e > 0:
                            tn.wait_ge(sVE.h, dn_evac[(e - 1, 3, NOWN - 1)])
                        base = slot * QBUF
                        last = None
                        for ec in range(NEC):
                            off = base + ec * GCOL[gq] + icl * 128
                            lg = wgq[:, off:off + 128]
                            lu = wuq[:, off:off + 128]
                            x0 = xnTf[:, ec * OWN + tc0 * 512:
                                      ec * OWN + (tc0 + 1) * 512]
                            x1 = xnTf[:, ec * OWN + tc1 * 512:
                                      ec * OWN + (tc1 + 1) * 512]
                            tn.matmul(bank[0][:], lg, x0, start=(ec == 0),
                                      stop=(ec == NEC - 1))
                            tn.matmul(bank[1][:], lg, x1, start=(ec == 0),
                                      stop=(ec == NEC - 1))
                            tn.matmul(bank[2][:], lu, x0, start=(ec == 0),
                                      stop=(ec == NEC - 1))
                            last = tn.matmul(bank[3][:], lu, x1,
                                             start=(ec == 0),
                                             stop=(ec == NEC - 1))
                        sPE.inc(last)
                    emit(blk.tensor, _, sPE)
                    gu_done[(e, ic, half)] = sPE.n

                    for k in range(2):
                        tc = 2 * half + k
                        gbank = bank[k]
                        ubank = bank[2 + k]
                        ssl = k

                        def _(sc, e=e, ic=ic, half=half, gbank=gbank, ssl=ssl):
                            sc.wait_ge(sPE.h, gu_done[(e, ic, half)])
                            if sgs_free[ssl]:
                                sc.wait_ge(sVE.h, sgs_free[ssl])
                            sSC.inc(sc.activation(
                                sgs[:, ssl * 512:(ssl + 1) * 512],
                                gbank[:], AF.Sigmoid))
                        emit(blk.scalar, _, sSC)
                        v_sg = sSC.n

                        def _(v, e=e, ic=ic, tc=tc, ubank=ubank, ssl=ssl):
                            v.wait_ge(sSC.h, v_sg)
                            if e > 0 and ic == 0 and tc == 0:
                                v.wait_ge(sPE.h, down_mm_last[e - 1])
                            sVE.inc(v.tensor_tensor(
                                hsb[:, ic * OWN + tc * 512:
                                    ic * OWN + (tc + 1) * 512],
                                sgs[:, ssl * 512:(ssl + 1) * 512],
                                ubank[:], ALU.mult))
                        emit(blk.vector, _, sVE)
                        sgs_free[ssl] = sVE.n
                    h_evac[(e, ic, half)] = sVE.n

            wgu_free[slot] = sPE.n
            _load_gu(e * 4 + gq + 2)

        for ecq in range(4):
            slot = (e * 4 + ecq) % 2
            for tt in range(NOWN):
                dps = gp_[tt % 2]

                def _(tn, e=e, ecq=ecq, tt=tt, slot=slot, dps=dps):
                    tn.wait_ge(sWD[slot].h, wld[(e, "d", ecq)])
                    if ecq == 0 and tt < 2:
                        tn.wait_ge(sVE.h, h_evac[(e, NIC - 1, 1)])
                        tn.wait_ge(sVE.h, h_evac[(e, NIC - 1, 0)])
                    elif tt >= 2:
                        tn.wait_ge(sVE.h, dn_evac[(e, ecq, tt - 2)])
                    else:
                        tn.wait_ge(sVE.h, dn_evac[(e, ecq - 1, NOWN - 2 + tt)])
                    last = None
                    for ic in range(NIC):
                        last = tn.matmul(
                            dps[:],
                            hsb[:, ic * OWN + tt * 128:ic * OWN + (tt + 1) * 128],
                            wdc[:, slot * (NIC * 512) + ic * 512:
                                slot * (NIC * 512) + (ic + 1) * 512],
                            start=(ic == 0), stop=(ic == NIC - 1))
                    sPE.inc(last)
                emit(blk.tensor, _, sPE)
                v_dn = sPE.n
                if tt == NOWN - 1:
                    wd_free[slot] = sPE.n
                    if ecq == 3:
                        down_mm_last[e] = sPE.n
                    _load_wd(e * 4 + ecq + 2)

                def _(v, e=e, ecq=ecq, tt=tt, dps=dps):
                    v.wait_ge(sPE.h, v_dn)
                    grp = (tt // 4) % 2
                    if tt % 4 == 0:
                        gi = e * 16 + ecq * 4 + tt // 4
                        if gi >= 2 and acc_dma.get(gi - 2):
                            v.wait_ge(sEV[grp].h, acc_dma[gi - 2])
                    dst = evb[:, grp * 2048 + (tt % 4) * 512:
                              grp * 2048 + (tt % 4 + 1) * 512]
                    if e == 0:
                        sVE.inc(v.tensor_copy(dst, dps[:]))
                    else:
                        sVE.inc(v.tensor_scalar(
                            dst, dps[:],
                            comb[:, tt * 8 + (e - 1):tt * 8 + e],
                            None, ALU.mult))
                emit(blk.vector, _, sVE)
                dn_evac[(e, ecq, tt)] = sVE.n

                if tt % 4 == 3:
                    gslot = (tt // 4) % 2

                    def _(g, e=e, ecq=ecq, tt=tt, grp=gslot):
                        g.wait_ge(sVE.h, dn_evac[(e, ecq, tt)])
                        g.wait_ge(sEV[grp].h, sEV[grp].n)
                        sEV[grp].inc(g.dma_start(
                            out=out_tpe[:, tt - 3:tt + 1,
                                        ecq * 512:(ecq + 1) * 512],
                            in_=evb[:, grp * 2048:(grp + 1) * 2048].rearrange(
                                "p (q c) -> p q c", c=512),
                            accum_op=ALU.add), 16)
                    emit(blk.gpsimd, _, sGP)
                    acc_dma[e * 16 + ecq * 4 + tt // 4] = sEV[gslot].n

        if DEBUG and e == 1:
            def _(s, e=e):
                s.wait_ge(sVE.h, h_evac[(e, NIC - 1, 1)])
                sSO.inc(s.dma_start(out=dbg["dbg_h"], in_=hsb[:]), 16)
            emit(blk.sync, _)

    def _(g):
        g.wait_ge(sAC.h, sAC.n)
        g.wait_ge(sSO.h, sSO.n)
        g.wait_ge(sEV[0].h, sEV[0].n)
        g.wait_ge(sEV[1].h, sEV[1].n)
    emit(blk.gpsimd, _, sGP)

    def _(s):
        s.wait_ge(sAC.h, sAC.n)
        s.wait_ge(sSO.h, sSO.n)
        s.wait_ge(sEV[0].h, sEV[0].n)
        s.wait_ge(sEV[1].h, sEV[1].n)
    emit(blk.sync, _)

    pc.close()
    es.close()
    return nc


def _host_prep(inputs):
    x = np.asarray(inputs["L0_attn_hidden_states"])
    wq = np.asarray(inputs["L0_attn_wq"])
    wk = np.asarray(inputs["L0_attn_wk"])
    wv = np.asarray(inputs["L0_attn_wv"])
    wo = np.asarray(inputs["L0_attn_wo"])
    qscale = np.asarray(inputs["L0_attn_query_scale"])
    kscale = np.asarray(inputs["L0_attn_key_scale"])
    rw = np.asarray(inputs["L0_ffn_router_w"])
    sg = np.asarray(inputs["L0_ffn_shared_gate"])
    su = np.asarray(inputs["L0_ffn_shared_up"])
    sd = np.asarray(inputs["L0_ffn_shared_down"])
    wg = np.asarray(inputs["L0_ffn_w_gate"])
    wu = np.asarray(inputs["L0_ffn_w_up"])
    wd = np.asarray(inputs["L0_ffn_w_down"])

    def pmajor(a):
        r, c = a.shape
        return np.ascontiguousarray(
            a.reshape(r // 128, 128, c).transpose(1, 0, 2).reshape(128, -1))

    wq4 = np.ascontiguousarray(
        wq.reshape(NEC, 128, 4, 512).transpose(1, 2, 0, 3).reshape(128, -1))
    wo4 = np.ascontiguousarray(
        wo.reshape(H, 128, 4, 512).transpose(1, 2, 0, 3).reshape(128, -1))
    wk_h = pmajor(wk)
    wv_h = pmajor(wv)
    rw_h = pmajor(np.ascontiguousarray(rw.T))

    def gate_h(w):
        wT = np.ascontiguousarray(w.T)
        t = wT.reshape(NEC, 128, NIC, 128)
        parts = [np.ascontiguousarray(
            t[:, :, a:b, :].transpose(1, 0, 2, 3).reshape(128, -1))
            for a, b in GGRP]
        return np.concatenate(parts, axis=1)

    def down_h(w):
        wT = np.ascontiguousarray(w.T)
        t = wT.reshape(NIC, 128, 4, 512)
        return np.ascontiguousarray(t.transpose(1, 2, 0, 3).reshape(128, -1))

    wg_h = np.stack([gate_h(sg)] + [gate_h(wg[i]) for i in range(8)])
    wu_h = np.stack([gate_h(su)] + [gate_h(wu[i]) for i in range(8)])
    wd_h = np.stack([down_h(sd)] + [down_h(wd[i]) for i in range(8)])

    half = D // 2
    inv_freq = 1.0 / (10000.0 ** (np.arange(half, dtype=np.float64) / half))

    ar = np.arange(128)
    tri_ge = (ar[:, None] >= ar[None, :]).astype(np.float32)
    tri_le = (ar[:, None] <= ar[None, :]).astype(np.float32)
    bmask = np.concatenate([tri_ge, tri_le], axis=1).astype(BF)

    common = dict(
        wq_h=wq4, wk_h=wk_h, wv_h=wv_h, wo_h=wo4, rw_h=rw_h,
        wg_h=wg_h, wu_h=wu_h, wd_h=wd_h,
        qscale_t=np.ascontiguousarray(np.tile(qscale[None, :], (128, 1))),
        kscale_t=np.ascontiguousarray(np.tile(kscale[None, :], (128, 1))),
        identf=np.eye(128, dtype=np.float32),
        identb=np.eye(128, dtype=np.float32).astype(BF),
        onescol=np.ones((128, 1), dtype=BF),
        onesrow=np.ones((1, 128), dtype=np.float32),
        epsln=np.full((128, 1), LNEPS, dtype=np.float32),
        eps30=np.full((128, 1), 1e-30, dtype=np.float32),
        bmask=bmask,
    )

    in_maps = []
    for c in range(NCORE):
        bi, qq = c // 4, (c % 4) * OWN
        lo, hi = qq - HALO, qq + OWN + HALO
        xs = np.zeros((SLC, E), dtype=np.float32)
        a, b_ = max(lo, 0), min(hi, S)
        xs[a - lo:b_ - lo] = x[bi, a:b_]
        pos = np.clip(np.arange(lo, hi), 0, S - 1).astype(np.float64)
        ang = np.outer(pos, inv_freq)
        cs = np.concatenate([np.cos(ang), np.cos(ang),
                             np.sin(ang), np.sin(ang)],
                            axis=1).astype(np.float32)
        kvalid = np.ones((128, NT), dtype=np.float32)
        rowpos = lo + (np.arange(NT)[None, :] * 128 + np.arange(128)[:, None])
        kvalid[(rowpos < 0) | (rowpos >= S)] = 0.0
        m = dict(common)
        m.update(x_slice=xs, cs=cs, kvalid01=kvalid)
        in_maps.append(m)
    return in_maps


def kernel(**inputs):
    in_maps = _host_prep(inputs)
    nc = build_nc()
    res = run_bass_kernel_spmd(nc, in_maps, core_ids=list(range(NCORE)))
    out = np.empty((B, S, E), dtype=np.float32)
    for c in range(NCORE):
        bi, qq = c // 4, (c % 4) * OWN
        out[bi, qq:qq + OWN] = res.results[c]["out"]
    return out


# version 08 (final)
# speedup vs baseline: 1.2717x; 1.2144x over previous
import math
from contextlib import ExitStack

import numpy as np
import ml_dtypes

import concourse.bass as bass
import concourse.bacc as bacc
import concourse.mybir as mybir
from concourse.bass_utils import run_bass_kernel_spmd
from concourse.library_config import mlp as MLP_LIB

F32 = mybir.dt.float32
BF16 = mybir.dt.bfloat16
I32 = mybir.dt.int32
I16 = mybir.dt.int16
AF = mybir.ActivationFunctionType
ALU = mybir.AluOpType
AX = mybir.AxisListType
BF = ml_dtypes.bfloat16
IndirectOffsetOnAxis = bass.IndirectOffsetOnAxis

B, S, E = 2, 8192, 2048
H, KV, D = 16, 4, 128
WIN, CAP = 1024, 50.0
EPS, LNEPS = 1e-6, 1e-5
IFF = 1408
NEXP = 9
NCORE = 8
OWN, HALO = 2048, 1024
SLC = OWN + 2 * HALO
NT = SLC // 128
OT0 = HALO // 128
NOWN = OWN // 128
NEC = E // 128
NIC = IFF // 128
NKT = 20
TANH_SC = 1.0 / (CAP * math.sqrt(D))
NEG = -30000.0
GGRP = ((0, 3), (3, 6), (6, 9), (9, 11))
GCOL = tuple((b - a) * 128 for a, b in GGRP)
GOFF = tuple(sum(NEC * GCOL[k] for k in range(g)) for g in range(4))
QBUF = NEC * 384
CAPE = 768
NSL = CAPE // 128
BIGV = 65536.0

DEBUG = False


class Ctr:
    def __init__(self, nc, name):
        self.h = nc.alloc_semaphore(name)
        self.n = 0

    def inc(self, instr, k=1):
        instr.then_inc(self.h, k)
        self.n += k
        return self.n


def qrange(idx):
    dr = idx - 8
    if dr <= -5:
        return 0, (dr + 9) * 128
    if dr >= 8:
        return (dr - 8) * 128, 512
    return 0, 512


def band_cls(idx):
    dr = idx - 8
    if -8 <= dr <= -5:
        return dr + 8
    if 8 <= dr <= 11:
        return dr - 4
    return None


TRASH = 8 * CAPE


def build_nc():
    nc = bacc.Bacc("TRN2")

    x_in = nc.dram_tensor("x_slice", [SLC, E], F32, kind="ExternalInput").ap()
    cs_in = nc.dram_tensor("cs", [SLC, 2 * D], F32, kind="ExternalInput").ap()
    kval_in = nc.dram_tensor("kvalid01", [128, NT], F32, kind="ExternalInput").ap()
    bm_in = nc.dram_tensor("bmask", [128, 2 * 128], BF16, kind="ExternalInput").ap()
    qs_in = nc.dram_tensor("qscale_t", [128, D], F32, kind="ExternalInput").ap()
    ks_in = nc.dram_tensor("kscale_t", [128, D], F32, kind="ExternalInput").ap()
    idf_in = nc.dram_tensor("identf", [128, 128], F32, kind="ExternalInput").ap()
    idb_in = nc.dram_tensor("identb", [128, 128], BF16, kind="ExternalInput").ap()
    onesc_in = nc.dram_tensor("onescol", [128, 1], BF16, kind="ExternalInput").ap()
    ones1_in = nc.dram_tensor("onesrow", [1, 128], F32, kind="ExternalInput").ap()
    epsln_in = nc.dram_tensor("epsln", [128, 1], F32, kind="ExternalInput").ap()
    eps30_in = nc.dram_tensor("eps30", [128, 1], F32, kind="ExternalInput").ap()
    wq_in = nc.dram_tensor("wq_h", [128, 4 * NEC * 512], F32, kind="ExternalInput").ap()
    wk_in = nc.dram_tensor("wk_h", [128, NEC * 512], F32, kind="ExternalInput").ap()
    wv_in = nc.dram_tensor("wv_h", [128, NEC * 512], F32, kind="ExternalInput").ap()
    wo_in = nc.dram_tensor("wo_h", [128, 4 * H * 512], F32, kind="ExternalInput").ap()
    rw_in = nc.dram_tensor("rw_h", [128, NEC * 8], F32, kind="ExternalInput").ap()
    wg_in = nc.dram_tensor("wg_h", [NEXP, 128, NEC * IFF], F32, kind="ExternalInput").ap()
    wu_in = nc.dram_tensor("wu_h", [NEXP, 128, NEC * IFF], F32, kind="ExternalInput").ap()
    wd_in = nc.dram_tensor("wd_h", [NEXP, 128, 4 * NIC * 512], F32, kind="ExternalInput").ap()
    ltri_in = nc.dram_tensor("ltri_h", [128, 128], F32, kind="ExternalInput").ap()
    ec8_in = nc.dram_tensor("ec8_h", [128, 8], F32, kind="ExternalInput").ap()
    chk_in = nc.dram_tensor("chk_h", [128, 128], F32, kind="ExternalInput").ap()
    btok64_in = nc.dram_tensor("btok64_h", [128, 16 * 512], F32, kind="ExternalInput").ap()
    ctp_in = nc.dram_tensor("ctpad_h", [128, 8 * 48], BF16, kind="ExternalInput").ap()
    idxf = nc.dram_tensor("idxf_h", [TRASH + 16, 64], F32, kind="ExternalInput").ap()
    macc = nc.dram_tensor("macc_h", [OWN + NSL, E], F32, kind="ExternalInput").ap()
    out_d = nc.dram_tensor("out", [OWN, E], F32, kind="ExternalOutput").ap()
    xnimg = nc.dram_tensor("xnimg", [128, NEC * OWN], BF16, kind="Internal").ap()
    xnrow = nc.dram_tensor("xnrow", [OWN + NSL, E], BF16, kind="Internal").ap()
    combd = nc.dram_tensor("combd_h", [OWN + NSL, 64], F32, kind="ExternalInput").ap()
    vmi_d = nc.dram_tensor("vmi_d", [128, 128], I16, kind="Internal").ap()

    out_tpe = out_d.rearrange("(tt p) e -> p tt e", p=128)
    xnimg_r = xnimg.rearrange("p (ec tk) -> p ec tk", tk=OWN)

    dbg = {}
    if DEBUG:
        for nm, shape, dt in [
            ("dbg_xnimg", [128, NEC * OWN], BF16),
            ("dbg_KT", [128, KV * SLC], BF16),
            ("dbg_V", [128, NT * 512], BF16),
            ("dbg_logits", [128, NOWN * 8], F32),
            ("dbg_comb", [128, NOWN * 8], F32),
            ("dbg_QT", [128, H * 512], BF16),
            ("dbg_OT", [128, H * 512], BF16),
            ("dbg_h", [128, NIC * OWN], BF16),
        ]:
            dbg[nm] = nc.dram_tensor(nm, shape, dt, kind="ExternalOutput").ap()

    es = ExitStack()

    def sb(name, shape, dt):
        return es.enter_context(nc.sbuf_tensor(name, shape, dt))

    def g3(ap, w):
        return ap.rearrange("p (g w) -> p g w", w=w)

    onesc = sb("onesc", [128, 1], BF16)
    ones1 = sb("ones1", [1, 128], F32)
    qsc = sb("qsc", [128, D], F32)
    ksc = sb("ksc", [128, D], F32)
    kval = sb("kval", [128, NT], F32)
    comb = sb("comb", [128, NOWN * 8], F32)
    epsln = sb("epsln_sb", [128, 1], F32)
    eps30 = sb("eps30_sb", [128, 1], F32)
    vmi16 = sb("vmi16", [128, NOWN * 8], I16)

    pab = ExitStack()

    def absb(n, s_, d_):
        return pab.enter_context(nc.sbuf_tensor(n, s_, d_))

    idb = absb("idb", [128, 128], BF16)
    idf = absb("idf_sb", [128, 128], F32)
    stash_opr = absb("stash_opr", [128, NOWN], F32)
    stash_nb = absb("stash_nb", [128, NOWN], F32)
    KT = absb("KT", [128, KV * SLC], BF16)
    Vsb = absb("Vsb", [128, NT * 512], BF16)

    st = {n: absb("st_" + n, [128, 1], F32)
          for n in ("ssq", "sx", "msq", "var", "std", "rstd", "nb", "rstdc")}
    nrm4 = absb("nrm4", [128, 4], F32)
    nrm4c = absb("nrm4c", [128, 4], F32)
    std4 = absb("std4", [128, 4], F32)
    den4 = absb("den4", [128, 4], F32)
    den4c = absb("den4c", [128, 4], F32)
    rn4 = absb("rn4", [128, 4], F32)
    rn4c = absb("rn4c", [128, 4], F32)
    rec4 = absb("rec4", [128, 4], F32)

    rro = absb("rro", [128, 512], F32)
    rt1 = absb("rt1", [128, 512], F32)
    rt2 = absb("rt2", [128, 512], F32)
    rfo = absb("rfo", [128, 512], F32)
    rfin = absb("rfin", [128, 512], BF16)
    sq512 = absb("sq512", [128, 512], F32)

    sLD = Ctr(nc, "sLD")
    sGW = Ctr(nc, "sGW")
    sPE = Ctr(nc, "sPE")
    sVE = Ctr(nc, "sVE")
    sSC = Ctr(nc, "sSC")
    sGP = Ctr(nc, "sGP")
    sSO = Ctr(nc, "sSO")
    sAC = Ctr(nc, "sAC")
    sCN = Ctr(nc, "sCN")
    sXQ = [Ctr(nc, "sXQ0"), Ctr(nc, "sXQ1")]
    sCS = [Ctr(nc, "sCS0"), Ctr(nc, "sCS1")]
    sQL = Ctr(nc, "sQL")
    sBM = Ctr(nc, "sBM")
    sXO = [Ctr(nc, "sXO0"), Ctr(nc, "sXO1")]
    sWQ = [Ctr(nc, "sWQ0"), Ctr(nc, "sWQ1")]
    sWG = [Ctr(nc, "sWG0"), Ctr(nc, "sWG1")]
    sWU = [Ctr(nc, "sWU0"), Ctr(nc, "sWU1")]
    sWD = [Ctr(nc, "sWD0"), Ctr(nc, "sWD1")]
    sEV = [Ctr(nc, "sEV0"), Ctr(nc, "sEV1")]
    sXR = [Ctr(nc, "sXR0"), Ctr(nc, "sXR1")]
    sBX = Ctr(nc, "sBX")
    sBI = Ctr(nc, "sBI")
    sBD = Ctr(nc, "sBD")
    sIX = Ctr(nc, "sIX")
    sBK = Ctr(nc, "sBK")
    sMG = Ctr(nc, "sMG")
    sTL = Ctr(nc, "sTL")
    sCG = Ctr(nc, "sCG")
    sSA = [Ctr(nc, f"sSA{i}") for i in range(4)]
    sAD = [Ctr(nc, f"sAD{i}") for i in range(4)]

    blk = es.enter_context(nc.Block())

    def emit_norm_chain(v_nrm_sc, eng="sc"):
        def _(g):
            g.wait_ge((sSC if eng == "sc" else sVE).h, v_nrm_sc)
            sGP.inc(g.tensor_scalar(nrm4c[:], nrm4[:], 1.0, None, ALU.mult))
        emit(blk.gpsimd, _, sGP)
        v1 = sGP.n

        def _(sc):
            sc.wait_ge(sGP.h, v1)
            sSC.inc(sc.activation(std4[:], nrm4c[:], AF.Ln,
                                  bias=eps30[:, 0:1]))
        emit(blk.scalar, _, sSC)
        v2 = sSC.n

        def _(g):
            g.wait_ge(sSC.h, v2)
            sGP.inc(g.tensor_scalar(den4[:], std4[:], 1.0, None, ALU.mult))
        emit(blk.gpsimd, _, sGP)
        v3 = sGP.n

        def _(sc):
            sc.wait_ge(sGP.h, v3)
            sSC.inc(sc.activation(rn4[:], den4[:], AF.Exp, scale=-0.5))
        emit(blk.scalar, _, sSC)
        v4 = sSC.n

        def _(v):
            v.wait_ge(sSC.h, v4)
            sVE.inc(v.tensor_scalar(den4c[:], rn4[:], EPS, 1.0,
                                    ALU.mult, ALU.add))
            v.wait_ge(sVE.h, sVE.n)
            sVE.inc(v.reciprocal(rec4[:], den4c[:]))
        emit(blk.vector, _, sVE)
        v5 = sVE.n

        def _(g):
            g.wait_ge(sVE.h, v5)
            for hh in range(4):
                sGP.inc(g.tensor_scalar(rn4c[:, hh:hh + 1], rec4[:, hh:hh + 1],
                                        rn4[:, hh:hh + 1], None, ALU.mult))
        emit(blk.gpsimd, _, sGP)
        return sGP.n

    def emit(dec, fn, ctr=None):
        if ctr is None:
            dec(fn)
            return

        def wrapped(eng):
            eng.wait_ge(ctr.h, ctr.n)
            fn(eng)
        dec(wrapped)

    pa = ExitStack()

    def asb(n, s_, d_):
        return pa.enter_context(nc.sbuf_tensor(n, s_, d_))

    def aps(n, s_, d_):
        return pa.enter_context(nc.psum_tensor(n, s_, d_))

    rwsb = asb("rwsb", [128, NEC * 8], F32)
    logits = asb("logits", [128, NOWN * 8], F32)
    xq = asb("xq", [128, 2 * E], F32)
    cst = asb("cst", [128, 2 * 256], F32)
    xnf = asb("xnf", [128, 2 * E], F32)
    sqscr = asb("sqscr", [128, E], F32)
    xnbt = asb("xnbt", [128, NEC * 128], BF16)
    xf32 = asb("xf32", [128, NEC * 128], F32)
    wk_sb = asb("wk_sb", [128, NEC * 512], BF16)
    wv_sb = asb("wv_sb", [128, NEC * 512], BF16)

    r_ = {n: asb("r_" + n, [128, NOWN], F32)
          for n in ("m1", "m1c", "m2", "m2c", "ss", "ssc", "rs", "rsc")}
    rb_ = {n: asb("rb_" + n, [128, NOWN * 8], F32)
           for n in ("eqt", "sh", "ew", "l2", "keep", "cmb0")}
    ltri_sb = asb("ltri_sb", [128, 128], F32)
    ec8s = asb("ec8s", [128, 8], F32)
    chksb = asb("chksb", [128, 128], F32)
    pf1 = asb("pf1", [128, NOWN * 8], F32)
    pf2 = asb("pf2", [128, NOWN * 8], F32)
    pf3 = asb("pf3", [128, NOWN * 8], F32)

    trA = aps("trA", [128, 512], F32)
    trB = aps("trB", [128, 512], F32)
    Kps = aps("Kps", [128, 512], F32)
    Vps = aps("Vps", [128, 512], F32)
    Lps = aps("Lps", [128, 8], F32)
    ktrA = aps("ktrA", [128, 512], BF16)
    ktrB = aps("ktrB", [128, 512], BF16)

    def _(s):
        for dst, src in ((idf, idf_in), (qsc, qs_in), (ksc, ks_in),
                         (kval, kval_in), (rwsb, rw_in),
                         (ones1, ones1_in), (epsln, epsln_in),
                         (eps30, eps30_in), (ltri_sb, ltri_in),
                         (ec8s, ec8_in), (chksb, chk_in)):
            sCN.inc(s.dma_start(out=dst[:], in_=src), 16)
    emit(blk.sync, _)
    V_CONST = sCN.n

    def _(g):
        g.load_library(MLP_LIB)
        sGW.inc(g.dma_start(out=idb[:], in_=idb_in), 16)
        sGW.inc(g.dma_start(out=onesc[:], in_=onesc_in), 16)
        sGW.inc(g.dma_start(out=wk_sb[:], in_=wk_in), 16)
        sGW.inc(g.dma_start(out=wv_sb[:], in_=wv_in), 16)
    emit(blk.gpsimd, _, sGP)
    V_WKV = sGW.n

    def _(s):
        for t in range(2):
            sXQ[t % 2].inc(s.dma_start(out=xq[:, (t % 2) * E:(t % 2 + 1) * E],
                                       in_=x_in[t * 128:(t + 1) * 128, :]), 16)
            sCS[t % 2].inc(s.dma_start(out=cst[:, (t % 2) * 256:(t % 2 + 1) * 256],
                                       in_=cs_in[t * 128:(t + 1) * 128, :]), 16)
    emit(blk.sync, _)

    tr_last = [0] * NT
    kv_done = [0] * NT
    vevac_done = [0] * NT
    rope_rd = [0] * NT
    spill_done = [0] * NT
    router_done = [0] * NT
    logev_done = [0] * NT
    ktr_done = [0] * NT
    pre_evac = [0, 0]
    pre_evac32 = [0, 0]

    for t in range(NT):
        own = OT0 <= t < OT0 + NOWN
        ot = t - OT0
        sl = t % 2
        xsl = xq[:, sl * E:(sl + 1) * E]
        xn_t = xnf[:, sl * E:(sl + 1) * E]
        cos_t = cst[:, sl * 256:sl * 256 + 128]
        sin_t = cst[:, sl * 256 + 128:sl * 256 + 256]

        def _(sc, xsl=xsl, t=t):
            if t == 0:
                sc.wait_ge(sCN.h, V_CONST)
            sc.wait_ge(sXQ[t % 2].h, 16 * (t // 2 + 1))
            sSC.inc(sc.activation(sqscr[:], xsl, AF.Square,
                                  accum_out=st["ssq"][:, 0:1]))
        emit(blk.scalar, _, sSC)
        v_ssq = sSC.n

        def _(v, xsl=xsl, t=t):
            v.wait_ge(sXQ[t % 2].h, 16 * (t // 2 + 1))
            sVE.inc(v.reduce_sum(st["sx"][:, 0:1], xsl, axis=AX.X))
        emit(blk.vector, _, sVE)
        v_sx = sVE.n

        def _(g):
            g.wait_ge(sVE.h, v_sx)
            sGP.inc(g.tensor_scalar(st["msq"][:, 0:1], st["sx"][:, 0:1],
                                    st["sx"][:, 0:1], 1.0 / (E * E),
                                    ALU.mult, ALU.mult))
        emit(blk.gpsimd, _, sGP)
        v_msq = sGP.n

        def _(v):
            v.wait_ge(sGP.h, v_msq)
            v.wait_ge(sSC.h, v_ssq)
            sVE.inc(v.scalar_tensor_tensor(st["var"][:, 0:1], st["ssq"][:, 0:1],
                                           1.0 / E, st["msq"][:, 0:1],
                                           ALU.mult, ALU.subtract))
        emit(blk.vector, _, sVE)
        v_var = sVE.n

        def _(sc):
            sc.wait_ge(sVE.h, v_var)
            sSC.inc(sc.activation(st["std"][:, 0:1], st["var"][:, 0:1],
                                  AF.Ln, bias=epsln[:, 0:1]))
        emit(blk.scalar, _, sSC)
        v_std = sSC.n

        def _(g):
            g.wait_ge(sSC.h, v_std)
            sGP.inc(g.tensor_scalar(st["nb"][:, 0:1], st["std"][:, 0:1],
                                    1.0, None, ALU.mult))
        emit(blk.gpsimd, _, sGP)
        v_lnc = sGP.n

        def _(sc):
            sc.wait_ge(sGP.h, v_lnc)
            sSC.inc(sc.activation(st["rstd"][:, 0:1], st["nb"][:, 0:1],
                                  AF.Exp, scale=-0.5))
        emit(blk.scalar, _, sSC)
        v_rstd = sSC.n

        def _(g, own=own, ot=ot):
            g.wait_ge(sSC.h, v_rstd)
            sGP.inc(g.tensor_scalar(st["nb"][:, 0:1], st["sx"][:, 0:1],
                                    st["rstd"][:, 0:1], -1.0 / E,
                                    ALU.mult, ALU.mult))
            sGP.inc(g.tensor_scalar(st["rstdc"][:, 0:1], st["rstd"][:, 0:1],
                                    1.0, None, ALU.mult))
            if own:
                sGP.inc(g.tensor_scalar(stash_opr[:, ot:ot + 1],
                                        st["rstd"][:, 0:1], 1.0, None, ALU.add))
                g.wait_ge(sGP.h, sGP.n)
                sGP.inc(g.tensor_scalar(stash_nb[:, ot:ot + 1],
                                        st["nb"][:, 0:1], 1.0, None, ALU.mult))
        emit(blk.gpsimd, _, sGP)
        v_nb = sGP.n

        def _(v, xsl=xsl, xn_t=xn_t, t=t, v_xr=sXR[t % 2].n):
            v.wait_ge(sGP.h, v_nb)
            if t >= 2:
                v.wait_ge(sPE.h, tr_last[t - 2])
            if v_xr:
                v.wait_ge(sXR[t % 2].h, v_xr)
            sVE.inc(v.tensor_scalar(xn_t, xsl, st["rstdc"][:, 0:1],
                                    st["nb"][:, 0:1], ALU.mult, ALU.add))
        emit(blk.vector, _, sVE)
        v_xn = sVE.n

        if own:
            def _(g, ot=ot, sl=sl):
                g.wait_ge(sVE.h, v_xn)
                sXR[sl].inc(g.dma_start(
                    out=xnrow[ot * 128:(ot + 1) * 128, :],
                    in_=xnf[:, sl * E:(sl + 1) * E]), 16)
            emit(blk.gpsimd, _, sGP)

        grp_evac = []
        grp_evac32 = []
        for grp in range(4):
            trp = trA if grp % 2 == 0 else trB

            def _(tn, grp=grp, trp=trp, xn_t=xn_t, t=t):
                if t == 0 and grp == 0:
                    tn.wait_ge(sCN.h, V_CONST)
                tn.wait_ge(sVE.h, v_xn)
                if grp >= 2:
                    tn.wait_ge(sVE.h, grp_evac[grp - 2])
                    if grp_evac32[grp - 2]:
                        tn.wait_ge(sVE.h, grp_evac32[grp - 2])
                else:
                    if pre_evac[grp]:
                        tn.wait_ge(sVE.h, pre_evac[grp])
                    if pre_evac32[grp]:
                        tn.wait_ge(sVE.h, pre_evac32[grp])
                for j in range(4):
                    ec = grp * 4 + j
                    sPE.inc(tn.transpose(trp[:, j * 128:(j + 1) * 128],
                                         xn_t[:, ec * 128:(ec + 1) * 128],
                                         idf[:]))
            emit(blk.tensor, _, sPE)
            v_tr = sPE.n
            if grp == 3:
                tr_last[t] = v_tr

            def _(v, grp=grp, trp=trp, t=t):
                v.wait_ge(sPE.h, v_tr)
                if t > 0 and grp == 0:
                    v.wait_ge(sPE.h, kv_done[t - 1])
                    if spill_done[t - 1]:
                        v.wait_ge(sAC.h, spill_done[t - 1])
                sVE.inc(v.tensor_copy(xnbt[:, grp * 512:(grp + 1) * 512],
                                      trp[:]))
            emit(blk.vector, _, sVE)
            grp_evac.append(sVE.n)

            if own:
                def _(v, grp=grp, trp=trp, t=t):
                    v.wait_ge(sPE.h, v_tr)
                    if grp == 0 and router_done[t - 1]:
                        v.wait_ge(sPE.h, router_done[t - 1])
                    sVE.inc(v.tensor_copy(
                        xf32[:, grp * 512:(grp + 1) * 512], trp[:]))
                emit(blk.vector, _, sVE)
                grp_evac32.append(sVE.n)
            else:
                grp_evac32.append(0)
        pre_evac = grp_evac[2:]
        pre_evac32 = grp_evac32[2:]

        def _(tn, t=t):
            tn.wait_ge(sVE.h, grp_evac[-1])
            if t == 0:
                tn.wait_ge(sGW.h, V_WKV)
            else:
                tn.wait_ge(sVE.h, rope_rd[t - 1])
                tn.wait_ge(sVE.h, vevac_done[t - 1])
            lastV = None
            for ec in range(NEC):
                lhs = xnbt[:, ec * 128:(ec + 1) * 128]
                tn.matmul(Kps[:], lhs, wk_sb[:, ec * 512:(ec + 1) * 512],
                          start=(ec == 0), stop=(ec == NEC - 1))
                lastV = tn.matmul(Vps[:], lhs,
                                  wv_sb[:, ec * 512:(ec + 1) * 512],
                                  start=(ec == 0), stop=(ec == NEC - 1))
            sPE.inc(lastV)
        emit(blk.tensor, _, sPE)
        kv_done[t] = sPE.n

        if own:
            def _(s, ot=ot):
                s.wait_ge(sVE.h, grp_evac[-1])
                sAC.inc(s.dma_start(
                    out=xnimg_r[:, :, ot * 128:(ot + 1) * 128],
                    in_=xnbt[:].rearrange("p (ec c) -> p ec c", c=128)), 16)
            emit(blk.sync, _)
            spill_done[t] = sAC.n

            def _(tn, t=t):
                tn.wait_ge(sVE.h, grp_evac32[-1])
                if logev_done[t - 1]:
                    tn.wait_ge(sVE.h, logev_done[t - 1])
                last = None
                for ec in range(NEC):
                    last = tn.matmul(Lps[:], xf32[:, ec * 128:(ec + 1) * 128],
                                     rwsb[:, ec * 8:(ec + 1) * 8],
                                     start=(ec == 0), stop=(ec == NEC - 1))
                sPE.inc(last)
            emit(blk.tensor, _, sPE)
            router_done[t] = sPE.n
            v_rt = sPE.n

            def _(v, ot=ot):
                v.wait_ge(sPE.h, v_rt)
                sVE.inc(v.tensor_copy(logits[:, ot * 8:(ot + 1) * 8], Lps[:]))
            emit(blk.vector, _, sVE)
            logev_done[t] = sVE.n
        else:
            spill_done[t] = spill_done[t - 1] if t else 0
            router_done[t] = router_done[t - 1] if t else 0
            logev_done[t] = logev_done[t - 1] if t else 0

        kp3 = g3(Kps[:], 128)
        rro3 = g3(rro[:], 128)

        def _(v, t=t, cos_t=cos_t, sin_t=sin_t, kp3=kp3):
            v.wait_ge(sPE.h, kv_done[t])
            v.wait_ge(sCS[t % 2].h, 16 * (t // 2 + 1))
            sVE.inc(v.tensor_scalar(rro3[:, :, 0:64], kp3[:, :, 64:128],
                                    -1.0, None, ALU.mult))
            sVE.inc(v.tensor_scalar(rro3[:, :, 64:128], kp3[:, :, 0:64],
                                    1.0, None, ALU.mult))
            cos_b = cos_t.rearrange("p (o d) -> p o d", o=1).broadcast_to([128, 4, 128])
            sin_b = sin_t.rearrange("p (o d) -> p o d", o=1).broadcast_to([128, 4, 128])
            sVE.inc(v.tensor_tensor(g3(rt1[:], 128), kp3, cos_b, ALU.mult))
            v.wait_ge(sVE.h, sVE.n)
            sVE.inc(v.tensor_tensor(g3(rt2[:], 128), rro3, sin_b, ALU.mult))
            v.wait_ge(sVE.h, sVE.n)
            sVE.inc(v.tensor_add(rfo[:], rt1[:], rt2[:]))
        emit(blk.vector, _, sVE)
        rope_rd[t] = sVE.n
        v_rfo = sVE.n

        if t + 2 < NT:
            t2 = t + 2

            def _(s, t2=t2):
                s.wait_ge(sVE.h, v_rfo)
                s.wait_ge(sXQ[t2 % 2].h, sXQ[t2 % 2].n)
                s.wait_ge(sCS[t2 % 2].h, sCS[t2 % 2].n)
                sXQ[t2 % 2].inc(s.dma_start(
                    out=xq[:, (t2 % 2) * E:(t2 % 2 + 1) * E],
                    in_=x_in[t2 * 128:(t2 + 1) * 128, :]), 16)
                sCS[t2 % 2].inc(s.dma_start(
                    out=cst[:, (t2 % 2) * 256:(t2 % 2 + 1) * 256],
                    in_=cs_in[t2 * 128:(t2 + 1) * 128, :]), 16)
            emit(blk.sync, _)

        def _(sc, t=t):
            sc.wait_ge(sVE.h, v_rfo)
            sSC.inc(sc.activation(sq512[:], rfo[:], AF.Square))
        emit(blk.scalar, _, sSC)
        v_sq = sSC.n

        def _(v, t=t):
            v.wait_ge(sSC.h, v_sq)
            sVE.inc(v.reduce_sum(g3(nrm4[:], 1), g3(sq512[:], 128),
                                 axis=AX.X))
            sVE.inc(v.tensor_copy(Vsb[:, t * 512:(t + 1) * 512], Vps[:]))
        emit(blk.vector, _, sVE)
        vevac_done[t] = sVE.n
        v_nrm = sVE.n

        v_r4c = emit_norm_chain(v_nrm, eng="ve")

        def _(v, t=t):
            v.wait_ge(sGP.h, v_r4c)
            if t == 0:
                v.wait_ge(sCN.h, V_CONST)
            else:
                v.wait_ge(sVE.h, ktr_done[t - 1])
            for hh in range(4):
                sVE.inc(v.scalar_tensor_tensor(
                    rfin[:, hh * 128:(hh + 1) * 128],
                    rfo[:, hh * 128:(hh + 1) * 128],
                    rn4c[:, hh:hh + 1], ksc[:], ALU.mult, ALU.mult))
        emit(blk.vector, _, sVE)
        v_rfin = sVE.n

        ktrp = ktrA if t % 2 == 0 else ktrB

        def _(tn, ktrp=ktrp, t=t):
            tn.wait_ge(sVE.h, v_rfin)
            if t >= 2:
                tn.wait_ge(sVE.h, ktr_done[t - 2])
            for hh in range(4):
                sPE.inc(tn.transpose(ktrp[:, hh * 128:(hh + 1) * 128],
                                     rfin[:, hh * 128:(hh + 1) * 128],
                                     idb[:]))
        emit(blk.tensor, _, sPE)
        v_ktr = sPE.n

        def _(v, ktrp=ktrp, t=t):
            v.wait_ge(sPE.h, v_ktr)
            for hh in range(4):
                sVE.inc(v.tensor_copy(
                    KT[:, hh * SLC + t * 128:hh * SLC + (t + 1) * 128],
                    ktrp[:, hh * 128:(hh + 1) * 128]))
        emit(blk.vector, _, sVE)
        ktr_done[t] = sVE.n

    v_logits = logev_done[OT0 + NOWN - 1]
    lg3 = g3(logits[:], 8)

    def _(v):
        v.wait_ge(sVE.h, v_logits)
        sVE.inc(v.reduce_max(g3(r_["m1"][:], 1), lg3, axis=AX.X))
    emit(blk.vector, _, sVE)
    rv1 = sVE.n

    def _(g):
        g.wait_ge(sVE.h, rv1)
        sGP.inc(g.tensor_scalar(r_["m1c"][:], r_["m1"][:], 1.0, None, ALU.mult))
    emit(blk.gpsimd, _, sGP)
    rg1 = sGP.n

    def _(v):
        v.wait_ge(sGP.h, rg1)
        m1b = g3(r_["m1c"][:], 1).broadcast_to([128, NOWN, 8])
        sVE.inc(v.tensor_tensor(g3(rb_["eqt"][:], 8), lg3, m1b, ALU.is_equal))
        sVE.inc(v.tensor_tensor(g3(rb_["sh"][:], 8), lg3, m1b, ALU.subtract))
    emit(blk.vector, _, sVE)
    rv2 = sVE.n

    def _(sc):
        sc.wait_ge(sVE.h, rv2)
        sSC.inc(sc.activation(rb_["ew"][:], rb_["sh"][:], AF.Exp))
    emit(blk.scalar, _, sSC)
    rs1 = sSC.n

    def _(v):
        sVE.inc(v.scalar_tensor_tensor(rb_["l2"][:], rb_["eqt"][:], -1e30,
                                       logits[:], ALU.mult, ALU.add))
        v.wait_ge(sVE.h, sVE.n)
        sVE.inc(v.reduce_max(g3(r_["m2"][:], 1), g3(rb_["l2"][:], 8),
                             axis=AX.X))
        v.wait_ge(sSC.h, rs1)
        sVE.inc(v.reduce_sum(g3(r_["ss"][:], 1), g3(rb_["ew"][:], 8),
                             axis=AX.X))
    emit(blk.vector, _, sVE)
    rv3 = sVE.n

    def _(g):
        g.wait_ge(sVE.h, rv3)
        sGP.inc(g.tensor_scalar(r_["m2c"][:], r_["m2"][:], 1.0, None, ALU.mult))
        sGP.inc(g.tensor_scalar(r_["ssc"][:], r_["ss"][:], 1.0, None, ALU.mult))
    emit(blk.gpsimd, _, sGP)
    rg2 = sGP.n

    def _(v):
        v.wait_ge(sGP.h, rg2)
        m2b = g3(r_["m2c"][:], 1).broadcast_to([128, NOWN, 8])
        sVE.inc(v.tensor_tensor(g3(rb_["keep"][:], 8), lg3, m2b, ALU.is_ge))
        sVE.inc(v.reciprocal(r_["rs"][:], r_["ssc"][:]))
        v.wait_ge(sVE.h, sVE.n)
        sVE.inc(v.tensor_mul(rb_["cmb0"][:], rb_["keep"][:], rb_["ew"][:]))
    emit(blk.vector, _, sVE)
    rv4 = sVE.n

    def _(g):
        g.wait_ge(sVE.h, rv4)
        sGP.inc(g.tensor_scalar(r_["rsc"][:], r_["rs"][:], 1.0, None, ALU.mult))
    emit(blk.gpsimd, _, sGP)
    rg3 = sGP.n

    def _(v):
        v.wait_ge(sGP.h, rg3)
        rsb = g3(r_["rsc"][:], 1).broadcast_to([128, NOWN, 8])
        sVE.inc(v.tensor_tensor(g3(comb[:], 8), g3(rb_["cmb0"][:], 8), rsb,
                                ALU.mult))
    emit(blk.vector, _, sVE)
    rvF = sVE.n

    def _(v):
        v.wait_ge(sVE.h, rvF)
        sVE.inc(v.tensor_copy(pf1[:], rb_["keep"][:]))
        src, dst = pf1, pf2
        for k in (1, 2, 4, 8):
            v.wait_ge(sVE.h, sVE.n)
            sVE.inc(v.tensor_copy(g3(dst[:], 8)[:, 0:k, :],
                                  g3(src[:], 8)[:, 0:k, :]))
            sVE.inc(v.tensor_tensor(g3(dst[:], 8)[:, k:NOWN, :],
                                    g3(src[:], 8)[:, k:NOWN, :],
                                    g3(src[:], 8)[:, 0:NOWN - k, :], ALU.add))
            src, dst = dst, src
        v.wait_ge(sVE.h, sVE.n)
        sVE.inc(v.tensor_tensor(pf2[:], pf1[:], rb_["keep"][:], ALU.subtract))
    emit(blk.vector, _, sVE)
    rvP = sVE.n

    def _(tn):
        tn.wait_ge(sVE.h, rvP)
        tn.wait_ge(sCN.h, V_CONST)
        sPE.inc(tn.matmul(Lps[:], ltri_sb[:],
                          pf1[:, (NOWN - 1) * 8:NOWN * 8],
                          start=True, stop=True))
    emit(blk.tensor, _, sPE)
    v_base_mm = sPE.n

    def _(v):
        v.wait_ge(sPE.h, v_base_mm)
        sVE.inc(v.tensor_copy(r_["m1"][:, 0:8], Lps[:]))
    emit(blk.vector, _, sVE)
    v_basev = sVE.n

    def _(g):
        g.wait_ge(sVE.h, v_basev)
        sGP.inc(g.tensor_scalar(r_["m1c"][:, 0:8], r_["m1"][:, 0:8],
                                1.0, None, ALU.mult))
    emit(blk.gpsimd, _, sGP)
    v_baseg = sGP.n

    def _(v):
        v.wait_ge(sGP.h, v_baseg)
        bb = r_["m1c"][:, 0:8].rearrange(
            "p (o e) -> p o e", o=1).broadcast_to([128, NOWN, 8])
        sVE.inc(v.tensor_tensor(g3(pf3[:], 8), g3(pf2[:], 8), bb, ALU.add))
        v.wait_ge(sVE.h, sVE.n)
        sVE.inc(v.tensor_scalar(pf2[:], pf3[:], float(CAPE), None, ALU.is_ge))
        v.wait_ge(sVE.h, sVE.n)
        sVE.inc(v.tensor_scalar(pf2[:], pf2[:], -1.0, 1.0, ALU.mult, ALU.add))
        v.wait_ge(sVE.h, sVE.n)
        sVE.inc(v.tensor_mul(pf2[:], pf2[:], rb_["keep"][:]))
        eb = ec8s[:].rearrange(
            "p (o e) -> p o e", o=1).broadcast_to([128, NOWN, 8])
        v.wait_ge(sVE.h, sVE.n)
        sVE.inc(v.tensor_tensor(g3(pf3[:], 8), g3(pf3[:], 8), eb, ALU.add))
        v.wait_ge(sVE.h, sVE.n)
        sVE.inc(v.tensor_tensor(pf3[:], pf3[:], chksb[:], ALU.subtract))
        v.wait_ge(sVE.h, sVE.n)
        sVE.inc(v.tensor_mul(pf3[:], pf3[:], pf2[:]))
        v.wait_ge(sVE.h, sVE.n)
        sVE.inc(v.tensor_tensor(pf3[:], pf3[:], chksb[:], ALU.add))
        v.wait_ge(sVE.h, sVE.n)
        sVE.inc(v.tensor_copy(vmi16[:], pf3[:]))
    emit(blk.vector, _, sVE)
    V_SEL = sVE.n

    if DEBUG:
        def _(s):
            for c in (sPE, sVE, sSC, sGP):
                s.wait_ge(c.h, c.n)
            s.wait_ge(sAC.h, sAC.n)
            sSO.inc(s.dma_start(out=dbg["dbg_KT"], in_=KT[:]), 16)
            sSO.inc(s.dma_start(out=dbg["dbg_V"], in_=Vsb[:]), 16)
            sSO.inc(s.dma_start(out=dbg["dbg_logits"], in_=logits[:]), 16)
            sSO.inc(s.dma_start(out=dbg["dbg_comb"], in_=comb[:]), 16)
            sSO.inc(s.dma_start(out=dbg["dbg_xnimg"], in_=xnimg), 16)
        emit(blk.sync, _)

    A_PE, A_VE, A_SC, A_GP = sPE.n, sVE.n, sSC.n, sGP.n
    A_LD, A_GW, A_AC = sLD.n, sGW.n, sAC.n
    pa.close()

    pb = ExitStack()

    def bsb(n, s_, d_):
        return pb.enter_context(nc.sbuf_tensor(n, s_, d_))

    def bps(n, s_, d_):
        return pb.enter_context(nc.psum_tensor(n, s_, d_))

    bmask = bsb("bmask_sb", [128, 2 * 128], BF16)
    ttb = bsb("ttb", [128, 2 * 1024], BF16)
    xnbtq = bsb("xnbtq", [128, 4 * NEC * 128], BF16)
    wqo = bsb("wqo", [128, 2 * NEC * 512], BF16)
    QT = bsb("QT", [128, H * 512], BF16)
    OTb = bsb("OTb", [128, H * 512], BF16)
    ab = bsb("ab", [128, 2 * 1024], BF16)
    rqf = bsb("rqf", [128, 512], F32)
    asum = bsb("asum", [128, 512], F32)
    tpr = bsb("tpr", [128, 512], BF16)
    asb_bf = bsb("asb_bf", [128, 512], BF16)
    rcp1 = bsb("rcp1", [1, 512], F32)
    bcs = bsb("bcs", [128, 512], F32)
    xo = bsb("xo", [128, 2 * 2048], F32)
    outs = bsb("outs", [128, 4 * 512], F32)
    t512 = bsb("t512", [128, 512], F32)
    cstq = bsb("cstq", [128, 4 * 256], F32)

    qp = [bps(f"qp{i}", [128, 512], F32) for i in range(4)]
    s2 = [bps(f"s2_{i}", [128, 1024], F32) for i in range(2)]

    barred = set()
    s2init = [0]

    def barA(eng, key):
        if key not in barred:
            eng.wait_ge(sPE.h, A_PE)
            eng.wait_ge(sVE.h, A_VE)
            eng.wait_ge(sSC.h, A_SC)
            eng.wait_ge(sGP.h, A_GP)
            eng.wait_ge(sGW.h, A_GW)
            eng.wait_ge(sAC.h, A_AC)
            for cc in (sXQ[0], sXQ[1], sCS[0], sCS[1]):
                eng.wait_ge(cc.h, cc.n)
            barred.add(key)

    wqo_ld = {}
    wslot_free = [0, 0]
    qproj_done = {}
    rope_q_rd = {}
    qtr_evac = {}
    tanh_done = {}
    avden_done = {}
    ot_evac = {}
    rcp_done = {}
    dmm_done = {}
    woproj_done = {}
    asm_done = {}
    asm_dma = {}

    for qb in range(4):
        if qb == 0:
            def _(v):
                barA(v, "vec_init")
                sVE.inc(v.memset(s2[0][:], 0.0))
                sVE.inc(v.memset(s2[1][:], 0.0))
            emit(blk.vector, _, sVE)
            s2init[0] = sVE.n

        def _(s, qb=qb):
            barA(s, "sync")
            if qb == 0:
                sBM.inc(s.dma_start(out=bmask[:], in_=bm_in), 16)
            if qb > 0:
                s.wait_ge(sPE.h, qproj_done[(qb - 1, 3)])
                s.wait_ge(sVE.h, rope_q_rd[(qb - 1, 3)])
                s.wait_ge(sQL.h, sQL.n)
            for qt in range(4):
                ot = qb * 4 + qt
                sQL.inc(s.dma_start(
                    out=xnbtq[:, qt * E:(qt + 1) * E].rearrange(
                        "p (ec c) -> p ec c", c=128),
                    in_=xnimg_r[:, :, ot * 128:(ot + 1) * 128]), 16)
                sQL.inc(s.dma_start(
                    out=cstq[:, qt * 256:(qt + 1) * 256],
                    in_=cs_in[HALO + ot * 128:HALO + (ot + 1) * 128, :]), 16)
        emit(blk.sync, _)
        v_qld = sQL.n

        for hc in range(4):
            slot = hc % 2

            def _(g, qb=qb, hc=hc, slot=slot):
                barA(g, "gp")
                if wslot_free[slot]:
                    g.wait_ge(sPE.h, wslot_free[slot])
                g.wait_ge(sWQ[slot].h, sWQ[slot].n)
                sWQ[slot].inc(g.dma_start(
                    out=wqo[:, slot * (NEC * 512):(slot + 1) * (NEC * 512)],
                    in_=wq_in[:, hc * (NEC * 512):(hc + 1) * (NEC * 512)]), 16)
                wqo_ld[(qb, "wq", hc)] = sWQ[slot].n
            emit(blk.gpsimd, _, sGP)

            def _(tn, qb=qb, hc=hc, slot=slot):
                barA(tn, "pe")
                tn.wait_ge(sWQ[slot].h, wqo_ld[(qb, "wq", hc)])
                tn.wait_ge(sQL.h, v_qld)
                if hc >= 1:
                    tn.wait_ge(sVE.h, rope_q_rd[(qb, hc - 1)])
                elif qb > 0:
                    tn.wait_ge(sVE.h, asm_done[(qb - 1, 3)])
                last = None
                for qt in range(4):
                    for ec in range(NEC):
                        last = tn.matmul(
                            qp[qt][:],
                            xnbtq[:, qt * E + ec * 128:qt * E + (ec + 1) * 128],
                            wqo[:, slot * (NEC * 512) + ec * 512:
                                slot * (NEC * 512) + (ec + 1) * 512],
                            start=(ec == 0), stop=(ec == NEC - 1))
                sPE.inc(last)
            emit(blk.tensor, _, sPE)
            qproj_done[(qb, hc)] = sPE.n
            wslot_free[slot] = sPE.n

            for qt in range(4):
                qps3 = g3(qp[qt][:], 128)
                cos_q = cstq[:, qt * 256:qt * 256 + 128]
                sin_q = cstq[:, qt * 256 + 128:qt * 256 + 256]

                def _(v, qb=qb, hc=hc, qps3=qps3, cos_q=cos_q, sin_q=sin_q):
                    v.wait_ge(sPE.h, qproj_done[(qb, hc)])
                    sVE.inc(v.tensor_scalar(rro3[:, :, 0:64],
                                            qps3[:, :, 64:128],
                                            -1.0, None, ALU.mult))
                    sVE.inc(v.tensor_scalar(rro3[:, :, 64:128],
                                            qps3[:, :, 0:64],
                                            1.0, None, ALU.mult))
                    cos_b = cos_q.rearrange("p (o d) -> p o d", o=1).broadcast_to([128, 4, 128])
                    sin_b = sin_q.rearrange("p (o d) -> p o d", o=1).broadcast_to([128, 4, 128])
                    sVE.inc(v.tensor_tensor(g3(rt1[:], 128), qps3, cos_b,
                                            ALU.mult))
                    v.wait_ge(sVE.h, sVE.n)
                    sVE.inc(v.tensor_tensor(g3(rt2[:], 128), rro3, sin_b,
                                            ALU.mult))
                    v.wait_ge(sVE.h, sVE.n)
                    sVE.inc(v.tensor_add(rfo[:], rt1[:], rt2[:]))
                emit(blk.vector, _, sVE)
                v_qrfo = sVE.n
                if qt == 3:
                    rope_q_rd[(qb, hc)] = sVE.n

                def _(sc):
                    sc.wait_ge(sVE.h, v_qrfo)
                    for hh in range(4):
                        sSC.inc(sc.activation(
                            sq512[:, hh * 128:(hh + 1) * 128],
                            rfo[:, hh * 128:(hh + 1) * 128],
                            AF.Square, accum_out=nrm4[:, hh:hh + 1]))
                emit(blk.scalar, _, sSC)
                v_qn = sSC.n

                v_qr4c = emit_norm_chain(v_qn)

                def _(v):
                    v.wait_ge(sGP.h, v_qr4c)
                    for hh in range(4):
                        sVE.inc(v.scalar_tensor_tensor(
                            rqf[:, hh * 128:(hh + 1) * 128],
                            rfo[:, hh * 128:(hh + 1) * 128],
                            rn4c[:, hh:hh + 1], qsc[:], ALU.mult, ALU.mult))
                emit(blk.vector, _, sVE)
                v_qfin = sVE.n

                tsl = (hc * 4 + qt) % 2

                def _(tn, qb=qb, hc=hc, qt=qt, tsl=tsl):
                    tn.wait_ge(sVE.h, v_qfin)
                    if qb == 0 and hc == 0 and qt == 0:
                        tn.wait_ge(sVE.h, s2init[0])
                    n = hc * 4 + qt
                    if n >= 2:
                        tn.wait_ge(sSC.h, qtr_evac[(qb, n - 2)])
                    elif qb > 0:
                        if n == 0:
                            tn.wait_ge(sSC.h, tanh_done[(qb - 1, H - 1, 8)])
                        else:
                            tn.wait_ge(sVE.h, ot_evac[(qb - 1, H - 1)])
                    for hh in range(4):
                        sPE.inc(tn.transpose(
                            s2[tsl][:, hh * 128:(hh + 1) * 128],
                            rqf[:, hh * 128:(hh + 1) * 128], idf[:]))
                emit(blk.tensor, _, sPE)
                v_qtr = sPE.n

                def _(sc, qb=qb, hc=hc, qt=qt, tsl=tsl):
                    sc.wait_ge(sPE.h, v_qtr)
                    for hh in range(4):
                        h = hc * 4 + hh
                        sSC.inc(sc.copy(
                            QT[:, h * 512 + qt * 128:h * 512 + (qt + 1) * 128],
                            s2[tsl][:, hh * 128:(hh + 1) * 128]))
                emit(blk.scalar, _, sSC)
                qtr_evac[(qb, hc * 4 + qt)] = sSC.n

        def _load_wo(ec, qb=qb):
            slot = ec % 2

            def _(g, qb=qb, ec=ec, slot=slot):
                if wslot_free[slot]:
                    g.wait_ge(sPE.h, wslot_free[slot])
                g.wait_ge(sWQ[slot].h, sWQ[slot].n)
                sWQ[slot].inc(g.dma_start(
                    out=wqo[:, slot * (NEC * 512):(slot + 1) * (NEC * 512)],
                    in_=wo_in[:, ec * (H * 512):(ec + 1) * (H * 512)]), 16)
                wqo_ld[(qb, "wo", ec)] = sWQ[slot].n
            emit(blk.gpsimd, _, sGP)

        for ec in range(2):
            _load_wo(ec)

        NPAIR = NKT // 2
        for h in range(H):
            kv = h // 4
            otp = qp[0] if h % 2 == 0 else qp[2]
            dnp = qp[2] if h % 2 == 0 else qp[0]
            v_sc = {}

            def emit_scores(pi, h=h, kv=kv, qb=qb):
                psl = pi % 2
                kt0 = qb * 4 + 2 * pi

                def _(tn, pi=pi, psl=psl, kt0=kt0, h=h, kv=kv, qb=qb):
                    if h == 0 and pi == 0:
                        tn.wait_ge(sSC.h, qtr_evac[(qb, 15)])
                    if pi >= 2:
                        tn.wait_ge(sSC.h, tanh_done[(qb, h, pi - 2)])
                    elif pi == 0:
                        if h > 0:
                            tn.wait_ge(sSC.h, tanh_done[(qb, h - 1, 8)])
                        elif qb > 0:
                            tn.wait_ge(sSC.h, tanh_done[(qb - 1, H - 1, 8)])
                    else:
                        if h > 0:
                            tn.wait_ge(sVE.h, ot_evac[(qb, h - 1)])
                        elif qb > 0:
                            tn.wait_ge(sVE.h, ot_evac[(qb - 1, H - 1)])
                    last = None
                    for half in range(2):
                        ql, qh = qrange(2 * pi + half)
                        last = tn.matmul(
                            s2[psl][:, half * 512 + ql:half * 512 + qh],
                            KT[:, kv * SLC + (kt0 + half) * 128:
                               kv * SLC + (kt0 + half + 1) * 128],
                            QT[:, h * 512 + ql:h * 512 + qh],
                            start=True, stop=True)
                    sPE.inc(last)
                emit(blk.tensor, _, sPE)
                v_sc[pi] = sPE.n

            emit_scores(0)
            for pi in range(NPAIR):
                psl = pi % 2
                kt0 = qb * 4 + 2 * pi

                def _(sc, psl=psl, pi=pi, h=h, qb=qb):
                    sc.wait_ge(sPE.h, v_sc[pi])
                    sSC.inc(sc.activation(ttb[:, psl * 1024:(psl + 1) * 1024],
                                          s2[psl][:], AF.Tanh,
                                          scale=TANH_SC))
                    tanh_done[(qb, h, pi)] = sSC.n
                    sc.wait_ge(sSC.h, sSC.n)
                    if pi >= 2:
                        sc.wait_ge(sPE.h, avden_done[(qb, h, pi - 2)])
                    elif h > 0:
                        sc.wait_ge(sPE.h,
                                   avden_done[(qb, h - 1, NPAIR - 2 + pi)])
                    elif qb > 0:
                        sc.wait_ge(sPE.h,
                                   avden_done[(qb - 1, H - 1, NPAIR - 2 + pi)])
                    sSC.inc(sc.activation(ab[:, psl * 1024:(psl + 1) * 1024],
                                          ttb[:, psl * 1024:(psl + 1) * 1024],
                                          AF.Exp, scale=CAP))
                emit(blk.scalar, _, sSC)
                v_exp = sSC.n

                v_msk = 0
                mask_ops = []
                for half in range(2):
                    kt = kt0 + half
                    dr = 2 * pi + half - 8
                    pad = kt < 8 or kt >= 24
                    base = psl * 1024 + half * 512
                    run0 = None
                    for j in range(4):
                        diff = dr - j
                        if abs(diff) <= 7:
                            st_ = "full"
                        elif diff == -8:
                            st_ = "ge"
                        elif diff == 8:
                            st_ = "le"
                        else:
                            st_ = "zero"
                        if st_ == "zero":
                            if run0 is None:
                                run0 = j
                        else:
                            if run0 is not None:
                                mask_ops.append((base + run0 * 128,
                                                 (j - run0) * 128, 0))
                                run0 = None
                            if st_ == "ge":
                                mask_ops.append((base + j * 128, 128, 1))
                            elif st_ == "le":
                                mask_ops.append((base + j * 128, 128, 2))
                    if run0 is not None:
                        mask_ops.append((base + run0 * 128,
                                         (4 - run0) * 128, 0))
                    if pad:
                        mask_ops.append((base, 512, 3, kt))
                def _(v, mask_ops=mask_ops, psl=psl, pi=pi):
                    if mask_ops:
                        v.wait_ge(sBM.h, 16)
                    v.wait_ge(sSC.h, v_exp)
                    for op in mask_ops:
                        if op[2] == 0:
                            sVE.inc(v.memset(ab[:, op[0]:op[0] + op[1]],
                                             0.0))
                        elif op[2] in (1, 2):
                            tri = bmask[:, (op[2] - 1) * 128:op[2] * 128]
                            sVE.inc(v.tensor_tensor(
                                ab[:, op[0]:op[0] + op[1]],
                                ab[:, op[0]:op[0] + op[1]],
                                tri, ALU.mult))
                        else:
                            sVE.inc(v.tensor_scalar(
                                ab[:, op[0]:op[0] + op[1]],
                                ab[:, op[0]:op[0] + op[1]],
                                kval[:, op[3]:op[3] + 1],
                                None, ALU.mult))
                        v.wait_ge(sVE.h, sVE.n)
                    sVE.inc(v.tensor_tensor(tpr[:],
                                            ab[:, psl * 1024:psl * 1024 + 512],
                                            ab[:, psl * 1024 + 512:
                                               psl * 1024 + 1024],
                                            ALU.add))
                    v.wait_ge(sVE.h, sVE.n)
                    if pi == 0:
                        sVE.inc(v.tensor_copy(asum[:], tpr[:]))
                    else:
                        sVE.inc(v.tensor_tensor(asum[:], asum[:], tpr[:],
                                                ALU.add))
                emit(blk.vector, _, sVE)
                v_msk = sVE.n

                if pi + 1 < NPAIR:
                    emit_scores(pi + 1)

                def _(tn, psl=psl, pi=pi, h=h, kt0=kt0, kv=kv, otp=otp,
                      v_msk=v_msk, qb=qb):
                    tn.wait_ge(sVE.h, v_msk)
                    if pi == 0:
                        if h >= 2:
                            tn.wait_ge(sVE.h, ot_evac[(qb, h - 2)])
                        elif qb > 0:
                            tn.wait_ge(sVE.h, ot_evac[(qb - 1, H - 2 + h)])
                        if h >= 1:
                            tn.wait_ge(sVE.h, rcp_done[(qb, h - 1)])
                        elif qb > 0:
                            tn.wait_ge(sVE.h, rcp_done[(qb - 1, H - 1)])
                    last = None
                    for half in range(2):
                        kt = kt0 + half
                        ql, qh = qrange(2 * pi + half)
                        if pi == 0 and half == 0:
                            ql, qh = 0, 512
                        a_ap = ab[:, psl * 1024 + half * 512 + ql:
                                  psl * 1024 + half * 512 + qh]
                        last = tn.matmul(
                            otp[:, ql:qh],
                            Vsb[:, kt * 512 + kv * 128:
                                kt * 512 + (kv + 1) * 128],
                            a_ap, start=(pi == 0 and half == 0),
                            stop=(pi == NPAIR - 1 and half == 1))
                    sPE.inc(last)
                emit(blk.tensor, _, sPE)
                avden_done[(qb, h, pi)] = sPE.n

            def _(v, h=h, qb=qb):
                if dmm_done.get((qb, h - 1)):
                    v.wait_ge(sPE.h, dmm_done[(qb, h - 1)])
                elif qb > 0 and h == 0:
                    v.wait_ge(sPE.h, dmm_done[(qb - 1, H - 1)])
                sVE.inc(v.tensor_copy(asb_bf[:], asum[:]))
            emit(blk.vector, _, sVE)
            v_acast = sVE.n

            def _(tn, h=h, dnp=dnp, qb=qb):
                tn.wait_ge(sVE.h, v_acast)
                if h >= 1:
                    tn.wait_ge(sVE.h, ot_evac[(qb, h - 1)])
                elif qb > 0:
                    tn.wait_ge(sVE.h, ot_evac[(qb - 1, H - 1)])
                sPE.inc(tn.matmul(dnp[0:1, :], onesc[:], asb_bf[:],
                                  start=True, stop=True))
                dmm_done[(qb, h)] = sPE.n
            emit(blk.tensor, _, sPE)
            v_dmm = sPE.n

            def _(v, h=h, dnp=dnp, qb=qb):
                v.wait_ge(sPE.h, v_dmm)
                sVE.inc(v.reciprocal(rcp1[:], dnp[0:1, :]))
                rcp_done[(qb, h)] = sVE.n
            emit(blk.vector, _, sVE)
            v_rcp = sVE.n

            def _(tn, qb=qb, h=h):
                tn.wait_ge(sVE.h, v_rcp)
                tn.wait_ge(sSC.h, tanh_done[(qb, h, NPAIR - 1)])
                sPE.inc(tn.matmul(s2[1][:, 0:512], ones1[:], rcp1[:],
                                  start=True, stop=True))
            emit(blk.tensor, _, sPE)
            v_bc = sPE.n

            def _(v, h=h, otp=otp, qb=qb):
                v.wait_ge(sPE.h, v_bc)
                if h == 0 and qb > 0:
                    v.wait_ge(sPE.h, woproj_done[(qb - 1, 3)])
                sVE.inc(v.tensor_copy(bcs[:], s2[1][:, 0:512]))
                v.wait_ge(sVE.h, sVE.n)
                sVE.inc(v.tensor_tensor(OTb[:, h * 512:(h + 1) * 512],
                                        otp[:], bcs[:], ALU.mult))
            emit(blk.vector, _, sVE)
            ot_evac[(qb, h)] = sVE.n

        x_tpe = x_in.rearrange("(tt p) e -> p tt e", p=128)
        xo_ld = {}
        for ec in range(4):
            slot = ec % 2

            def _(s, qb=qb, ec=ec, slot=slot):
                if ec >= 2:
                    s.wait_ge(sVE.h, asm_done[(qb, ec - 2)])
                elif qb > 0:
                    s.wait_ge(sVE.h, asm_done[(qb - 1, 2 + ec)])
                s.wait_ge(sXO[slot].h, sXO[slot].n)
                sXO[slot].inc(s.dma_start(
                    out=xo[:, slot * 2048:(slot + 1) * 2048].rearrange(
                        "p (q c) -> p q c", c=512),
                    in_=x_tpe[:, OT0 + qb * 4:OT0 + qb * 4 + 4,
                              ec * 512:(ec + 1) * 512]), 16)
                xo_ld[ec] = sXO[slot].n
            emit(blk.sync, _)

            def _(tn, qb=qb, ec=ec, slot=slot):
                tn.wait_ge(sWQ[slot].h, wqo_ld[(qb, "wo", ec)])
                tn.wait_ge(sVE.h, ot_evac[(qb, H - 1)])
                if ec >= 1:
                    tn.wait_ge(sVE.h, asm_done[(qb, ec - 1)])
                last = None
                for qt in range(4):
                    for hcc in range(H):
                        last = tn.matmul(
                            qp[qt][:],
                            OTb[:, hcc * 512 + qt * 128:hcc * 512 + (qt + 1) * 128],
                            wqo[:, slot * (NEC * 512) + hcc * 512:
                                slot * (NEC * 512) + (hcc + 1) * 512],
                            start=(hcc == 0), stop=(hcc == H - 1))
                sPE.inc(last)
            emit(blk.tensor, _, sPE)
            woproj_done[(qb, ec)] = sPE.n
            wslot_free[slot] = sPE.n
            if ec + 2 <= 3:
                _load_wo(ec + 2)

            def _(v, qb=qb, ec=ec, slot=slot):
                v.wait_ge(sPE.h, woproj_done[(qb, ec)])
                v.wait_ge(sXO[ec % 2].h, xo_ld[ec])
                if asm_dma.get((qb, ec - 1)):
                    v.wait_ge(sSO.h, asm_dma[(qb, ec - 1)])
                elif qb > 0 and ec == 0:
                    v.wait_ge(sSO.h, asm_dma[(qb - 1, 3)])
                for qt in range(4):
                    ot = qb * 4 + qt
                    sVE.inc(v.tensor_scalar(t512[:],
                                            xo[:, slot * 2048 + qt * 512:
                                                slot * 2048 + (qt + 1) * 512],
                                            stash_opr[:, ot:ot + 1],
                                            stash_nb[:, ot:ot + 1],
                                            ALU.mult, ALU.add))
                    v.wait_ge(sVE.h, sVE.n)
                    sVE.inc(v.tensor_tensor(outs[:, qt * 512:(qt + 1) * 512],
                                            t512[:], qp[qt][:], ALU.add))
                    v.wait_ge(sVE.h, sVE.n)
            emit(blk.vector, _, sVE)
            asm_done[(qb, ec)] = sVE.n

            def _(s, qb=qb, ec=ec):
                s.wait_ge(sVE.h, asm_done[(qb, ec)])
                for qt in range(4):
                    sSO.inc(s.dma_start(
                        out=out_tpe[:, qb * 4 + qt, ec * 512:(ec + 1) * 512],
                        in_=outs[:, qt * 512:(qt + 1) * 512]), 16)
            emit(blk.sync, _)
            asm_dma[(qb, ec)] = sSO.n

    if DEBUG:
        def _(s):
            s.wait_ge(sSC.h, qtr_evac[(3, 15)])
            sSO.inc(s.dma_start(out=dbg["dbg_QT"], in_=QT[:]), 16)
            s.wait_ge(sVE.h, ot_evac[(3, H - 1)])
            sSO.inc(s.dma_start(out=dbg["dbg_OT"], in_=OTb[:]), 16)
        emit(blk.sync, _)

    B_PE, B_VE, B_SC, B_GP = sPE.n, sVE.n, sSC.n, sGP.n
    B_SO, B_LD, B_GW = sSO.n, sLD.n, sGW.n
    pb.close()
    pab.close()

    pc = ExitStack()

    def csb(n, s_, d_):
        return pc.enter_context(nc.sbuf_tensor(n, s_, d_))

    def cps(n, s_, d_):
        return pc.enter_context(nc.psum_tensor(n, s_, d_))

    pcx = ExitStack()
    pcw = ExitStack()

    def wsb(n, s_, d_):
        return pcw.enter_context(nc.sbuf_tensor(n, s_, d_))

    hsb = wsb("hsb", [128, NIC * OWN], BF16)
    wgq = wsb("wgq", [128, 2 * QBUF], BF16)
    wuq = wsb("wuq", [128, 2 * QBUF], BF16)
    wdc = wsb("wdc", [128, 2 * NIC * 512], BF16)
    sgs = wsb("sgs", [128, 2 * 512], F32)
    evb = wsb("evb", [128, 2 * 4 * 512], BF16)

    gp_ = [cps(f"gp{i}", [128, 512], F32) for i in range(4)]
    up_ = [cps(f"up{i}", [128, 512], F32) for i in range(4)]

    cbarred = set()

    def barB(eng, key):
        if key not in cbarred:
            eng.wait_ge(sPE.h, B_PE)
            eng.wait_ge(sVE.h, B_VE)
            eng.wait_ge(sSC.h, B_SC)
            eng.wait_ge(sGP.h, B_GP)
            for cc in (sQL, sXO[0], sXO[1], sWQ[0], sWQ[1]):
                eng.wait_ge(cc.h, cc.n)
            cbarred.add(key)

    def _(g):
        barB(g, "gp")
        g.wait_ge(sSO.h, B_SO)
    emit(blk.gpsimd, _, sGP)

    idx16 = csb("idx16", [128, 8 * 48], I16)
    cg = csb("cg", [128, 2 * 6 * 64], F32)
    ctp_sb = csb("ctp_sb", [128, 8 * 48], BF16)
    xnTf = pcx.enter_context(nc.sbuf_tensor("xnTf", [128, NEC * OWN], BF16))
    btok64 = pcx.enter_context(nc.sbuf_tensor("btok64", [128, 512], F32))

    def _(s):
        barB(s, "sync")
        s.wait_ge(sSO.h, B_SO)
        sLD.inc(s.dma_start(out=xnTf[:], in_=xnimg), 16)
    emit(blk.sync, _)
    v_xnTf = sLD.n
    bidx = pcx.enter_context(nc.sbuf_tensor("bidx", [128, 1024], I16))
    idxe_r = pcx.enter_context(nc.sbuf_tensor("idxe_r", [128, 24 * 64], F32))
    idxe_f = pcx.enter_context(nc.sbuf_tensor("idxe_f", [128, 48], F32))
    pdm = pcx.enter_context(nc.sbuf_tensor("pdm", [128, 48], F32))

    def _(s):
        s.wait_ge(sBK.h, sBK.n)
        sBK.inc(s.dma_start(out=ctp_sb[:], in_=ctp_in), 16)
    emit(blk.sync, _)
    V_CTP = sBK.n

    def _(s):
        s.wait_ge(sVE.h, V_SEL)
        sBX.inc(s.dma_start(out=vmi_d, in_=vmi16[:]), 16)
        s.wait_ge(sBX.h, sBX.n)
        sBX.inc(s.dma_start(
            out=combd[0:OWN, 0:8].rearrange("(tt p) j -> p tt j", p=128),
            in_=comb[:]), 16)
        s.wait_ge(sBX.h, sBX.n)
        s.wait_ge(sXR[0].h, sXR[0].n)
        s.wait_ge(sXR[1].h, sXR[1].n)
        sBX.inc(s.dma_start(out=xnrow[OWN:OWN + NSL, :],
                            in_=xnrow[0:NSL, :]), 16)
        s.wait_ge(sBX.h, sBX.n)
        for rr in range(8):
            s.wait_ge(sBI.h, sBI.n)
            sBI.inc(s.dma_start(
                out=bidx[rr * 16:(rr + 1) * 16, :],
                in_=vmi_d.rearrange("(p8 p16) c -> p16 p8 c", p16=16)), 16)
    emit(blk.sync, _)
    V_BIDX = sBI.n

    def _bld(c):
        def _(s, c=c):
            if c >= 1:
                s.wait_ge(sBD.h, 16 * c)
            s.wait_ge(sBK.h, sBK.n)
            sBK.inc(s.dma_start(
                out=btok64[:],
                in_=btok64_in[:, c * 512:(c + 1) * 512]), 16)
        emit(blk.sync, _)
        v_bk = sBK.n

        def _(g, c=c, v_bk=v_bk):
            if c == 0:
                g.wait_ge(sBI.h, V_BIDX)
            g.wait_ge(sBK.h, v_bk)
            g.wait_ge(sBD.h, sBD.n)
            sBD.inc(g.dma_scatter_add(
                out_ap=idxf,
                in_ap=btok64[:].rearrange("p (c j) -> p c j", j=64),
                idxs_ap=bidx[:, c * 64:(c + 1) * 64],
                num_idxs=1024, num_idxs_reg=1024, elem_size=64,
                elem_step=64), 16)
        emit(blk.gpsimd, _, sGP)

    ix_done = {}

    def _ixt(e):
        ex_m = {}
        for hf in range(2):
            def _(s, e=e, hf=hf):
                s.wait_ge(sBD.h, 16 * 16)
                if e > 1 and hf == 0:
                    s.wait_ge(sVE.h, ix_done[e - 1])
                elif hf == 1:
                    s.wait_ge(sVE.h, ex_m[0])
                s.wait_ge(sIX.h, sIX.n)
                for rr in range(8):
                    s.wait_ge(sIX.h, sIX.n)
                    sIX.inc(s.dma_start(
                        out=idxe_r[rr * 16:(rr + 1) * 16, :],
                        in_=idxf[(e - 1) * CAPE + hf * 384:
                                 (e - 1) * CAPE + (hf + 1) * 384, :].rearrange(
                            "(c p) j -> p c j", p=16)), 16)
            emit(blk.sync, _)
            v_ix = sIX.n

            def _(v, e=e, hf=hf, v_ix=v_ix):
                v.wait_ge(sIX.h, v_ix)
                sVE.inc(v.tensor_copy(
                    idxe_f[:, hf * 24:(hf + 1) * 24],
                    idxe_r[:].rearrange("p (c j) -> p c j", j=64)[:, :, 0:1]))
            emit(blk.vector, _, sVE)
            ex_m[hf] = sVE.n

        def _(v, e=e):
            v.wait_ge(sVE.h, ex_m[1])
            v.wait_ge(sBK.h, V_CTP)
            sVE.inc(v.tensor_scalar(pdm[:], idxe_f[:], float(OWN) - 0.5,
                                    None, ALU.is_ge))
            v.wait_ge(sVE.h, sVE.n)
            sVE.inc(v.tensor_mul(pdm[:], pdm[:],
                                 ctp_sb[:, (e - 1) * 48:e * 48]))
            v.wait_ge(sVE.h, sVE.n)
            sVE.inc(v.tensor_tensor(idxe_f[:], idxe_f[:], pdm[:], ALU.add))
            v.wait_ge(sVE.h, sVE.n)
            sVE.inc(v.tensor_copy(idx16[:, (e - 1) * 48:e * 48], idxe_f[:]))
        emit(blk.vector, _, sVE)
        ix_done[e] = sVE.n

    wld = {}
    sgs_free = [0, 0]
    wgu_free = [0, 0]
    wd_free = [0, 0]
    gu_done = {}
    h_evac = {}
    dn_evac = {}
    acc_dma = {}
    down_mm_last = [0] * NEXP

    def _load_gu(qidx):
        if qidx >= NEXP * 4:
            return
        e, gq = qidx // 4, qidx % 4
        slot = qidx % 2

        def _(g, e=e, gq=gq, slot=slot):
            if wgu_free[slot]:
                g.wait_ge(sPE.h, wgu_free[slot])
            g.wait_ge(sWG[slot].h, sWG[slot].n)
            g.wait_ge(sWU[slot].h, sWU[slot].n)
            cols = NEC * GCOL[gq]
            sWG[slot].inc(g.dma_start(
                out=wgq[:, slot * QBUF:slot * QBUF + cols],
                in_=wg_in[e, :, GOFF[gq]:GOFF[gq] + cols]), 16)
            wld[(e, "g", gq)] = sWG[slot].n
            sWU[slot].inc(g.dma_start(
                out=wuq[:, slot * QBUF:slot * QBUF + cols],
                in_=wu_in[e, :, GOFF[gq]:GOFF[gq] + cols]), 16)
            wld[(e, "u", gq)] = sWU[slot].n
        emit(blk.gpsimd, _, sGP)

    def _load_wd(didx):
        if didx >= NEXP * 4:
            return
        e, ecq = didx // 4, didx % 4
        slot = didx % 2

        def _(g, e=e, ecq=ecq, slot=slot):
            if wd_free[slot]:
                g.wait_ge(sPE.h, wd_free[slot])
            g.wait_ge(sWD[slot].h, sWD[slot].n)
            sWD[slot].inc(g.dma_start(
                out=wdc[:, slot * (NIC * 512):(slot + 1) * (NIC * 512)],
                in_=wd_in[e, :, ecq * (NIC * 512):(ecq + 1) * (NIC * 512)]), 16)
            wld[(e, "d", ecq)] = sWD[slot].n
        emit(blk.gpsimd, _, sGP)

    _load_gu(0)
    _load_gu(1)
    _load_wd(0)
    _load_wd(1)

    e = 0
    E0_SG = {}
    for gq in range(4):
        slot = gq % 2
        for icl in range(GGRP[gq][1] - GGRP[gq][0]):
            ic = GGRP[gq][0] + icl
            for half in range(2):
                bank = ([gp_[0], gp_[1], up_[0], up_[1]] if half == 0
                        else [gp_[2], gp_[3], up_[2], up_[3]])
                tc0, tc1 = 2 * half, 2 * half + 1

                def _(tn, gq=gq, icl=icl, ic=ic, half=half,
                      slot=slot, bank=bank, tc0=tc0, tc1=tc1):
                    barB(tn, "pe")
                    tn.wait_ge(sWG[slot].h, wld[(0, "g", gq)])
                    tn.wait_ge(sWU[slot].h, wld[(0, "u", gq)])
                    if gq == 0 and icl == 0 and half == 0:
                        tn.wait_ge(sLD.h, v_xnTf)
                    if ic > 0:
                        tn.wait_ge(sVE.h, h_evac[(0, ic - 1, half)])
                    base = slot * QBUF
                    last = None
                    for ec in range(NEC):
                        off = base + ec * GCOL[gq] + icl * 128
                        lg = wgq[:, off:off + 128]
                        lu = wuq[:, off:off + 128]
                        x0 = xnTf[:, ec * OWN + tc0 * 512:
                                  ec * OWN + (tc0 + 1) * 512]
                        x1 = xnTf[:, ec * OWN + tc1 * 512:
                                  ec * OWN + (tc1 + 1) * 512]
                        tn.matmul(bank[0][:], lg, x0, start=(ec == 0),
                                  stop=(ec == NEC - 1))
                        tn.matmul(bank[1][:], lg, x1, start=(ec == 0),
                                  stop=(ec == NEC - 1))
                        tn.matmul(bank[2][:], lu, x0, start=(ec == 0),
                                  stop=(ec == NEC - 1))
                        last = tn.matmul(bank[3][:], lu, x1,
                                         start=(ec == 0),
                                         stop=(ec == NEC - 1))
                    sPE.inc(last)
                emit(blk.tensor, _, sPE)
                gu_done[(0, ic, half)] = sPE.n

                for k in range(2):
                    tc = 2 * half + k
                    gbank = bank[k]
                    ubank = bank[2 + k]
                    ssl = k

                    def _(sc, ic=ic, half=half, gbank=gbank, ssl=ssl):
                        sc.wait_ge(sPE.h, gu_done[(0, ic, half)])
                        if sgs_free[ssl]:
                            sc.wait_ge(sVE.h, sgs_free[ssl])
                        sSC.inc(sc.activation(
                            sgs[:, ssl * 512:(ssl + 1) * 512],
                            gbank[:], AF.Sigmoid))
                    emit(blk.scalar, _, sSC)
                    v_sg = sSC.n

                    def _(v, ic=ic, tc=tc, ubank=ubank, ssl=ssl, v_sg=v_sg):
                        v.wait_ge(sSC.h, v_sg)
                        sVE.inc(v.tensor_tensor(
                            hsb[:, ic * OWN + tc * 512:
                                ic * OWN + (tc + 1) * 512],
                            sgs[:, ssl * 512:(ssl + 1) * 512],
                            ubank[:], ALU.mult))
                    emit(blk.vector, _, sVE)
                    sgs_free[ssl] = sVE.n
                h_evac[(0, ic, half)] = sVE.n
                if ic == NIC - 1:
                    E0_SG[half] = sSC.n

        wgu_free[slot] = sPE.n
        _load_gu(gq + 2)
        for i in range(4):
            _bld(gq * 4 + i)
    for e in range(1, NEXP):
        _ixt(e)
    E0_GU = sPE.n

    pcx.close()
    xgT = csb("xgT", [128, 2 * NEC * CAPE], BF16)
    sgs_sp = csb("sgs_sp", [128, 2 * CAPE], F32)

    tl_mark = {}
    cg_mark = {}
    gu_last = {}
    dn_last = {}

    def _stage(e):
        def _(g, e=e, slot=e % 2):
            g.wait_ge(sVE.h, ix_done[NEXP - 1])
            g.wait_ge(sBX.h, sBX.n)
            g.wait_ge(sPE.h, E0_GU)
            if e >= 3:
                g.wait_ge(sPE.h, gu_last[e - 2])
            g.wait_ge(sTL.h, sTL.n)
            sTL.inc(g.dma_gather(
                out_ap=xgT[:, slot * (NEC * CAPE):
                           (slot + 1) * (NEC * CAPE)].rearrange(
                    "p (ec s) -> p ec s", s=CAPE),
                in_ap=xnrow, idxs_ap=idx16[:, (e - 1) * 48:e * 48],
                num_idxs=CAPE, num_idxs_reg=CAPE, elem_size=E,
                transpose=True), 16)
            tl_mark[e] = sTL.n
        emit(blk.gpsimd, _, sGP)

        def _(g, e=e, slot=e % 2):
            g.wait_ge(sVE.h, ix_done[e])
            g.wait_ge(sBX.h, sBX.n)
            if e >= 3:
                g.wait_ge(sVE.h, dn_last[e - 2])
            g.wait_ge(sCG.h, sCG.n)
            sCG.inc(g.dma_gather(
                out_ap=cg[:, slot * 384:(slot + 1) * 384].rearrange(
                    "p (c j) -> p c j", j=64),
                in_ap=combd, idxs_ap=idx16[:, (e - 1) * 48:e * 48],
                num_idxs=CAPE, num_idxs_reg=CAPE, elem_size=64,
                transpose=False), 16)
            cg_mark[e] = sCG.n
        emit(blk.gpsimd, _, sGP)

    _stage(1)
    _stage(2)

    for ecq in range(4):
        slot = ecq % 2
        for tt in range(NOWN):
            dps = up_[tt % 2]

            def _(tn, ecq=ecq, tt=tt, slot=slot, dps=dps):
                tn.wait_ge(sWD[slot].h, wld[(0, "d", ecq)])
                if ecq == 0 and tt < 2:
                    tn.wait_ge(sVE.h, h_evac[(0, NIC - 1, 1)])
                    tn.wait_ge(sVE.h, h_evac[(0, NIC - 1, 0)])
                elif tt >= 2:
                    tn.wait_ge(sVE.h, dn_evac[(0, ecq, tt - 2)])
                else:
                    tn.wait_ge(sVE.h, dn_evac[(0, ecq - 1, NOWN - 2 + tt)])
                last = None
                for ic in range(NIC):
                    last = tn.matmul(
                        dps[:],
                        hsb[:, ic * OWN + tt * 128:ic * OWN + (tt + 1) * 128],
                        wdc[:, slot * (NIC * 512) + ic * 512:
                            slot * (NIC * 512) + (ic + 1) * 512],
                        start=(ic == 0), stop=(ic == NIC - 1))
                sPE.inc(last)
            emit(blk.tensor, _, sPE)
            v_dn = sPE.n
            if tt == NOWN - 1:
                wd_free[slot] = sPE.n
                if ecq == 3:
                    down_mm_last[0] = sPE.n
                _load_wd(ecq + 2)

            def _(v, ecq=ecq, tt=tt, dps=dps, v_dn=v_dn):
                v.wait_ge(sPE.h, v_dn)
                grp = (tt // 4) % 2
                if tt % 4 == 0:
                    gi = ecq * 4 + tt // 4
                    if gi >= 2 and acc_dma.get(gi - 2):
                        v.wait_ge(sEV[grp].h, acc_dma[gi - 2])
                dst = evb[:, grp * 2048 + (tt % 4) * 512:
                          grp * 2048 + (tt % 4 + 1) * 512]
                sVE.inc(v.tensor_copy(dst, dps[:]))
            emit(blk.vector, _, sVE)
            dn_evac[(0, ecq, tt)] = sVE.n

            if tt % 4 == 3:
                gslot = (tt // 4) % 2

                def _(g, ecq=ecq, tt=tt, grp=gslot):
                    g.wait_ge(sVE.h, dn_evac[(0, ecq, tt)])
                    g.wait_ge(sEV[grp].h, sEV[grp].n)
                    sEV[grp].inc(g.dma_start(
                        out=out_tpe[:, tt - 3:tt + 1,
                                    ecq * 512:(ecq + 1) * 512],
                        in_=evb[:, grp * 2048:(grp + 1) * 2048].rearrange(
                            "p (q c) -> p q c", c=512),
                        accum_op=ALU.add), 16)
                emit(blk.gpsimd, _, sGP)
                acc_dma[ecq * 4 + tt // 4] = sEV[gslot].n

    evw = evb[:].bitcast(F32)
    sg_done = {}
    h_done = {}
    sa_mark = {}
    bankfree = {0: (E0_SG[0], h_evac[(0, NIC - 1, 0)]),
                1: (E0_SG[1], h_evac[(0, NIC - 1, 1)])}
    sgs_rd = {k: h_evac[(0, NIC - 1, 1)]
              for k in ((0, 0), (0, 1), (1, 0), (1, 1))}

    for e in range(1, NEXP):
        xsl = e % 2
        xbase = xsl * (NEC * CAPE)
        for gq in range(4):
            wsl = (e * 4 + gq) % 2
            for icl in range(GGRP[gq][1] - GGRP[gq][0]):
                ic = GGRP[gq][0] + icl
                pi = ic % 2
                gA, gB = gp_[2 * pi], gp_[2 * pi + 1]
                for ch in range(2):
                    c0 = ch * 512
                    cwd = 512 if ch == 0 else CAPE - 512

                    def _(tn, e=e, gq=gq, icl=icl, ic=ic, ch=ch, wsl=wsl,
                          gA=gA, gB=gB, c0=c0, cwd=cwd, pi=pi, xbase=xbase):
                        tn.wait_ge(sWG[wsl].h, wld[(e, "g", gq)])
                        tn.wait_ge(sWU[wsl].h, wld[(e, "u", gq)])
                        if gq == 0 and icl == 0 and ch == 0:
                            tn.wait_ge(sTL.h, tl_mark[e])
                        if ch == 0:
                            sg_m, h_m = bankfree[pi]
                            tn.wait_ge(sSC.h, sg_m)
                            tn.wait_ge(sVE.h, h_m)
                        else:
                            tn.wait_ge(sSC.h, sg_done[(e, ic, 0)])
                            tn.wait_ge(sVE.h, h_done[(e, ic, 0)])
                        base = wsl * QBUF
                        last = None
                        for ec in range(NEC):
                            off = base + ec * GCOL[gq] + icl * 128
                            lg = wgq[:, off:off + 128]
                            lu = wuq[:, off:off + 128]
                            xr = xgT[:, xbase + ec * CAPE + c0:
                                     xbase + ec * CAPE + c0 + cwd]
                            tn.matmul(gA[:, 0:cwd], lg, xr, start=(ec == 0),
                                      stop=(ec == NEC - 1))
                            last = tn.matmul(gB[:, 0:cwd], lu, xr,
                                             start=(ec == 0),
                                             stop=(ec == NEC - 1))
                        sPE.inc(last)
                    emit(blk.tensor, _, sPE)
                    mm_m = sPE.n

                    def _(sc, e=e, ic=ic, ch=ch, gA=gA, c0=c0, cwd=cwd,
                          mm_m=mm_m, prev=sgs_rd[(ic % 2, ch)]):
                        sc.wait_ge(sPE.h, mm_m)
                        sc.wait_ge(sVE.h, prev)
                        sSC.inc(sc.activation(
                            sgs_sp[:, (ic % 2) * CAPE + c0:
                                   (ic % 2) * CAPE + c0 + cwd],
                            gA[:, 0:cwd], AF.Sigmoid))
                    emit(blk.scalar, _, sSC)
                    sg_done[(e, ic, ch)] = sSC.n

                    def _(v, e=e, ic=ic, ch=ch, gB=gB, c0=c0, cwd=cwd):
                        v.wait_ge(sSC.h, sg_done[(e, ic, ch)])
                        if ic == 0 and ch == 0:
                            v.wait_ge(sPE.h, down_mm_last[e - 1])
                        sVE.inc(v.tensor_tensor(
                            hsb[:, ic * CAPE + c0:ic * CAPE + c0 + cwd],
                            sgs_sp[:, (ic % 2) * CAPE + c0:
                                   (ic % 2) * CAPE + c0 + cwd],
                            gB[:, 0:cwd], ALU.mult))
                    emit(blk.vector, _, sVE)
                    h_done[(e, ic, ch)] = sVE.n
                    sgs_rd[(ic % 2, ch)] = sVE.n
                    if ch == 1:
                        bankfree[pi] = (sg_done[(e, ic, 1)], h_done[(e, ic, 1)])

            wgu_free[wsl] = sPE.n
            _load_gu(e * 4 + gq + 2)
        gu_last[e] = sPE.n

        for ecq in range(4):
            wsl = (e * 4 + ecq) % 2
            for tt in range(NSL):
                dps = up_[tt % 2]
                kk = (e - 1) * 4 * NSL + ecq * NSL + tt
                r = kk % 4

                def _(tn, e=e, ecq=ecq, tt=tt, wsl=wsl, dps=dps):
                    tn.wait_ge(sWD[wsl].h, wld[(e, "d", ecq)])
                    if ecq == 0 and tt < 2:
                        tn.wait_ge(sVE.h, h_done[(e, NIC - 1, 1)])
                        if e == 1:
                            tn.wait_ge(sVE.h, dn_evac[(0, 3, NOWN - 1)])
                        else:
                            tn.wait_ge(sVE.h, dn_evac[(e - 1, 3, NSL - 2 + tt)])
                    elif tt >= 2:
                        tn.wait_ge(sVE.h, dn_evac[(e, ecq, tt - 2)])
                    else:
                        tn.wait_ge(sVE.h, dn_evac[(e, ecq - 1, NSL - 2 + tt)])
                    last = None
                    for ic in range(NIC):
                        last = tn.matmul(
                            dps[:],
                            hsb[:, ic * CAPE + tt * 128:
                                ic * CAPE + (tt + 1) * 128],
                            wdc[:, wsl * (NIC * 512) + ic * 512:
                                wsl * (NIC * 512) + (ic + 1) * 512],
                            start=(ic == 0), stop=(ic == NIC - 1))
                    sPE.inc(last)
                emit(blk.tensor, _, sPE)
                v_dn = sPE.n
                if tt == NSL - 1:
                    wd_free[wsl] = sPE.n
                    if ecq == 3:
                        down_mm_last[e] = sPE.n
                    _load_wd(e * 4 + ecq + 2)

                def _(v, e=e, ecq=ecq, tt=tt, dps=dps, v_dn=v_dn,
                      kk=kk, r=r, xsl=xsl):
                    v.wait_ge(sPE.h, v_dn)
                    v.wait_ge(sCG.h, cg_mark[e])
                    if kk < 4:
                        v.wait_ge(sEV[0].h, sEV[0].n)
                        v.wait_ge(sEV[1].h, sEV[1].n)
                    else:
                        v.wait_ge(sSA[(kk - 4) % 4].h, sa_mark[kk - 4])
                    sVE.inc(v.tensor_scalar(
                        evw[:, r * 512:(r + 1) * 512], dps[:],
                        cg[:, xsl * 384 + tt * 64 + (e - 1):
                            xsl * 384 + tt * 64 + e],
                        None, ALU.mult))
                emit(blk.vector, _, sVE)
                dn_evac[(e, ecq, tt)] = sVE.n

                def _(g, e=e, ecq=ecq, tt=tt, r=r, kk=kk, ev_m=sVE.n):
                    g.wait_ge(sVE.h, ev_m)
                    g.wait_ge(sSA[kk % 4].h, sSA[kk % 4].n)
                    if e > 1 and ecq == 0 and tt == 0:
                        for q in range(4):
                            g.wait_ge(sSA[q].h, sSA[q].n)
                    sSA[kk % 4].inc(g.dma_scatter_add(
                        out_ap=macc[:, ecq * 512:(ecq + 1) * 512],
                        in_ap=evw[:, r * 512:(r + 1) * 512].rearrange(
                            "p (o c) -> p o c", o=1),
                        idxs_ap=idx16[:, (e - 1) * 48 + tt * 8:
                                      (e - 1) * 48 + (tt + 1) * 8],
                        num_idxs=128, num_idxs_reg=128, elem_size=512,
                        elem_step=E), 16)
                    sa_mark[kk] = sSA[kk % 4].n
                emit(blk.gpsimd, _, sGP)
        dn_last[e] = dn_evac[(e, 3, NSL - 1)]
        if e + 2 < NEXP:
            _stage(e + 2)

    mstg = xgT[:].bitcast(F32)
    ad_mark = {}
    mg_mark = {}
    for tt in range(NOWN):
        k = tt % 4

        def _(s, tt=tt, k=k):
            if tt == 0:
                for q in range(4):
                    s.wait_ge(sSA[q].h, sSA[q].n)
                s.wait_ge(sPE.h, gu_last[NEXP - 1])
            if tt >= 4:
                s.wait_ge(sAD[k].h, ad_mark[tt - 4])
            s.wait_ge(sMG.h, sMG.n)
            sMG.inc(s.dma_start(
                out=mstg[:, k * E:(k + 1) * E],
                in_=macc[tt * 128:(tt + 1) * 128, 0:E]), 16)
        emit(blk.sync, _)
        mg_mark[tt] = sMG.n

        def _(g, tt=tt, k=k):
            if tt == 0:
                g.wait_ge(sEV[0].h, sEV[0].n)
                g.wait_ge(sEV[1].h, sEV[1].n)
                g.wait_ge(sSO.h, sSO.n)
            g.wait_ge(sMG.h, mg_mark[tt])
            g.wait_ge(sAD[k].h, sAD[k].n)
            sAD[k].inc(g.dma_start(
                out=out_tpe[:, tt, :],
                in_=mstg[:, k * E:(k + 1) * E],
                accum_op=ALU.add), 16)
        emit(blk.gpsimd, _, sGP)
        ad_mark[tt] = sAD[k].n

    def _(g):
        g.wait_ge(sAC.h, sAC.n)
        g.wait_ge(sSO.h, sSO.n)
        g.wait_ge(sEV[0].h, sEV[0].n)
        g.wait_ge(sEV[1].h, sEV[1].n)
        for q in range(4):
            g.wait_ge(sAD[q].h, sAD[q].n)
            g.wait_ge(sSA[q].h, sSA[q].n)
        g.wait_ge(sCG.h, sCG.n)
        g.wait_ge(sTL.h, sTL.n)
        g.wait_ge(sBD.h, sBD.n)
    emit(blk.gpsimd, _, sGP)

    def _(s):
        s.wait_ge(sAC.h, sAC.n)
        s.wait_ge(sSO.h, sSO.n)
        s.wait_ge(sEV[0].h, sEV[0].n)
        s.wait_ge(sEV[1].h, sEV[1].n)
        for q in range(4):
            s.wait_ge(sAD[q].h, sAD[q].n)
            s.wait_ge(sSA[q].h, sSA[q].n)
        s.wait_ge(sCG.h, sCG.n)
        s.wait_ge(sTL.h, sTL.n)
        s.wait_ge(sBX.h, sBX.n)
        s.wait_ge(sBI.h, sBI.n)
        s.wait_ge(sIX.h, sIX.n)
        s.wait_ge(sBK.h, sBK.n)
        s.wait_ge(sMG.h, sMG.n)
    emit(blk.sync, _)

    pc.close()
    pcw.close()
    es.close()
    return nc


def _host_prep(inputs):
    x = np.asarray(inputs["L0_attn_hidden_states"])
    wq = np.asarray(inputs["L0_attn_wq"])
    wk = np.asarray(inputs["L0_attn_wk"])
    wv = np.asarray(inputs["L0_attn_wv"])
    wo = np.asarray(inputs["L0_attn_wo"])
    qscale = np.asarray(inputs["L0_attn_query_scale"])
    kscale = np.asarray(inputs["L0_attn_key_scale"])
    rw = np.asarray(inputs["L0_ffn_router_w"])
    sg = np.asarray(inputs["L0_ffn_shared_gate"])
    su = np.asarray(inputs["L0_ffn_shared_up"])
    sd = np.asarray(inputs["L0_ffn_shared_down"])
    wg = np.asarray(inputs["L0_ffn_w_gate"])
    wu = np.asarray(inputs["L0_ffn_w_up"])
    wd = np.asarray(inputs["L0_ffn_w_down"])

    def pmajor(a):
        r, c = a.shape
        return np.ascontiguousarray(
            a.reshape(r // 128, 128, c).transpose(1, 0, 2).reshape(128, -1))

    wq4 = np.ascontiguousarray(
        wq.reshape(NEC, 128, 4, 512).transpose(1, 2, 0, 3).reshape(128, -1))
    wo4 = np.ascontiguousarray(
        wo.reshape(H, 128, 4, 512).transpose(1, 2, 0, 3).reshape(128, -1))
    wk_h = pmajor(wk)
    wv_h = pmajor(wv)
    rw_h = pmajor(np.ascontiguousarray(rw.T))

    def gate_h(w):
        wT = np.ascontiguousarray(w.T)
        t = wT.reshape(NEC, 128, NIC, 128)
        parts = [np.ascontiguousarray(
            t[:, :, a:b, :].transpose(1, 0, 2, 3).reshape(128, -1))
            for a, b in GGRP]
        return np.concatenate(parts, axis=1)

    def down_h(w):
        wT = np.ascontiguousarray(w.T)
        t = wT.reshape(NIC, 128, 4, 512)
        return np.ascontiguousarray(t.transpose(1, 2, 0, 3).reshape(128, -1))

    wg_h = np.stack([gate_h(sg)] + [gate_h(wg[i]) for i in range(8)])
    wu_h = np.stack([gate_h(su)] + [gate_h(wu[i]) for i in range(8)])
    wd_h = np.stack([down_h(sd)] + [down_h(wd[i]) for i in range(8)])

    half = D // 2
    inv_freq = 1.0 / (10000.0 ** (np.arange(half, dtype=np.float64) / half))

    ar = np.arange(128)
    tri_ge = (ar[:, None] >= ar[None, :]).astype(np.float32)
    tri_le = (ar[:, None] <= ar[None, :]).astype(np.float32)
    bmask = np.concatenate([tri_ge, tri_le], axis=1).astype(BF)

    pp, tt_, ee = np.meshgrid(np.arange(128), np.arange(NOWN), np.arange(8),
                              indexing="ij")
    chk_h = np.ascontiguousarray(
        (TRASH + (pp // 16) * 2 + tt_ // 8).astype(np.float32).reshape(
            128, NOWN * 8))
    ii = ((((pp // 16) * 16 + tt_) * 8 + ee) * 16 + (pp % 16)).reshape(-1)
    tok = (tt_ * 128 + pp).reshape(-1)
    btok = np.zeros((128, 128), dtype=np.float32)
    btok[ii % 128, ii // 128] = tok - 2048.0
    btok64_h = np.zeros((128, 16, 8, 64), dtype=np.float32)
    btok64_h[:, :, :, 0] = btok.reshape(128, 16, 8)
    btok64_h = np.ascontiguousarray(btok64_h.reshape(128, 16 * 512))
    ctpad_h = np.ascontiguousarray(
        np.tile((np.arange(48) // 8).astype(np.float32), (128, 8))).astype(BF)

    common = dict(
        wq_h=wq4, wk_h=wk_h, wv_h=wv_h, wo_h=wo4, rw_h=rw_h,
        wg_h=wg_h, wu_h=wu_h, wd_h=wd_h,
        qscale_t=np.ascontiguousarray(np.tile(qscale[None, :], (128, 1))),
        kscale_t=np.ascontiguousarray(np.tile(kscale[None, :], (128, 1))),
        identf=np.eye(128, dtype=np.float32),
        identb=np.eye(128, dtype=np.float32).astype(BF),
        onescol=np.ones((128, 1), dtype=BF),
        onesrow=np.ones((1, 128), dtype=np.float32),
        epsln=np.full((128, 1), LNEPS, dtype=np.float32),
        eps30=np.full((128, 1), 1e-30, dtype=np.float32),
        bmask=bmask,
        ltri_h=np.triu(np.ones((128, 128), dtype=np.float32), 1),
        ec8_h=np.ascontiguousarray(
            np.tile(np.arange(8, dtype=np.float32) * CAPE, (128, 1))),
        chk_h=chk_h, btok64_h=btok64_h, ctpad_h=ctpad_h,
        idxf_h=np.full((TRASH + 16, 64), 2048.0, dtype=np.float32),
        macc_h=np.zeros((OWN + NSL, E), dtype=np.float32),
        combd_h=np.zeros((OWN + NSL, 64), dtype=np.float32),
    )

    in_maps = []
    for c in range(NCORE):
        bi, qq = c // 4, (c % 4) * OWN
        lo, hi = qq - HALO, qq + OWN + HALO
        xs = np.zeros((SLC, E), dtype=np.float32)
        a, b_ = max(lo, 0), min(hi, S)
        xs[a - lo:b_ - lo] = x[bi, a:b_]
        pos = np.clip(np.arange(lo, hi), 0, S - 1).astype(np.float64)
        ang = np.outer(pos, inv_freq)
        cs = np.concatenate([np.cos(ang), np.cos(ang),
                             np.sin(ang), np.sin(ang)],
                            axis=1).astype(np.float32)
        kvalid = np.ones((128, NT), dtype=np.float32)
        rowpos = lo + (np.arange(NT)[None, :] * 128 + np.arange(128)[:, None])
        kvalid[(rowpos < 0) | (rowpos >= S)] = 0.0
        m = dict(common)
        m.update(x_slice=xs, cs=cs, kvalid01=kvalid)
        in_maps.append(m)
    return in_maps


def kernel(**inputs):
    in_maps = _host_prep(inputs)
    nc = build_nc()
    nc.finalize()
    res = run_bass_kernel_spmd(nc, in_maps, core_ids=list(range(NCORE)))
    out = np.empty((B, S, E), dtype=np.float32)
    for c in range(NCORE):
        bi, qq = c // 4, (c % 4) * OWN
        out[bi, qq:qq + OWN] = res.results[c]["out"]
    return out
